# version 23
# baseline (speedup 1.0000x reference)
import numpy as np
import ml_dtypes
import concourse.bass as bass
import concourse.mybir as mybir
from concourse.bass_utils import run_bass_kernel_spmd

F32 = mybir.dt.float32
BF16 = mybir.dt.bfloat16
AF = mybir.ActivationFunctionType
ALU = mybir.AluOpType
AX = mybir.AxisListType

T, D, NT = 2048, 1024, 16
HEADS, DH, WIN = 8, 64, 512
HID, FF2 = 2730, 5460
NEG = -1.0e30
BF = ml_dtypes.bfloat16
ENGINES = ("tensor", "vector", "scalar", "gpsimd", "sync")


class RecOp:
    __slots__ = ("meth", "a", "k", "incs")

    def __init__(self, meth, a, k):
        self.meth, self.a, self.k = meth, a, k
        self.incs = []

    def then_inc(self, sem, k=1):
        self.incs.append((sem, k))
        return self

    def replay(self, eng):
        try:
            r = getattr(eng, self.meth)(*self.a, **self.k)
        except Exception as e:
            shapes = [tuple(x.shape) if isinstance(x, bass.AP) else x
                      for x in self.a]
            raise RuntimeError(f"replay {self.meth} {shapes} {self.k}: {e}") from e
        merged = {}
        for sem, k in self.incs:
            merged[sem] = merged.get(sem, 0) + k
        for sem, k in merged.items():
            r = r.then_inc(sem, k)


def _apspan(ap):
    try:
        W = ap.tensor.shape[1]
        off = ap.offset
        if not isinstance(off, int):
            return None
        c0 = off % W
        pairs = list(ap.ap)[1:]
        span = 1
        for st, num in pairs:
            span += abs(st) * (num - 1)
        return (ap.tensor.name, c0, c0 + span)
    except Exception:
        return None


def _overlap(a, b):
    return a and b and a[0] == b[0] and a[1] < b[2] and b[1] < a[2]


class Rec:

    def __init__(self, phase, engname):
        self._phase, self._eng = phase, engname

    def __getattr__(self, meth):
        def call(*a, **k):
            ph = self._phase
            ss = ph.selfsem.get(self._eng)
            if ss is None or meth in ("wait_ge", "sem_inc"):
                op = RecOp(meth, a, k)
                ph.ops.append((self._eng, op))
                return op
            st = ph.dep.setdefault(self._eng, {"writes": [], "waited": 0})
            outs, ins = [], []
            aps = [x for x in a if isinstance(x, bass.AP)]
            if meth in ("dma_start",):
                outs = aps[:1]
                ins = aps[1:]
            elif meth in ("memset", "memzero"):
                outs = aps[:1]
            else:
                outs = aps[:1]
                ins = aps[1:]
            for kk in ("bias", "scale", "initial", "scalar1", "scalar2", "scalar"):
                v = k.get(kk)
                if isinstance(v, bass.AP):
                    ins.append(v)
            ao = k.get("accum_out")
            if isinstance(ao, bass.AP):
                outs.append(ao)
            need = 0
            spans_in = [_apspan(x) for x in ins]
            spans_out = [_apspan(x) for x in outs]
            for w_span, cnt in st["writes"]:
                if cnt <= st["waited"]:
                    continue
                for sp in spans_in + spans_out:
                    if _overlap(sp, w_span):
                        need = max(need, cnt)
            if need > st["waited"]:
                ph.ops.append((self._eng, RecOp("wait_ge", (ss.h, need), {})))
                st["waited"] = need
            op = RecOp(meth, a, k)
            if meth != "dma_start":
                op.incs.append((ss.h, 1))
                ss.inc()
                for sp in spans_out:
                    if sp:
                        st["writes"].append((sp, ss.n))
                if len(st["writes"]) > 24:
                    st["writes"] = st["writes"][-24:]
            ph.ops.append((self._eng, op))
            return op
        return call


class Phase:
    def __init__(self, nc, selfsem=None):
        self.nc = nc
        self.ops = []
        self.selfsem = selfsem or {}
        self.nblocks = 0
        self.marks = []
        self.dep = {}
        for e in ENGINES:
            setattr(self, e, Rec(self, e))

    def mark(self, label):
        snap = self.snap() if hasattr(self, "snap") else {}
        self.marks.append((label, self.nblocks, snap))

    def run(self):
        if not self.ops:
            return
        used = [e for e in ENGINES if any(o[0] == e for o in self.ops)]
        with self.nc.Block() as blk:
            for e in used:
                eops = [op for en, op in self.ops if en == e]
                getattr(blk, e)(lambda eng, eops=eops: [op.replay(eng) for op in eops])
        self.ops = []
        self.nblocks += 1


class S:
    def __init__(self, h):
        self.h, self.n = h, 0

    def inc(self, k=1):
        self.n += k
        return self.n


class FArena:

    def __init__(self, nc, name, dtype, cols):
        self.t = nc.alloc_sbuf_tensor(name, [128, cols], dtype)
        self.free = [(0, cols)]
        self.meta = {}
        self.name = name

    def alloc16(self, n, align=16, top=False):
        assert n % 2 == 0
        ap = self.alloc(n // 2, align, top)
        b = ap.bitcast(BF16)
        self.meta[id(b)] = self.meta.pop(id(ap))
        return b

    def alloc(self, n, align=16, top=False):
        order = list(enumerate(self.free))
        if top:
            order = order[::-1]
        for idx, (o, ln) in order:
            if top:
                oa = (o + ln - n) // align * align
                if oa < o:
                    continue
            else:
                oa = (o + align - 1) // align * align
                if oa + n > o + ln:
                    continue
            pre = (o, oa - o)
            post = (oa + n, o + ln - oa - n)
            repl = [p for p in (pre, post) if p[1] > 0]
            self.free[idx:idx + 1] = repl
            ap = self.t[:, oa:oa + n]
            self.meta[id(ap)] = (oa, n)
            return ap
        raise MemoryError(f"{self.name}: no slot for {n} (free={self.free})")

    def freeap(self, ap):
        o, n = self.meta.pop(id(ap))
        self.free.append((o, n))
        self.free.sort()
        merged = []
        for seg in self.free:
            if merged and merged[-1][0] + merged[-1][1] == seg[0]:
                merged[-1] = (merged[-1][0], merged[-1][1] + seg[1])
            else:
                merged.append(list(seg) and (seg[0], seg[1]))
        self.free = merged

    def used_bytes(self, esz):
        tot = self.t.shape[1] - sum(n for _, n in self.free)
        return tot * esz


def bcast_axis(ap, pos, n):
    pairs = [list(p) for p in list(ap.ap)]
    pairs.insert(pos, [0, n])
    return bass.AP(ap.tensor, ap.offset, pairs)


def _host_constants():
    c = {}
    c["ident"] = np.eye(128, dtype=BF)
    c["id8f"] = np.eye(8, dtype=np.float32)
    c["id1f"] = np.ones((1, 1), np.float32)
    kk = np.arange(128)[:, None]
    qq = np.arange(128)[None, :]
    c["m0t"] = np.where(kk > qq, 0.0, NEG).astype(BF)
    c["m4t"] = np.where(kk <= qq, 0.0, NEG).astype(BF)
    half = np.arange(0, DH, 2, dtype=np.float64)
    inv_freq = 1.0 / 10000.0 ** (half / DH)
    t = np.arange(T, dtype=np.float64)
    freqs = np.concatenate([t[:, None] * inv_freq[None, :]] * 2, axis=-1)
    base = (half + 0.4 * DH) / (1.4 * DH)
    power = (t - T // 2) / float(WIN // 2)
    scale = np.concatenate([base[None, :] ** power[:, None]] * 2, axis=-1)
    cos, sin = np.cos(freqs), np.sin(freqs)

    def pk(a):
        return np.ascontiguousarray(
            a.reshape(16, 128, DH).transpose(1, 0, 2).reshape(128, 16 * DH)
        ).astype(BF)
    sgn = np.concatenate([-np.ones(DH // 2), np.ones(DH // 2)])
    c["qcos"] = pk(cos * scale * 8.0)
    c["qsin"] = pk(sin * scale * 8.0 * sgn)
    c["kcos"] = pk(cos / scale)
    c["ksin"] = pk(sin / scale * sgn)
    e4 = np.zeros((128, 16), np.float32)
    for t in range(4):
        e4[:, t * 4 + t] = 1.0
    c["e4"] = e4.astype(BF)
    sd = np.zeros((128, 128), np.float32)
    sd[np.arange(127), np.arange(1, 128)] = 1.0
    c["shiftmat"] = sd
    ep = np.zeros((128, 128), np.float32)
    ep[127, 0] = 1.0
    c["prevmat"] = ep
    return c


def _w1_perm():
    idx = []
    for b in range(21):
        idx.extend(range(b * 128, b * 128 + 128))
        idx.extend(range(HID + b * 128, HID + b * 128 + 128))
    idx.extend(range(21 * 128, HID))
    idx.extend(range(HID + 21 * 128, FF2))
    return np.array(idx)


def _w1_chunks():
    out = []
    off = 0
    for b in range(21):
        out.append((off, 128, "a", b)); off += 128
        out.append((off, 128, "g", b)); off += 128
    out.append((off, 42, "a", 21)); off += 42
    out.append((off, 42, "g", 21)); off += 42
    return out


def build(dbg=(), stop_after=None):
    nc = bass.Bass("TRN2", target_bir_lowering=False)
    C = _host_constants()

    dx = nc.dram_tensor("x", [T, D], F32, kind="ExternalInput").ap()
    dout = nc.dram_tensor("out", [T, D], F32, kind="ExternalOutput").ap()
    park = nc.dram_tensor("park_x1", [T, D], F32, kind="Internal").ap()

    dWT = {w: nc.dram_tensor(w, [D, D], F32, kind="ExternalInput").ap()
           for w in ("WoutT",)}
    dssm = nc.dram_tensor("ssmprep", [128, 8 * 3 * 8 * 128], F32,
                          kind="ExternalInput").ap()
    dbig = nc.dram_tensor("b_ig", [1, D], F32, kind="ExternalInput").ap()
    dbrg = nc.dram_tensor("b_rg", [1, D], F32, kind="ExternalInput").ap()
    drows8 = {w: nc.dram_tensor(w, [8, 128], F32, kind="ExternalInput").ap()
              for w in ("lam8", "h08", "nw8", "lg8_0", "lb8_0", "lg8_1", "lb8_1")}
    dwqkv = [nc.dram_tensor(f"wqkv{l}", [D, 1536], F32, kind="ExternalInput").ap()
             for l in range(2)]
    dwo = [nc.dram_tensor(f"wo{l}", [512, D], F32, kind="ExternalInput").ap()
           for l in range(2)]
    dw1 = [nc.dram_tensor(f"w1prep{l}", [128, 8 * FF2], F32,
                          kind="ExternalInput").ap()
           for l in range(2)]
    db1 = [nc.dram_tensor(f"b1p{l}", [1, FF2], F32, kind="ExternalInput").ap()
           for l in range(2)]
    dw1b = [nc.dram_tensor(f"w1b{l}", [128, 8 * FF2], BF16, kind="Internal").ap()
            for l in range(2)]
    dw2b = [nc.dram_tensor(f"w2b{l}", [HID, D], BF16, kind="Internal").ap()
            for l in range(2)]
    dw2 = [nc.dram_tensor(f"w2_{l}", [HID, D], F32, kind="ExternalInput").ap()
           for l in range(2)]
    db2 = [nc.dram_tensor(f"b2_{l}", [1, D], F32, kind="ExternalInput").ap()
           for l in range(2)]
    dqs = [nc.dram_tensor(f"qs{l}", [1, DH], F32, kind="ExternalInput").ap()
           for l in range(2)]
    dks = [nc.dram_tensor(f"ks{l}", [1, DH], F32, kind="ExternalInput").ap()
           for l in range(2)]

    dconst = {k: nc.inline_tensor(C[k], "c_" + k).ap() for k in C}

    dbg_out = {}

    def dbgt(name, shape, dtype=F32):
        if name in dbg:
            dbg_out[name] = nc.dram_tensor("dbg_" + name, list(shape), dtype,
                                           kind="ExternalOutput").ap()
            return dbg_out[name]
        return None

    AR = FArena(nc, "ar", F32, 47360)

    class _A32:
        alloc = staticmethod(AR.alloc)
        freeap = staticmethod(AR.freeap)

    class _A16:
        alloc = staticmethod(AR.alloc16)
        freeap = staticmethod(AR.freeap)

    A32, A16 = _A32, _A16

    XY = A32.alloc(NT * D)
    XYv = XY.rearrange("p (i n) -> p i n", n=D)
    rstd_cols = A32.alloc(16)
    mean_cols = A32.alloc(16)
    cp = A32.alloc(64)
    hcol = A32.alloc(8)
    h0col = A32.alloc(8)
    nwcol = A32.alloc(8)
    lgcol = [A32.alloc(8) for _ in range(2)]
    lbcol = [A32.alloc(8) for _ in range(2)]
    fstat = A32.alloc(NT * 2)
    fstatv = fstat.rearrange("p (i k) -> p i k", k=2)
    id8f = A32.alloc(8)
    id1f = A32.alloc(1)
    ones128f = A32.alloc(128)
    qsrowb = [A32.alloc(64) for _ in range(2)]
    ksrowb = [A32.alloc(64) for _ in range(2)]
    ssqrow = A32.alloc(512)
    stage8 = A32.alloc(128)

    ident = A16.alloc(128)
    onescol = A16.alloc(16)
    e4b = A16.alloc(16)
    onesrow5 = A16.alloc(512)
    bigrow = A16.alloc(D)
    brgrow = A16.alloc(D)

    def row(ap, n=None, r0=0, rn=1):
        return ap[r0:r0 + rn, 0:(n if n else ap.shape[1])]

    PB01 = [nc.alloc_psum_tensor(f"pb{i}", [128, 512], F32) for i in range(2)]
    SIMP = [nc.alloc_psum_tensor(f"simp{i}", [128, 1024], F32) for i in range(2)]
    SIMa = [s[:] for s in SIMP]
    PBa = [PB01[0][:], PB01[1][:], SIMa[0][:, 0:512], SIMa[0][:, 512:1024],
           SIMa[1][:, 0:512], SIMa[1][:, 512:1024]]
    PB = PBa
    PT16 = nc.alloc_psum_tensor("pt16", [128, 1024], BF16)[:]
    PR = nc.alloc_psum_tensor("pr", [128, 512], F32)[:]
    PT16b = PR.bitcast(BF16)

    sem_h = {n: nc.semaphore(n).__enter__()
             for n in ("pe", "sc", "dv", "pl", "dmg", "dms", "dmx",
                       "xpa", "xpb", "upa", "upb", "sta", "stb", "w2a", "w2b",
                       "pia", "pib", "poa", "pob", "w2c")}
    sPE, sSC, sDV, sPL = (S(sem_h[n]) for n in ("pe", "sc", "dv", "pl"))
    sDG, sDS, sDX = (S(sem_h[n]) for n in ("dmg", "dms", "dmx"))
    sXP = [S(sem_h["xpa"]), S(sem_h["xpb"])]
    sW2 = [S(sem_h["w2a"]), S(sem_h["w2b"])]
    sPI = [S(sem_h["pia"]), S(sem_h["pib"])]
    sPO = [S(sem_h["poa"]), S(sem_h["pob"])]
    sW3 = [S(sem_h["w2a"]), S(sem_h["w2b"]), S(sem_h["w2c"])]
    sUP = [S(sem_h["upa"]), S(sem_h["upb"])]
    sST = [S(sem_h["sta"]), S(sem_h["stb"])]

    P = Phase(nc, selfsem={"vector": sDV, "scalar": sSC, "gpsimd": sPL})
    P.snap = lambda: {"pe": sPE.n, "dv": sDV.n, "sc": sSC.n}

    cregs = A32.alloc(16)
    _cvals = [1e-08, 4.0, 0.5, -0.5, -0.25, 0.25 * (1.0 + 1e-06), 1.0 / D, 1e-06]
    for _ci, _cv in enumerate(_cvals):
        nc.const_aps.register(F32, float(_cv), cregs[:, _ci:_ci + 1])
        P.gpsimd.memset(cregs[:, _ci:_ci + 1], float(_cv))

    P.sync.dma_start(XYv, dx.rearrange("(i p) n -> p i n", p=128)) \
        .then_inc(sDS.h, 16); sDS.inc(16)
    P.sync.dma_start(ident, dconst["ident"]).then_inc(sDS.h, 16); sDS.inc(16)
    P.sync.dma_start(e4b, dconst["e4"]).then_inc(sDS.h, 16); sDS.inc(16)
    P.sync.dma_start(id8f[0:8, :], dconst["id8f"]).then_inc(sDS.h, 16); sDS.inc(16)
    P.sync.dma_start(id1f[0:1, :], dconst["id1f"]).then_inc(sDS.h, 16); sDS.inc(16)
    P.gpsimd.dma_start(row(bigrow), dbig).then_inc(sDG.h, 16); sDG.inc(16)
    P.gpsimd.dma_start(row(brgrow), dbrg).then_inc(sDG.h, 16); sDG.inc(16)
    for l in range(2):
        P.sync.dma_start(row(qsrowb[l]), dqs[l]).then_inc(sDS.h, 16); sDS.inc(16)
        P.sync.dma_start(row(ksrowb[l]), dks[l]).then_inc(sDS.h, 16); sDS.inc(16)
    P.gpsimd.memset(row(onesrow5), 1.0)
    P.gpsimd.memset(onescol[:, 0:1], 1.0)
    P.gpsimd.memset(row(ones128f), 1.0).then_inc(sPL.h, 1); sPL.inc()
    P.run()

    def row_to_col(dname, dst_col, prep=False):
        P.sync.dma_start(stage8[0:8, :], drows8[dname]).then_inc(sDS.h, 16); sDS.inc(16)
        if prep:
            P.scalar.wait_ge(sDS.h, sDS.n)
            P.scalar.activation(stage8[0:8, :], stage8[0:8, :], AF.Sigmoid)
            P.scalar.activation(stage8[0:8, :], stage8[0:8, :], AF.Ln, bias=1e-08,
                                scale=1.0)
            P.scalar.mul(stage8[0:8, :], stage8[0:8, :], 4.0) \
                .then_inc(sSC.h, 1); sSC.inc()
            P.tensor.wait_ge(sSC.h, sSC.n)
        else:
            P.tensor.wait_ge(sDS.h, sDS.n)
        P.tensor.wait_ge(sDS.h, sDS.n)
        P.tensor.wait_ge(sDV.h, sDV.n)
        P.tensor.transpose(PR[:, 0:8], stage8[0:8, :], id8f[0:8, :]) \
            .then_inc(sPE.h, 1); sPE.inc()
        P.vector.wait_ge(sPE.h, sPE.n)
        P.vector.tensor_copy(dst_col, PR[:, 0:8]).then_inc(sDV.h, 1); sDV.inc()
        P.run()

    row_to_col("lam8", hcol, prep=True)
    row_to_col("h08", h0col)
    row_to_col("nw8", nwcol)
    for l in range(2):
        row_to_col(f"lg8_{l}", lgcol[l])
        row_to_col(f"lb8_{l}", lbcol[l])

    sq_scr = A32.alloc(D)

    def ln_stats(src_ap, mu, var, save_mean_col=None, known_mu=None):
        if known_mu is None:
            P.vector.reduce_sum(mu, src_ap, axis=AX.X)
            P.vector.tensor_scalar_mul(mu, mu, 1.0 / D)
        else:
            P.vector.tensor_copy(mu, known_mu)
        if save_mean_col is not None:
            P.vector.tensor_copy(save_mean_col, mu)
        P.vector.scalar_tensor_tensor(sq_scr, src_ap, 1.0, src_ap,
                                      op0=ALU.mult, op1=ALU.mult, accum_out=var)
        P.vector.tensor_scalar(cp[:, 63:64], mu, mu, -1.0, op0=ALU.mult, op1=ALU.mult)
        P.vector.scalar_tensor_tensor(var, var, 1.0 / D, cp[:, 63:64],
                                      op0=ALU.mult, op1=ALU.add)
        P.vector.tensor_scalar_add(var, var, 1e-05)

    P.mark("P0")
    xnT = A16.alloc(8 * T)
    xnTv = xnT.rearrange("p (c n) -> p c n", c=8)
    lntmp = A16.alloc(D)

    last_evac_sc = sSC.n
    for i in range(NT):
        xt = XYv[:, i, :]
        mu, var, rs, nb = (cp[:, k:k + 1] for k in (0, 1, 2, 3))
        ln_stats(xt, mu, var, save_mean_col=mean_cols[:, i:i + 1])
        P.vector.sem_inc(sDV.h, 1); sDV.inc()
        P.scalar.wait_ge(sDV.h, sDV.n)
        P.scalar.activation(rs, var, AF.Sqrt).then_inc(sSC.h, 1); sSC.inc()
        P.vector.wait_ge(sSC.h, sSC.n)
        P.vector.reciprocal(rs, rs)
        P.vector.wait_ge(sPE.h, sPE.n)
        P.vector.tensor_scalar(lntmp, xt, mu, rs, op0=ALU.subtract, op1=ALU.mult) \
            .then_inc(sDV.h, 1); sDV.inc()
        P.tensor.wait_ge(sDV.h, sDV.n)
        P.tensor.wait_ge(sSC.h, last_evac_sc)
        for c in range(8):
            op = P.tensor.transpose(PT16[:, c * 128:(c + 1) * 128],
                                    lntmp[:, c * 128:(c + 1) * 128], ident)
        op.then_inc(sPE.h, 1); sPE.inc()
        P.scalar.wait_ge(sPE.h, sPE.n)
        for c in range(8):
            op = P.scalar.copy(xnTv[:, c, i * 128:(i + 1) * 128],
                               PT16[:, c * 128:(c + 1) * 128])
        op.then_inc(sSC.h, 1); sSC.inc()
        last_evac_sc = sSC.n
    P.run()

    dbg_lnT = dbgt("ln1T", [128, 8 * T])
    if dbg_lnT is not None:
        scr = A32.alloc(T)
        for c in range(8):
            P.vector.tensor_copy(scr, xnTv[:, c, :]).then_inc(sDV.h, 1); sDV.inc()
            P.sync.wait_ge(sDV.h, sDV.n)
            P.sync.dma_start(dbg_lnT.rearrange("p (c n) -> p c n", c=8)[:, c, :], scr) \
                .then_inc(sDS.h, 16); sDS.inc(16)
            P.vector.wait_ge(sDS.h, sDS.n)
        P.run()
        A32.freeap(scr)

    if stop_after == "P1":
        _dev_finish(P, sDX, dout, XYv)
        return nc, dbg_out

    P.mark("P1_ln1")
    A16.freeap(lntmp)
    hs_bf = A16.alloc(8 * T)
    hs_bfv = hs_bf.rearrange("p (c n) -> p c n", c=8)
    HSQB = A16.alloc(T)
    Wst = A16.alloc(2 * 3 * 8 * 128)
    Wstv = Wst.rearrange("p (b j k n) -> p b j k n", b=2, j=3, k=8)
    Abuf = A32.alloc(T)
    Gbuf = A32.alloc(T)
    ASQ = A32.alloc(T)
    HS = ASQ

    dbg_a = dbgt("A0", [128, T]); dbg_g = dbgt("G0", [128, T])
    dbg_h = dbgt("HS0", [128, T])

    w_free_pe = [0, 0]
    w_ready = [0, 0]
    psum_free = {}
    scan_dv = 0
    pcjobs = []
    rsplit = [0, 684, 1368, 2052, HID]
    for _l in range(2):
        for k in range(8):
            pcjobs.append((dw1b[_l][:, k * FF2:(k + 1) * FF2],
                           dw1[_l][:, k * FF2:(k + 1) * FF2]))
        for k in range(4):
            pcjobs.append((dw2b[_l][rsplit[k]:rsplit[k + 1], :],
                           dw2[_l][rsplit[k]:rsplit[k + 1], :]))
    pc_gate = [0, 0]
    ssq_pe = 0
    for hc in range(8):
        wb = hc % 2
        P.gpsimd.wait_ge(sPE.h, w_free_pe[wb])
        P.gpsimd.dma_start(
            Wstv[:, wb], dssm[:, hc * 3072:(hc + 1) * 3072]
        ).then_inc(sDG.h, 16); sDG.inc(16)
        w_ready[wb] = sDG.n
        for _ in range(3):
            if pcjobs:
                _dst, _src = pcjobs.pop(0)
                P.gpsimd.dma_start(_dst, _src).then_inc(sDG.h, 16); sDG.inc(16)
                _ndone = 24 - len(pcjobs)
                if _ndone == 12:
                    pc_gate[0] = sDG.n
                elif _ndone == 24:
                    pc_gate[1] = sDG.n

        for tt in range(4):
            tsl = slice(tt * 512, tt * 512 + 512)
            pp = tt % 2
            Bj = [PBa[3 * pp + j] for j in range(3)]
            P.tensor.wait_ge(sDG.h, w_ready[wb])
            if psum_free.get(pp):
                P.tensor.wait_ge(sSC.h, psum_free[pp][0])
                P.tensor.wait_ge(sDV.h, psum_free[pp][1])
            for j in range(3):
                for kc in range(8):
                    op = P.tensor.matmul(Bj[j], Wstv[:, wb, j, kc, :],
                                         xnTv[:, kc, tsl],
                                         start=(kc == 0), stop=(j == 0 and kc == 7))
                if j == 1:
                    op = P.tensor.matmul(Bj[j], row(bigrow)[:, hc * 128:(hc + 1) * 128],
                                         row(onesrow5), start=False, stop=True)
                elif j == 2:
                    op = P.tensor.matmul(Bj[j], row(brgrow)[:, hc * 128:(hc + 1) * 128],
                                         row(onesrow5), start=False, stop=True)
            op.then_inc(sPE.h, 1); sPE.inc()
            mm_pe = sPE.n
            if tt == 3:
                w_free_pe[wb] = sPE.n

            hcc = hcol[:, hc:hc + 1]
            P.scalar.wait_ge(sPE.h, mm_pe)
            if hc > 0 and tt == 0:
                P.scalar.wait_ge(sDV.h, scan_dv)
            P.scalar.activation(Gbuf[:, tsl], Bj[1], AF.Tanh, scale=0.5)
            P.scalar.activation(Abuf[:, tsl], Bj[2], AF.Tanh, scale=0.5)
            P.scalar.activation(Abuf[:, tsl], Abuf[:, tsl], AF.Exp,
                                bias=hcc, scale=hcc) \
                .then_inc(sSC.h, 1); sSC.inc()
            P.vector.wait_ge(sSC.h, sSC.n)
            P.vector.scalar_tensor_tensor(Gbuf[:, tsl], Gbuf[:, tsl], 1.0, Bj[0],
                                          op0=ALU.add, op1=ALU.mult)
            P.vector.tensor_tensor(ASQ[:, tsl], Abuf[:, tsl], Abuf[:, tsl],
                                   op=ALU.mult).then_inc(sDV.h, 1); sDV.inc()
            psum_free[pp] = (sSC.n, sDV.n)

        P.scalar.wait_ge(sDV.h, psum_free[0][1])
        P.scalar.wait_ge(sDV.h, psum_free[1][1])
        for tt in range(4):
            tsl = slice(tt * 512, tt * 512 + 512)
            op = P.scalar.activation(ASQ[:, tsl], ASQ[:, tsl], AF.Sqrt,
                                     scale=-0.25, bias=0.25 * (1.0 + 1e-06))
        op.then_inc(sSC.h, 1); sSC.inc()
        P.vector.wait_ge(sSC.h, sSC.n)
        P.vector.tensor_tensor(Gbuf, Gbuf, ASQ, op=ALU.mult)
        P.vector.tensor_tensor_scan(HS, Abuf, Gbuf, h0col[:, hc:hc + 1],
                                    op0=ALU.mult, op1=ALU.add) \
            .then_inc(sDV.h, 1); sDV.inc()
        scan_dv = sDV.n
        P.vector.tensor_tensor(HSQB, HS, HS, op=ALU.mult)
        P.vector.tensor_scalar_mul(hs_bfv[:, hc, :], HS, nwcol[:, hc:hc + 1]) \
            .then_inc(sDV.h, 1); sDV.inc()
        hsq_dv = sDV.n

        P.tensor.wait_ge(sDV.h, hsq_dv)
        for tt in range(4):
            tsl = slice(tt * 512, tt * 512 + 512)
            op = P.tensor.matmul(PR[0:4, 0:512], e4b[:, tt * 4:(tt + 1) * 4],
                                 HSQB[:, tsl], start=(hc == 0), stop=(hc == 7))
        op.then_inc(sPE.h, 1); sPE.inc()
        ssq_pe = sPE.n

        if hc == 0 and (dbg_a is not None or dbg_g is not None or dbg_h is not None):
            P.sync.wait_ge(sDV.h, sDV.n)
            for o, b in ((dbg_a, Abuf), (dbg_g, Gbuf), (dbg_h, HS)):
                if o is not None:
                    P.sync.dma_start(o, b).then_inc(sDS.h, 16); sDS.inc(16)
            P.vector.wait_ge(sDS.h, sDS.n)
        P.run()

    P.mark("P2_ssm")
    P.scalar.wait_ge(sPE.h, ssq_pe)
    P.scalar.activation(ssqrow[0:4, 0:512], PR[0:4, 0:512], AF.Sqrt,
                        scale=1.0 / D, bias=1e-06).then_inc(sSC.h, 1); sSC.inc()
    P.vector.wait_ge(sSC.h, sSC.n)
    P.vector.reciprocal(ssqrow[0:4, 0:512], ssqrow[0:4, 0:512]) \
        .then_inc(sDV.h, 1); sDV.inc()
    P.tensor.wait_ge(sDV.h, sDV.n)
    for q in range(4):
        op = P.tensor.transpose(PR[:, 128 + q * 4:128 + (q + 1) * 4],
                                ssqrow[0:4, q * 128:(q + 1) * 128], id8f[0:4, 0:4])
    op.then_inc(sPE.h, 1); sPE.inc()
    P.vector.wait_ge(sPE.h, sPE.n)
    P.vector.tensor_copy(
        rstd_cols.rearrange("p (r q) -> p r q", r=4),
        PR[:, 128:144].rearrange("p (q r) -> p r q", q=4)) \
        .then_inc(sDV.h, 1); sDV.inc()
    P.run()

    for ap in (Abuf, Gbuf, ASQ, ssqrow):
        A32.freeap(ap)
    A16.freeap(Wst)
    A16.freeap(HSQB)
    A16.freeap(xnT)

    P.mark("P3")
    WoF = A16.alloc(8 * D)
    WoFv = WoF.rearrange("p (c n) -> p c n", c=8)
    x1b = A32.alloc(2 * D)
    x1bv = x1b.rearrange("p (b n) -> p b n", b=2)
    SSo = A32.alloc(D)
    P.gpsimd.dma_start(WoFv, dWT["WoutT"].rearrange("(c p) n -> p c n", p=128)) \
        .then_inc(sDG.h, 16); sDG.inc(16)
    wout_ready = sDG.n

    dbg_y0 = dbgt("y0", [T, D])
    park_done = {}
    x1b_free_sc = [0, 0]
    evac_free = 0
    for i in range(NT):
        tsl = slice(i * 128, (i + 1) * 128)
        P.tensor.wait_ge(sDG.h, wout_ready)
        if evac_free:
            P.tensor.wait_ge(sSC.h, evac_free)
        for half in range(2):
            for c in range(8):
                op = P.tensor.matmul(PBa[4 + half], hs_bfv[:, c, tsl],
                                     WoFv[:, c, half * 512:(half + 1) * 512],
                                     start=(c == 0), stop=(c == 7))
        op.then_inc(sPE.h, 1); sPE.inc()
        rc = rstd_cols[:, i:i + 1]
        P.scalar.wait_ge(sPE.h, sPE.n)
        P.scalar.activation(SSo[:, 0:512], PBa[4], AF.Copy, scale=rc)
        P.scalar.activation(SSo[:, 512:1024], PBa[5], AF.Copy, scale=rc) \
            .then_inc(sSC.h, 1); sSC.inc()
        evac_free = sSC.n
        mu, var, rs, nb = (cp[:, k:k + 1] for k in (0, 1, 2, 3))
        P.vector.wait_ge(sSC.h, sSC.n)
        ln_stats(SSo, mu, var)
        P.vector.sem_inc(sDV.h, 1); sDV.inc()
        P.scalar.wait_ge(sDV.h, sDV.n)
        P.scalar.activation(rs, var, AF.Sqrt).then_inc(sSC.h, 1); sSC.inc()
        P.vector.wait_ge(sSC.h, sSC.n)
        P.vector.reciprocal(rs, rs)
        P.vector.tensor_scalar(nb, mu, rs, -1.0, op0=ALU.mult, op1=ALU.mult) \
            .then_inc(sDV.h, 1); sDV.inc()
        P.scalar.wait_ge(sDV.h, sDV.n)
        P.scalar.activation(SSo, SSo, AF.Identity, scale=rs, bias=nb) \
            .then_inc(sSC.h, 1); sSC.inc()
        bpar = i % 2
        P.vector.wait_ge(sSC.h, sSC.n)
        if i >= 2:
            P.vector.wait_ge(sXP[bpar].h, sXP[bpar].n)
            P.vector.wait_ge(sSC.h, x1b_free_sc[bpar])
        P.vector.tensor_tensor(x1bv[:, bpar, :], XYv[:, i, :], SSo, op=ALU.add) \
            .then_inc(sDV.h, 1); sDV.inc()
        x1_dv = sDV.n
        P.scalar.wait_ge(sDV.h, x1_dv)
        P.scalar.dma_start(park.rearrange("(i p) n -> p i n", p=128)[:, i, :],
                           x1bv[:, bpar, :]).then_inc(sXP[bpar].h, 16)
        sXP[bpar].inc(16)
        mu2, var2, rs2, nb2 = (cp[:, k:k + 1] for k in (8, 9, 10, 11))
        ln_stats(x1bv[:, bpar, :], mu2, var2, known_mu=mean_cols[:, i:i + 1])
        P.vector.sem_inc(sDV.h, 1); sDV.inc()
        P.scalar.wait_ge(sDV.h, sDV.n)
        P.scalar.activation(rs2, var2, AF.Sqrt).then_inc(sSC.h, 1); sSC.inc()
        P.vector.wait_ge(sSC.h, sSC.n)
        P.vector.reciprocal(rs2, rs2)
        P.vector.tensor_scalar(nb2, mu2, rs2, -1.0, op0=ALU.mult, op1=ALU.mult) \
            .then_inc(sDV.h, 1); sDV.inc()
        P.scalar.wait_ge(sDV.h, sDV.n)
        P.scalar.activation(XYv[:, i, :], x1bv[:, bpar, :], AF.Identity,
                            scale=rs2, bias=nb2).then_inc(sSC.h, 1); sSC.inc()
        x1b_free_sc[bpar] = sSC.n
    P.run()

    if dbg_y0 is not None:
        P.sync.wait_ge(sSC.h, sSC.n)
        P.sync.dma_start(dbg_y0.rearrange("(i p) n -> p i n", p=128), XYv) \
            .then_inc(sDS.h, 16); sDS.inc(16)
        P.vector.wait_ge(sDS.h, sDS.n)
        P.run()

    A16.freeap(hs_bf)
    A16.freeap(WoF)
    A32.freeap(x1b)
    A32.freeap(SSo)

    if stop_after == "P4":
        _dev_finish(P, sDX, dout, XYv)
        return nc, dbg_out

    P.mark("P4_wout")
    dbg_qr = dbgt("qrot0", [T, 512]); dbg_kr = dbgt("krot0", [T, 512])
    dbg_o = dbgt("o0", [T, 512]); dbg_yn = dbgt("ynew0", [T, D])
    dbg_h2 = dbgt("h2ln0", [128, 8 * T]); dbg_p = dbgt("p0", [T, HID])
    dbg_y3 = dbgt("y3_0", [T, D])

    park_waited = False
    for l in range(2):
        tabs = {k: A16.alloc(NT * DH) for k in ("qcos", "qsin", "kcos", "ksin")}
        tabv = {k: v.rearrange("p (i n) -> p i n", n=DH) for k, v in tabs.items()}
        qsb = A16.alloc(DH)
        ksb = A16.alloc(DH)
        WQKV = A16.alloc(8 * 1536)
        WQKVv = WQKV.rearrange("p (c n) -> p c n", c=8)
        WO = A16.alloc(4 * D)
        WOv = WO.rearrange("p (c n) -> p c n", c=4)
        KT = A16.alloc(4 * T)
        KTv = KT.rearrange("p (c n) -> p c n", c=4)
        VB = A16.alloc(NT * 520)
        VB65v = VB.rearrange("p (i h e) -> p i h e", h=8, e=65)
        QT = A16.alloc(4 * 128)
        QTv = QT.rearrange("p (c n) -> p c n", c=4)
        yT = A16.alloc(2 * 8 * 132)
        yTv = yT.rearrange("p (b c w) -> p b c w", b=2, c=8)[:, :, :, 0:129]
        YC = A16.alloc(D)
        QRB = A16.alloc(512)
        KRB = A16.alloc(512)
        PTS = [A16.alloc(640) for _ in range(2)]
        ONB = A16.alloc(512)
        ONBv = ONB.rearrange("p (h e) -> p h e", h=8)
        OTB = A16.alloc(4 * 128)
        OTBv = OTB.rearrange("p (c n) -> p c n", c=4)
        m0t = A16.alloc(128)
        m4t = A16.alloc(128)
        QF = A16.alloc(512)
        KF = A16.alloc(512)
        SQ16 = A16.alloc(512)
        RH = A16.alloc(512)
        M2 = A16.alloc(512)
        RB8 = A16.alloc(16)
        if not park_waited:
            P.sync.wait_ge(sXP[0].h, sXP[0].n)
            P.sync.wait_ge(sXP[1].h, sXP[1].n)
            P.gpsimd.wait_ge(sXP[0].h, sXP[0].n)
            P.gpsimd.wait_ge(sXP[1].h, sXP[1].n)
            park_waited = True
        for k in ("qcos", "qsin", "kcos", "ksin"):
            P.sync.dma_start(tabs[k], dconst[k]).then_inc(sDS.h, 16); sDS.inc(16)
        P.sync.dma_start(m0t, dconst["m0t"]).then_inc(sDS.h, 16); sDS.inc(16)
        P.sync.dma_start(m4t, dconst["m4t"]).then_inc(sDS.h, 16); sDS.inc(16)
        tab_ready = sDS.n
        P.gpsimd.dma_start(WQKVv, dwqkv[l].rearrange("(c p) n -> p c n", p=128)) \
            .then_inc(sDG.h, 16); sDG.inc(16)
        P.gpsimd.dma_start(WOv, dwo[l].rearrange("(c p) n -> p c n", p=128)) \
            .then_inc(sDG.h, 16); sDG.inc(16)
        wqkv_ready = sDG.n
        P.tensor.wait_ge(sDS.h, tab_ready)
        P.tensor.wait_ge(sPL.h, 1)
        P.tensor.wait_ge(sDV.h, sDV.n)
        P.tensor.matmul(PR[:, 0:64], row(ones128f), row(qsrowb[l]),
                        start=True, stop=True)
        P.tensor.matmul(PR[:, 64:128], row(ones128f), row(ksrowb[l]),
                        start=True, stop=True).then_inc(sPE.h, 1); sPE.inc()
        P.vector.wait_ge(sPE.h, sPE.n)
        P.vector.tensor_copy(qsb, PR[:, 0:64])
        P.vector.tensor_copy(ksb, PR[:, 64:128]).then_inc(sDV.h, 1); sDV.inc()
        P.gpsimd.wait_ge(sDS.h, tab_ready)
        P.gpsimd.wait_ge(sDV.h, sDV.n)
        for k, sb in (("qcos", qsb), ("qsin", qsb), ("kcos", ksb), ("ksin", ksb)):
            for i in range(NT):
                op = P.gpsimd.tensor_tensor(tabv[k][:, i, :], tabv[k][:, i, :], sb,
                                            op=ALU.mult)
        op.then_inc(sPL.h, 1); sPL.inc()
        tabs_done = sPL.n
        P.gpsimd.memset(yTv[:, 0, :, 0:1], 0.0)
        P.gpsimd.memset(yTv[:, 1, :, 0:1], 0.0).then_inc(sPL.h, 1); sPL.inc()
        P.gpsimd.memset(VB65v[:, :, :, 64:65], 1.0).then_inc(sPL.h, 1); sPL.inc()
        vones_pl = sPL.n
        P.run()

        st = {"yc": 0, "qf": (0, 0), "qkt": 0, "wo": 0, "otb": 0, "norm": 0}
        ytr = {}
        qkvpe = {}
        qkvevac = {}
        rotp1 = {}
        rotrsq = {}
        rotdone = {}
        qkready = {}
        exp_done_sc = [0, 0]
        pts_rd_pe = [0, 0]

        def emitA_yroll(j):
            pj = j % 2
            P.vector.wait_ge(sPE.h, st["yc"])
            P.vector.tensor_copy(YC, XYv[:, j, :]).then_inc(sDV.h, 1); sDV.inc()
            P.tensor.wait_ge(sDV.h, sDV.n)
            P.tensor.wait_ge(sDV.h, qkready.get(j - 1, 0))
            for c in range(8):
                op = P.tensor.transpose(PT16b[:, c * 128:(c + 1) * 128],
                                        YC[:, c * 128:(c + 1) * 128], ident)
            op.then_inc(sPE.h, 1); sPE.inc()
            st["yc"] = sPE.n
            P.vector.wait_ge(sPE.h, sPE.n)
            P.vector.tensor_copy(yTv[:, pj, :, 1:129],
                                 PT16b.rearrange("p (c n) -> p c n", c=8))
            if j > 0:
                P.vector.tensor_copy(yTv[:, pj, :, 0:1],
                                     yTv[:, 1 - pj, :, 128:129]) \
                    .then_inc(sDV.h, 1); sDV.inc()
            ytr[j] = sDV.n

        def emitA_qkv(j, ca, cb):
            pj = j % 2
            if ca == 0:
                P.tensor.wait_ge(sDG.h, wqkv_ready)
                P.tensor.wait_ge(sDV.h, ytr[j])
                P.tensor.wait_ge(sPL.h, tabs_done + 1)
                if qkvevac.get(j - 1):
                    P.tensor.wait_ge(sDV.h, qkvevac[j - 1])
                if st["wo"]:
                    P.tensor.wait_ge(sDV.h, st["wo"])
            for c in range(ca, cb):
                lhs = yTv[:, pj, c, 1:129] if c < 4 else yTv[:, pj, c, 0:128]
                for n3 in range(3):
                    dst = PR if n3 == 2 else PBa[n3]
                    op = P.tensor.matmul(dst, lhs,
                                         WQKVv[:, c, n3 * 512:(n3 + 1) * 512],
                                         start=(c == 0), stop=(c == 7))
            op.then_inc(sPE.h, 1); sPE.inc()
            if cb == 8:
                qkvpe[j] = sPE.n

        def emitA_evac(j):
            P.vector.wait_ge(sPE.h, qkvpe[j])
            P.vector.tensor_copy(QF, PBa[0])
            P.vector.tensor_copy(KF, PBa[1])
            P.vector.tensor_copy(VB65v[:, j, :, 0:64],
                                 PR.rearrange("p (h e) -> p h e", h=8)) \
                .then_inc(sDV.h, 1); sDV.inc()
            qkvevac[j] = sDV.n

        rv3 = lambda a: a.rearrange("p (h e) -> p h e", h=8)
        rv4 = lambda a: a.rearrange("p (h b e) -> p h b e", h=8, b=2)

        def emitA_rotP1(j):
            ss_q = cp[:, 16:24]
            ss_k = cp[:, 48:56]
            P.vector.wait_ge(sPE.h, st["qkt"])
            P.vector.tensor_tensor(SQ16, QF, QF, op=ALU.mult)
            P.vector.tensor_reduce(ss_q, rv3(SQ16), axis=AX.X, op=ALU.add)
            P.vector.tensor_tensor(SQ16, KF, KF, op=ALU.mult)
            P.vector.tensor_reduce(ss_k, rv3(SQ16), axis=AX.X, op=ALU.add)
            P.vector.tensor_scalar_max(ss_q, ss_q, 1e-24)
            P.vector.tensor_scalar_max(ss_k, ss_k, 1e-24)
            P.vector.tensor_copy(rv4(RH)[:, :, 0, :], rv4(QF)[:, :, 1, :])
            P.vector.tensor_copy(rv4(RH)[:, :, 1, :], rv4(QF)[:, :, 0, :])
            P.vector.tensor_copy(rv4(M2)[:, :, 0, :], rv4(KF)[:, :, 1, :])
            P.vector.tensor_copy(rv4(M2)[:, :, 1, :], rv4(KF)[:, :, 0, :]) \
                .then_inc(sDV.h, 1); sDV.inc()
            rotp1[j] = sDV.n

        def emitA_rotRSQ(j):
            P.scalar.wait_ge(sDV.h, rotp1[j])
            P.scalar.activation(cp[:, 24:32], cp[:, 16:24], AF.Ln)
            P.scalar.activation(cp[:, 24:32], cp[:, 24:32], AF.Exp, scale=-0.5)
            P.scalar.activation(cp[:, 56:64], cp[:, 48:56], AF.Ln)
            P.scalar.activation(cp[:, 56:64], cp[:, 56:64], AF.Exp, scale=-0.5) \
                .then_inc(sSC.h, 1); sSC.inc()
            rotrsq[j] = sSC.n

        def emitA_rotP2(j):
            P.vector.wait_ge(sSC.h, rotrsq[j])
            rotq_dv = 0
            for SRC, RHx, OUTB, rn, ck, sk in (
                    (QF, RH, QRB, cp[:, 24:32], "qcos", "qsin"),
                    (KF, M2, KRB, cp[:, 56:64], "kcos", "ksin")):
                cosb = bcast_axis(tabv[ck][:, j, :], 1, 8)
                sinb = bcast_axis(tabv[sk][:, j, :], 1, 8)
                P.vector.tensor_tensor(rv3(SRC), rv3(SRC), cosb, op=ALU.mult)
                P.vector.tensor_tensor(rv3(RHx), rv3(RHx), sinb, op=ALU.mult)
                P.vector.tensor_tensor(SRC, SRC, RHx, op=ALU.add)
                rb = bcast_axis(rn, 2, 64)
                P.vector.tensor_tensor(rv3(OUTB), rv3(SRC), rb, op=ALU.mult) \
                    .then_inc(sDV.h, 1); sDV.inc()
                if SRC is QF:
                    rotq_dv = sDV.n
            rotdone[j] = (rotq_dv, sDV.n)
            if dbg_qr is not None and l == 0:
                P.gpsimd.wait_ge(sDV.h, rotdone[j][1])
                P.gpsimd.dma_start(dbg_qr.rearrange("(i p) n -> p i n", p=128)[:, j, :],
                                   QRB).then_inc(sDG.h, 16); sDG.inc(16)
                P.gpsimd.dma_start(dbg_kr.rearrange("(i p) n -> p i n", p=128)[:, j, :],
                                   KRB).then_inc(sDG.h, 16); sDG.inc(16)
                P.tensor.wait_ge(sDG.h, sDG.n)

        def emitA_qkT(j, half):
            if half == 0:
                P.tensor.wait_ge(sDV.h, rotdone[j][0])
                P.tensor.wait_ge(sDV.h, qkvevac[j])
                st["qkt"] = sPE.n + 1
                for c in range(4):
                    op = P.tensor.transpose(PT16b[:, c * 128:(c + 1) * 128],
                                            QRB[:, c * 128:(c + 1) * 128], ident)
                op.then_inc(sPE.h, 1); sPE.inc()
                return
            P.tensor.wait_ge(sDV.h, rotdone[j][1])
            for c in range(4):
                op = P.tensor.transpose(PT16b[:, 512 + c * 128:512 + (c + 1) * 128],
                                        KRB[:, c * 128:(c + 1) * 128], ident)
            op.then_inc(sPE.h, 1); sPE.inc()
            st["qkt"] = sPE.n
            P.vector.wait_ge(sPE.h, sPE.n)
            jsl = slice(j * 128, (j + 1) * 128)
            P.vector.tensor_copy(QT, PT16b[:, 0:512])
            P.vector.tensor_copy(
                KTv[:, :, jsl],
                PT16b[:, 512:1024].rearrange("p (c n) -> p c n", c=4)) \
                .then_inc(sDV.h, 1); sDV.inc()
            qkready[j] = sDV.n

        emitA_yroll(0)
        emitA_qkv(0, 0, 8)
        emitA_evac(0)
        emitA_rotP1(0)
        emitA_rotRSQ(0)
        emitA_rotP2(0)
        emitA_qkT(0, 0)
        emitA_qkT(0, 1)
        emitA_yroll(1)

        def tile_geo(j):
            if j < 4:
                return 128 * (j + 1), 0
            return 640, j * 128 - 512

        sim_pe = {}
        exp_sc = {}
        pv_pe = {}
        oev_sc = {}

        def emit_sim(j, h):
            nk, k0 = tile_geo(j)
            nch = nk // 128
            p = h % 2
            ch, pb = h // 2, 64 * (h % 2)
            Sb = SIMa[p]
            P.tensor.wait_ge(sDV.h, qkready[j])
            P.tensor.wait_ge(sSC.h, exp_done_sc[p])
            P.tensor.wait_ge(sDV.h, st["norm"])
            for c in range(nch):
                m = m4t if c == nch - 1 else \
                    (m0t if (c == 0 and nk == 640) else None)
                op = P.tensor.matmul(
                    Sb[:, c * 128:(c + 1) * 128],
                    KTv[:, ch, k0 + c * 128:k0 + (c + 1) * 128][pb:pb + 64, :],
                    QTv[:, ch, :][pb:pb + 64, :],
                    start=True, stop=(m is None))
                if m is not None:
                    op = P.tensor.matmul(Sb[:, c * 128:(c + 1) * 128],
                                         ident, m, start=False, stop=True)
            op.then_inc(sPE.h, 1); sPE.inc()
            sim_pe[(j, h)] = sPE.n

        def emit_exp(j, h):
            nk, k0 = tile_geo(j)
            p = h % 2
            P.scalar.wait_ge(sPE.h, sim_pe[(j, h)])
            P.scalar.wait_ge(sPE.h, pts_rd_pe[p])
            P.scalar.activation(PTS[p][:, 0:nk], SIMa[p][:, 0:nk], AF.Exp)
            exp_sc[(j, h)] = sSC.n
            exp_done_sc[p] = sSC.n

        emit_sim(0, 0)
        emit_sim(0, 1)
        emit_exp(0, 0)
        for i in range(NT):
            nk, k0 = tile_geo(i)
            nch = nk // 128
            tsl = slice(i * 128, (i + 1) * 128)
            if i + 1 < NT:
                emitA_qkv(i + 1, 0, 8)
            for h in range(HEADS):
                p = h % 2
                obank, ocol = SIMa[p], 640 + (h // 2) * 65
                if h + 2 < HEADS:
                    emit_sim(i, h + 2)
                P.tensor.wait_ge(sSC.h, exp_sc[(i, h)])
                if h == 0:
                    P.tensor.wait_ge(sPL.h, vones_pl)
                for c in range(nch):
                    vt = (k0 + c * 128) // 128
                    op = P.tensor.matmul(obank[:, ocol:ocol + 65],
                                         PTS[p][:, c * 128:(c + 1) * 128],
                                         VB65v[:, vt, h, :],
                                         start=(c == 0), stop=(c == nch - 1))
                op.then_inc(sPE.h, 1); sPE.inc()
                pv_pe[(i, h)] = sPE.n
                pts_rd_pe[p] = sPE.n
                if i + 1 < NT:
                    if h == 3:
                        emitA_evac(i + 1)
                        emitA_rotP1(i + 1)
                    elif h == 5:
                        emitA_rotRSQ(i + 1)
                        emitA_rotP2(i + 1)
                    elif h == 6:
                        emitA_qkT(i + 1, 0)
                    elif h == 7:
                        emitA_qkT(i + 1, 1)
                        if i + 2 < NT:
                            emitA_yroll(i + 2)
                if h + 1 < HEADS:
                    emit_exp(i, h + 1)
                P.scalar.wait_ge(sPE.h, pv_pe[(i, h)])
                P.scalar.copy(ONBv[:, h, :], obank[:, ocol:ocol + 64])
                oev_sc[h] = sSC.n
            tails0 = SIMa[0][:, 640:900].rearrange("p (k s) -> p k s", s=65)
            tails1 = SIMa[1][:, 640:900].rearrange("p (k s) -> p k s", s=65)
            rd8 = cp[:, 32:40]
            rd8v = rd8.rearrange("p (k e) -> p k e", e=2)
            P.vector.wait_ge(sPE.h, pv_pe[(i, HEADS - 1)])
            P.vector.reciprocal(rd8v[:, :, 0:1], tails0[:, :, 64:65])
            P.vector.reciprocal(rd8v[:, :, 1:2], tails1[:, :, 64:65])
            P.vector.wait_ge(sSC.h, oev_sc[HEADS - 1])
            P.vector.tensor_tensor(rv3(ONB), rv3(ONB), bcast_axis(rd8, 2, 64),
                                   op=ALU.mult).then_inc(sDV.h, 1); sDV.inc()
            st["norm"] = sDV.n
            P.tensor.wait_ge(sDV.h, st["norm"])
            P.tensor.wait_ge(sSC.h, st["otb"])
            for c in range(4):
                op = P.tensor.transpose(PT16[:, c * 128:(c + 1) * 128],
                                        ONB[:, c * 128:(c + 1) * 128], ident)
            op.then_inc(sPE.h, 1); sPE.inc()
            ot_pe = sPE.n
            P.scalar.wait_ge(sPE.h, ot_pe)
            P.scalar.copy(OTB, PT16[:, 0:512])
            st["otb"] = sSC.n
            ot_ready = sSC.n
            if i + 1 < NT:
                emit_sim(i + 1, 0)
                emit_sim(i + 1, 1)
                emit_exp(i + 1, 0)

            P.tensor.wait_ge(sSC.h, ot_ready)
            if i + 1 < NT:
                P.tensor.wait_ge(sDV.h, qkvevac[i + 1])
            if st["wo"]:
                P.tensor.wait_ge(sDV.h, st["wo"])
            for c in range(4):
                for half in range(2):
                    op = P.tensor.matmul(PBa[half],
                                         OTBv[:, c, :],
                                         WOv[:, c, half * 512:(half + 1) * 512],
                                         start=(c == 0), stop=(c == 3))
            op.then_inc(sPE.h, 1); sPE.inc()
            P.vector.wait_ge(sPE.h, sPE.n)
            P.vector.tensor_tensor(XYv[:, i, 0:512], XYv[:, i, 0:512], PBa[0],
                                   op=ALU.add)
            P.vector.tensor_tensor(XYv[:, i, 512:1024], XYv[:, i, 512:1024], PBa[1],
                                   op=ALU.add).then_inc(sDV.h, 1); sDV.inc()
            st["wo"] = sDV.n

            if dbg_o is not None and l == 0:
                P.gpsimd.wait_ge(sSC.h, ot_ready)
                P.gpsimd.dma_start(dbg_o.rearrange("(i p) n -> p i n", p=128)[:, i, :],
                                   OTB).then_inc(sDG.h, 16); sDG.inc(16)
                P.tensor.wait_ge(sDG.h, sDG.n)
        P.run()

        if dbg_yn is not None and l == 0:
            P.sync.wait_ge(sDV.h, sDV.n)
            P.sync.dma_start(dbg_yn.rearrange("(i p) n -> p i n", p=128), XYv) \
                .then_inc(sDS.h, 16); sDS.inc(16)
            P.vector.wait_ge(sDS.h, sDS.n)
            P.run()

        if stop_after == f"L{l}C":
            _dev_finish(P, sDX, dout, XYv)
            return nc, dbg_out

        for _b in (WQKV, WO, KT, VB, QT, yT, QRB, KRB, ONB,
                   PTS[0], PTS[1], OTB):
            A16.freeap(_b)
        for _b in (m0t, m4t, QF, KF, SQ16, RH, M2, RB8):
            A16.freeap(_b)
        for _k in tabs:
            A32.freeap(tabs[_k])
        A32.freeap(qsb)
        A32.freeap(ksb)

        P.mark(f"L{l}_attnABC")
        yTN = A16.alloc(8 * 2064, top=True)
        yTNv = yTN.rearrange("p (c n) -> p c n", c=8)[:, :, 0:2049]
        murow = A16.alloc(T)
        rsrow = A16.alloc(T)
        frow_t = A32.alloc(T)
        H2a = A16.alloc(4 * T)
        H2b = A16.alloc(4 * T)
        _h2av = H2a.rearrange("p (c n) -> p c n", c=4)
        _h2bv = H2b.rearrange("p (c n) -> p c n", c=4)

        class _H2V:
            def __getitem__(self, idx):
                _, c, sl = idx
                return (_h2av if c < 4 else _h2bv)[:, c % 4, sl]
        H2v = _H2V()
        SQB = A16.alloc(512)
        P.gpsimd.memset(yTNv[:, :, 0:1], 0.0).then_inc(sPL.h, 1); sPL.inc()

        for i in range(NT):
            P.vector.wait_ge(sPE.h, sPE.n)
            P.vector.tensor_copy(YC, XYv[:, i, :]).then_inc(sDV.h, 1); sDV.inc()
            P.tensor.wait_ge(sDV.h, sDV.n)
            P.tensor.wait_ge(sSC.h, sSC.n)
            for c in range(8):
                op = P.tensor.transpose(PT16[:, c * 128:(c + 1) * 128],
                                        YC[:, c * 128:(c + 1) * 128], ident)
            op.then_inc(sPE.h, 1); sPE.inc()
            P.scalar.wait_ge(sPE.h, sPE.n)
            for c in range(8):
                op = P.scalar.copy(yTNv[:, c, 1 + i * 128:1 + (i + 1) * 128],
                                   PT16[:, c * 128:(c + 1) * 128])
            op.then_inc(sSC.h, 1); sSC.inc()
        ytn_done = sSC.n
        P.run()

        def h2src(c, tsl):
            if c < 4:
                return yTNv[:, c, 1 + tsl.start:1 + tsl.stop]
            return yTNv[:, c, tsl.start:tsl.stop]

        for tt in range(4):
            tsl = slice(tt * 512, tt * 512 + 512)
            P.tensor.wait_ge(sSC.h, ytn_done)
            P.tensor.wait_ge(sPL.h, sPL.n)
            if tt > 0:
                P.tensor.wait_ge(sDV.h, sDV.n)
            for c in range(8):
                op = P.tensor.matmul(PB[0][0:1, 0:512], onescol[:, 0:1],
                                     h2src(c, tsl), start=(c == 0), stop=(c == 7))
            op.then_inc(sPE.h, 1); sPE.inc()
            for c in range(8):
                P.scalar.wait_ge(sPE.h, sPE.n)
                op = P.scalar.activation(SQB, h2src(c, tsl), AF.Square)
                op.then_inc(sSC.h, 1); sSC.inc()
                P.tensor.wait_ge(sSC.h, sSC.n)
                op = P.tensor.matmul(PB[1][0:1, 0:512], onescol[:, 0:1],
                                     SQB, start=(c == 0), stop=(c == 7))
                op.then_inc(sPE.h, 1); sPE.inc()
            P.vector.wait_ge(sPE.h, sPE.n)
            mrow = row(murow)[:, tsl]
            rrow = row(rsrow)[:, tsl]
            frow = row(frow_t)[:, tsl]
            P.vector.tensor_scalar_mul(frow, PB[0][0:1, 0:512], 1.0 / D)
            P.vector.tensor_copy(mrow, frow)
            P.vector.scalar_tensor_tensor(frow, frow, -1.0, frow,
                                          op0=ALU.mult, op1=ALU.mult)
            P.vector.scalar_tensor_tensor(frow, PB[1][0:1, 0:512], 1.0 / D, frow,
                                          op0=ALU.mult, op1=ALU.add)
            P.vector.tensor_scalar_add(frow, frow, 1e-05).then_inc(sDV.h, 1); sDV.inc()
            P.scalar.wait_ge(sDV.h, sDV.n)
            P.scalar.activation(frow, frow, AF.Sqrt).then_inc(sSC.h, 1); sSC.inc()
            P.vector.wait_ge(sSC.h, sSC.n)
            P.vector.reciprocal(frow, frow)
            P.vector.tensor_copy(rrow, frow).then_inc(sDV.h, 1); sDV.inc()
        P.run()

        MUB = A16.alloc(T)
        RSB = A16.alloc(T)
        for tt in range(4):
            tsl = slice(tt * 512, tt * 512 + 512)
            P.tensor.wait_ge(sDV.h, sDV.n)
            if tt > 0:
                P.tensor.wait_ge(sSC.h, sSC.n)
            P.tensor.matmul(PB[2][0:128, 0:512], onesrow5[0:1, 0:128],
                            row(murow)[:, tsl], start=True, stop=True)
            P.tensor.matmul(PB[3][0:128, 0:512], onesrow5[0:1, 0:128],
                            row(rsrow)[:, tsl], start=True, stop=True) \
                .then_inc(sPE.h, 1); sPE.inc()
            P.scalar.wait_ge(sPE.h, sPE.n)
            P.scalar.copy(MUB[:, tsl], PB[2][0:128, 0:512])
            P.scalar.copy(RSB[:, tsl], PB[3][0:128, 0:512]) \
                .then_inc(sSC.h, 1); sSC.inc()
        bc_done = sSC.n

        for c in range(8):
            src_ = yTNv[:, c, 1:2049] if c < 4 else yTNv[:, c, 0:2048]
            P.vector.wait_ge(sSC.h, bc_done)
            P.vector.tensor_tensor(H2v[:, c, :], src_, MUB, op=ALU.subtract)
            P.vector.tensor_tensor(H2v[:, c, :], H2v[:, c, :], RSB, op=ALU.mult) \
                .then_inc(sDV.h, 1); sDV.inc()
            P.scalar.wait_ge(sDV.h, sDV.n)
            P.scalar.activation(H2v[:, c, :], H2v[:, c, :], AF.Identity,
                                scale=lgcol[l][:, c:c + 1], bias=lbcol[l][:, c:c + 1]) \
                .then_inc(sSC.h, 1); sSC.inc()
        P.run()
        A16.freeap(yTN)
        A16.freeap(MUB)
        A16.freeap(RSB)
        A16.freeap(SQB)
        A16.freeap(murow)
        A16.freeap(rsrow)
        A32.freeap(frow_t)

        if dbg_h2 is not None and l == 0:
            scr = A32.alloc(T)
            for c in range(8):
                P.vector.tensor_copy(scr, H2v[:, c, :]).then_inc(sDV.h, 1); sDV.inc()
                P.sync.wait_ge(sDV.h, sDV.n)
                P.sync.dma_start(dbg_h2.rearrange("p (c n) -> p c n", c=8)[:, c, :],
                                 scr).then_inc(sDS.h, 16); sDS.inc(16)
                P.vector.wait_ge(sDS.h, sDS.n)
            P.run()
            A32.freeap(scr)

        if stop_after == f"L{l}D":
            _dev_finish(P, sDX, dout, XYv)
            return nc, dbg_out

        P.mark(f"L{l}_D")
        A16.freeap(YC)
        PP = A16.alloc(22 * 512, top=True)
        W2Ha = A16.alloc(11 * 512)
        W2Hb = A16.alloc(11 * 512)
        W1S = A16.alloc(3 * 2 * 8 * 128)
        W1Sv = W1S.rearrange("p (b a k n) -> p b a k n", b=3, a=2, k=8)
        GEL2 = [A16.alloc(512) for _ in range(2)]
        B1La = A16.alloc(2816)
        B1Lb = A16.alloc(FF2 - 2816 + 12)
        B2L = A16.alloc(D)

        def b1ap(off, n):
            if off < 2816:
                return B1La[0:1, off:off + n]
            return B1Lb[0:1, off - 2816:off - 2816 + n]
        P.gpsimd.dma_start(B1La[0:1, :], db1[l][:, 0:2816]).then_inc(sDG.h, 16)
        sDG.inc(16)
        P.gpsimd.dma_start(B1Lb[0:1, 0:FF2 - 2816], db1[l][:, 2816:FF2]) \
            .then_inc(sDG.h, 16); sDG.inc(16)
        P.gpsimd.dma_start(row(B2L), db2[l]).then_inc(sDG.h, 16); sDG.inc(16)
        P.sync.wait_ge(sDG.h, pc_gate[l])
        P.run()

        w1ch = _w1_chunks()
        w1pairs = [(w1ch[2 * b], w1ch[2 * b + 1]) for b in range(22)]
        w1off = []
        off = 0
        for coff, npart, kind, pch in w1ch:
            w1off.append(off)
            off += 8 * npart
        if l == 1:
            fstat_dv = {}
        w2h_pe = 0
        pair_pe = {}
        pair_rd = {}
        for q in range(4):
            qsl = slice(q * 512, q * 512 + 512)
            P.sync.wait_ge(sPE.h, w2h_pe)
            P.sync.dma_start(
                W2Ha.rearrange("p (c n) -> p c n", c=11),
                dw2b[l][0:11 * 128, 0:512].rearrange("(c p) n -> p c n", p=128)) \
                .then_inc(sDS.h, 16); sDS.inc(16)
            P.sync.dma_start(
                W2Hb[:, 0:10 * 512].rearrange("p (c n) -> p c n", c=10),
                dw2b[l][11 * 128:21 * 128, 0:512]
                .rearrange("(c p) n -> p c n", p=128)) \
                .then_inc(sDS.h, 16); sDS.inc(16)
            P.sync.dma_start(W2Hb[0:42, 10 * 512:11 * 512],
                             dw2b[l][21 * 128:HID, 0:512]) \
                .then_inc(sDS.h, 16); sDS.inc(16)
            w2h_ready = sDS.n

            pa_free = {}
            gel_free = {}
            gelp = [None, None]
            for b, (ach, gch) in enumerate(w1pairs):
                npart = ach[1]
                par = b % 3
                weng = (P.sync, P.sync, P.sync)[par]
                weng.wait_ge(sPE.h, pair_pe.get((q, b - 3), 0 if q == 0 else
                                                pair_pe.get((q - 1, 19 + b), 0)))
                boff = w1off[b * 2]
                weng.dma_start(
                    W1Sv[:, par, :, :, 0:npart] if npart == 128 else
                    W1Sv[:, par, :, :, 0:npart],
                    dw1b[l][:, boff:boff + 16 * npart]
                    .rearrange("p (a k n) -> p a k n", a=2, k=8)) \
                    .then_inc(sW3[par].h, 16)
                sW3[par].inc(16)
                pair_rd[(q, b)] = sW3[par].n
                P.tensor.wait_ge(sW3[par].h, pair_rd[(q, b)])
                P.tensor.wait_ge(sDV.h, pa_free.get(b % 2, 0))
                psl_a = PB[2 + 2 * (b % 2)][0:npart, 0:512]
                psl_g = PB[3 if b % 2 == 0 else 5][0:npart, 0:512]
                for kc in range(8):
                    P.tensor.matmul(psl_a, W1Sv[:, par, 0, kc, 0:npart],
                                    H2v[:, kc, qsl], start=(kc == 0), stop=False)
                P.tensor.matmul(psl_a, b1ap(ach[0], npart),
                                row(onesrow5), start=False, stop=True)
                P.tensor.wait_ge(sSC.h, gel_free.get(b % 2, 0))
                for kc in range(8):
                    P.tensor.matmul(psl_g, W1Sv[:, par, 1, kc, 0:npart],
                                    H2v[:, kc, qsl], start=(kc == 0), stop=False)
                op = P.tensor.matmul(psl_g, b1ap(gch[0], npart),
                                     row(onesrow5), start=False, stop=True)
                op.then_inc(sPE.h, 1); sPE.inc()
                pair_pe[(q, b)] = sPE.n
                P.scalar.wait_ge(sPE.h, sPE.n)
                P.scalar.wait_ge(sDV.h, pa_free.get(b % 2, 0))
                gsl = GEL2[b % 2][0:npart, 0:512]
                P.scalar.activation(gsl, psl_g, AF.Gelu)
                gel_free[b % 2] = sSC.n
                P.vector.wait_ge(sSC.h, sSC.n)
                dst = PP[0:npart, ach[3] * 512:ach[3] * 512 + 512]
                P.vector.tensor_tensor(dst, gsl, psl_a, op=ALU.mult)
                pa_free[b % 2] = sDV.n
            pp_done_dv = sDV.n
            P.run()

            for colh in range(2):
                csl = slice(colh * 512, colh * 512 + 512)
                if colh == 1:
                    P.sync.wait_ge(sPE.h, w2h_pe)
                    P.sync.dma_start(
                        W2Ha.rearrange("p (c n) -> p c n", c=11),
                        dw2b[l][0:11 * 128, 512:1024]
                        .rearrange("(c p) n -> p c n", p=128)) \
                        .then_inc(sDS.h, 16); sDS.inc(16)
                    P.sync.dma_start(
                        W2Hb[:, 0:10 * 512].rearrange("p (c n) -> p c n", c=10),
                        dw2b[l][11 * 128:21 * 128, 512:1024]
                        .rearrange("(c p) n -> p c n", p=128)) \
                        .then_inc(sDS.h, 16); sDS.inc(16)
                    P.sync.dma_start(W2Hb[0:42, 10 * 512:11 * 512],
                                     dw2b[l][21 * 128:HID, 512:1024]) \
                        .then_inc(sDS.h, 16); sDS.inc(16)
                    w2h_ready = sDS.n
                w2ps_free = [0, 0]
                for it in range(4):
                    i = q * 4 + it
                    bank = 4 + (it % 2)
                    P.tensor.wait_ge(sDS.h, w2h_ready)
                    if it == 0:
                        P.tensor.wait_ge(sDV.h, pp_done_dv)
                    P.tensor.wait_ge(sDV.h, w2ps_free[it % 2])
                    for ic in range(22):
                        npart = 128 if ic < 21 else 42
                        lhs = PP[0:npart, ic * 512 + it * 128:ic * 512 + (it + 1) * 128]
                        if ic < 11:
                            rhs = W2Ha[0:npart, ic * 512:(ic + 1) * 512]
                        else:
                            rhs = W2Hb[0:npart, (ic - 11) * 512:(ic - 10) * 512]
                        P.tensor.matmul(PB[bank][0:128, 0:512], lhs, rhs,
                                        start=(ic == 0), stop=False)
                    op = P.tensor.matmul(PB[bank][0:128, 0:512],
                                         onesrow5[0:1, 0:128],
                                         row(B2L)[:, csl], start=False, stop=True)
                    op.then_inc(sPE.h, 1); sPE.inc()
                    w2h_pe = sPE.n
                    P.vector.wait_ge(sPE.h, sPE.n)
                    P.vector.tensor_tensor(XYv[:, i, csl], XYv[:, i, csl],
                                           PB[bank][0:128, 0:512], op=ALU.add)
                    w2ps_free[it % 2] = sDV.n
                    if l == 1 and colh == 1:
                        ln_stats(XYv[:, i, :], fstatv[:, i, 0:1], fstatv[:, i, 1:2])
                        fstat_dv[i] = sDV.n
            P.run()

        A16.freeap(PP)
        A16.freeap(W2Ha)
        A16.freeap(W2Hb)
        A16.freeap(W1S)
        A16.freeap(GEL2[0])
        A16.freeap(GEL2[1])
        A16.freeap(B1La)
        A16.freeap(B1Lb)
        A16.freeap(B2L)
        A16.freeap(H2a)
        A16.freeap(H2b)

        if dbg_y3 is not None and l == 0:
            P.sync.wait_ge(sDV.h, sDV.n)
            P.sync.dma_start(dbg_y3.rearrange("(i p) n -> p i n", p=128), XYv) \
                .then_inc(sDS.h, 16); sDS.inc(16)
            P.vector.wait_ge(sDS.h, sDS.n)
            P.run()

        if stop_after == f"L{l}E":
            _dev_finish(P, sDX, dout, XYv)
            return nc, dbg_out

    P.mark("L_E_last")
    x1r = A32.alloc(2 * D)
    x1rv = x1r.rearrange("p (b n) -> p b n", b=2)
    OT = A32.alloc(2 * D)
    OTv = OT.rearrange("p (b n) -> p b n", b=2)
    for i in range(min(2, NT)):
        P.scalar.dma_start(x1rv[:, i % 2, :],
                           park.rearrange("(i p) n -> p i n", p=128)[:, i, :]) \
            .then_inc(sUP[i % 2].h, 16)
        sUP[i % 2].inc(16)
    for i in range(NT):
        bpar = i % 2
        mu, var, rs, nb = (cp[:, k:k + 1] for k in (40, 41, 42, 43))
        P.scalar.wait_ge(sDV.h, sDV.n)
        P.scalar.activation(rs, fstatv[:, i, 1:2], AF.Sqrt).then_inc(sSC.h, 1)
        sSC.inc()
        P.vector.wait_ge(sSC.h, sSC.n)
        P.vector.reciprocal(rs, rs)
        P.vector.tensor_scalar(nb, fstatv[:, i, 0:1], rs, -1.0, op0=ALU.mult,
                               op1=ALU.mult).then_inc(sDV.h, 1); sDV.inc()
        P.scalar.wait_ge(sDV.h, sDV.n)
        P.scalar.activation(sq_scr, XYv[:, i, :], AF.Identity, scale=rs, bias=nb) \
            .then_inc(sSC.h, 1); sSC.inc()
        P.vector.wait_ge(sSC.h, sSC.n)
        P.vector.wait_ge(sUP[bpar].h, sUP[bpar].n)
        if i >= 2:
            P.vector.wait_ge(sST[bpar].h, sST[bpar].n)
        P.vector.tensor_tensor(OTv[:, bpar, :], x1rv[:, bpar, :], sq_scr,
                               op=ALU.add).then_inc(sDV.h, 1); sDV.inc()
        P.scalar.wait_ge(sDV.h, sDV.n)
        P.scalar.dma_start(dout.rearrange("(i p) n -> p i n", p=128)[:, i, :],
                           OTv[:, bpar, :]).then_inc(sST[bpar].h, 16)
        sST[bpar].inc(16)
        if i + 2 < NT:
            P.scalar.wait_ge(sDV.h, sDV.n)
            P.scalar.dma_start(x1rv[:, (i + 2) % 2, :],
                               park.rearrange("(i p) n -> p i n", p=128)[:, i + 2, :]) \
                .then_inc(sUP[(i + 2) % 2].h, 16)
            sUP[(i + 2) % 2].inc(16)
    P.scalar.wait_ge(sST[0].h, sST[0].n)
    P.scalar.wait_ge(sST[1].h, sST[1].n)
    P.run()

    P.mark("P6")
    return nc, dbg_out


def _dev_finish(P, sDX, dout, XYv):
    P.scalar.dma_start(dout.rearrange("(i p) n -> p i n", p=128), XYv) \
        .then_inc(sDX.h, 16); sDX.inc(16)
    P.scalar.wait_ge(sDX.h, sDX.n)
    P.run()


def _prep_inputs(inputs, core):
    f = np.float32
    perm = _w1_perm()
    d = {}
    d["x"] = np.ascontiguousarray(np.asarray(inputs["L0_attn_x"], f)[core])
    d["WoutT"] = np.ascontiguousarray(np.asarray(inputs["L0_attn_W_out"], f).T)
    ssm = []
    for key in ("L0_attn_W_in", "L0_attn_W_ig", "L0_attn_W_rg"):
        Wt = np.asarray(inputs[key], f).T
        ssm.append(Wt.reshape(8, 128, 8, 128).transpose(1, 2, 0, 3))
    ssm = np.stack(ssm, axis=2)
    d["ssmprep"] = np.ascontiguousarray(ssm.reshape(128, 8 * 3 * 8 * 128))
    d["b_ig"] = np.asarray(inputs["L0_attn_b_ig"], f).reshape(1, D)
    d["b_rg"] = np.asarray(inputs["L0_attn_b_rg"], f).reshape(1, D)
    d["lam8"] = np.ascontiguousarray(np.asarray(inputs["L0_attn_lam"], f).reshape(8, 128))
    d["h08"] = np.ascontiguousarray(np.asarray(inputs["L0_attn_h0"], f).reshape(8, 128))
    d["nw8"] = np.ascontiguousarray(np.asarray(inputs["L0_attn_norm_w"], f).reshape(8, 128))
    wqkv = np.asarray(inputs["L0_ffn_wqkv"], f)
    wo = np.asarray(inputs["L0_ffn_wo"], f)
    w1 = np.asarray(inputs["L0_ffn_w1"], f)
    b1 = np.asarray(inputs["L0_ffn_b1"], f)
    w2 = np.asarray(inputs["L0_ffn_w2"], f)
    b2 = np.asarray(inputs["L0_ffn_b2"], f)
    qs = np.asarray(inputs["L0_ffn_q_scale"], f)
    ks = np.asarray(inputs["L0_ffn_k_scale"], f)
    lg = np.asarray(inputs["L0_ffn_ln_g"], f)
    lb = np.asarray(inputs["L0_ffn_ln_b"], f)
    for l in range(2):
        d[f"wqkv{l}"] = np.ascontiguousarray(wqkv[l])
        d[f"wo{l}"] = np.ascontiguousarray(wo[l])
        wp3 = w1[l][:, perm].reshape(8, 128, FF2)
        segs = []
        off = 0
        for coff, npart, kind, pch in _w1_chunks():
            segs.append(wp3[:, :, coff:coff + npart].transpose(1, 0, 2)
                        .reshape(128, 8 * npart))
        d[f"w1prep{l}"] = np.ascontiguousarray(np.concatenate(segs, axis=1))
        d[f"b1p{l}"] = np.ascontiguousarray(b1[l][perm].reshape(1, FF2))
        d[f"w2_{l}"] = np.ascontiguousarray(w2[l])
        d[f"b2_{l}"] = np.ascontiguousarray(b2[l].reshape(1, D))
        d[f"qs{l}"] = np.ascontiguousarray(qs[l].reshape(1, DH))
        d[f"ks{l}"] = np.ascontiguousarray(ks[l].reshape(1, DH))
        d[f"lg8_{l}"] = np.ascontiguousarray(lg[l].reshape(8, 128))
        d[f"lb8_{l}"] = np.ascontiguousarray(lb[l].reshape(8, 128))
    return d


_CACHE = {}


def kernel(**inputs):
    key = "main"
    if key not in _CACHE:
        _CACHE[key] = build()
    nc, _ = _CACHE[key]
    in_maps = [_prep_inputs(inputs, c) for c in range(8)]
    res = run_bass_kernel_spmd(nc, in_maps, core_ids=list(range(8)))
    out = np.stack([res.results[c]["out"] for c in range(8)], axis=0)
    return out.astype(np.float32)


# version 24
# speedup vs baseline: 1.1569x; 1.1569x over previous
import numpy as np
import ml_dtypes
import concourse.bass as bass
import concourse.mybir as mybir
from concourse.bass_utils import run_bass_kernel_spmd

F32 = mybir.dt.float32
BF16 = mybir.dt.bfloat16
AF = mybir.ActivationFunctionType
ALU = mybir.AluOpType
AX = mybir.AxisListType

T, D, NT = 2048, 1024, 16
HEADS, DH, WIN = 8, 64, 512
HID, FF2 = 2730, 5460
NEG = -1.0e30
BF = ml_dtypes.bfloat16
ENGINES = ("tensor", "vector", "scalar", "gpsimd", "sync")


class RecOp:
    __slots__ = ("meth", "a", "k", "incs")

    def __init__(self, meth, a, k):
        self.meth, self.a, self.k = meth, a, k
        self.incs = []

    def then_inc(self, sem, k=1):
        self.incs.append((sem, k))
        return self

    def replay(self, eng):
        try:
            r = getattr(eng, self.meth)(*self.a, **self.k)
        except Exception as e:
            shapes = [tuple(x.shape) if isinstance(x, bass.AP) else x
                      for x in self.a]
            raise RuntimeError(f"replay {self.meth} {shapes} {self.k}: {e}") from e
        merged = {}
        for sem, k in self.incs:
            merged[sem] = merged.get(sem, 0) + k
        for sem, k in merged.items():
            r = r.then_inc(sem, k)


def _apspan(ap):
    try:
        W = ap.tensor.shape[1]
        off = ap.offset
        if not isinstance(off, int):
            return None
        c0 = off % W
        pairs = list(ap.ap)[1:]
        span = 1
        for st, num in pairs:
            span += abs(st) * (num - 1)
        return (ap.tensor.name, c0, c0 + span)
    except Exception:
        return None


def _overlap(a, b):
    return a and b and a[0] == b[0] and a[1] < b[2] and b[1] < a[2]


class Rec:

    def __init__(self, phase, engname):
        self._phase, self._eng = phase, engname

    def __getattr__(self, meth):
        def call(*a, **k):
            ph = self._phase
            ss = ph.selfsem.get(self._eng)
            if ss is None or meth in ("wait_ge", "sem_inc"):
                op = RecOp(meth, a, k)
                ph.ops.append((self._eng, op))
                return op
            st = ph.dep.setdefault(self._eng, {"writes": [], "waited": 0})
            outs, ins = [], []
            aps = [x for x in a if isinstance(x, bass.AP)]
            if meth in ("dma_start",):
                outs = aps[:1]
                ins = aps[1:]
            elif meth in ("memset", "memzero"):
                outs = aps[:1]
            else:
                outs = aps[:1]
                ins = aps[1:]
            for kk in ("bias", "scale", "initial", "scalar1", "scalar2", "scalar"):
                v = k.get(kk)
                if isinstance(v, bass.AP):
                    ins.append(v)
            ao = k.get("accum_out")
            if isinstance(ao, bass.AP):
                outs.append(ao)
            need = 0
            spans_in = [_apspan(x) for x in ins]
            spans_out = [_apspan(x) for x in outs]
            for w_span, cnt in st["writes"]:
                if cnt <= st["waited"]:
                    continue
                for sp in spans_in + spans_out:
                    if _overlap(sp, w_span):
                        need = max(need, cnt)
            if need > st["waited"]:
                ph.ops.append((self._eng, RecOp("wait_ge", (ss.h, need), {})))
                st["waited"] = need
            op = RecOp(meth, a, k)
            if meth != "dma_start":
                op.incs.append((ss.h, 1))
                ss.inc()
                for sp in spans_out:
                    if sp:
                        st["writes"].append((sp, ss.n))
                if len(st["writes"]) > 24:
                    st["writes"] = st["writes"][-24:]
            ph.ops.append((self._eng, op))
            return op
        return call


class Phase:
    def __init__(self, nc, selfsem=None):
        self.nc = nc
        self.ops = []
        self.selfsem = selfsem or {}
        self.nblocks = 0
        self.marks = []
        self.dep = {}
        for e in ENGINES:
            setattr(self, e, Rec(self, e))

    def mark(self, label):
        snap = self.snap() if hasattr(self, "snap") else {}
        self.marks.append((label, self.nblocks, snap))

    def run(self):
        if not self.ops:
            return
        used = [e for e in ENGINES if any(o[0] == e for o in self.ops)]
        with self.nc.Block() as blk:
            for e in used:
                eops = [op for en, op in self.ops if en == e]
                getattr(blk, e)(lambda eng, eops=eops: [op.replay(eng) for op in eops])
        self.ops = []
        self.nblocks += 1


class S:
    def __init__(self, h):
        self.h, self.n = h, 0

    def inc(self, k=1):
        self.n += k
        return self.n


class FArena:

    def __init__(self, nc, name, dtype, cols):
        self.t = nc.alloc_sbuf_tensor(name, [128, cols], dtype)
        self.free = [(0, cols)]
        self.meta = {}
        self.name = name

    def alloc16(self, n, align=16, top=False):
        assert n % 2 == 0
        ap = self.alloc(n // 2, align, top)
        b = ap.bitcast(BF16)
        self.meta[id(b)] = self.meta.pop(id(ap))
        return b

    def alloc(self, n, align=16, top=False):
        order = list(enumerate(self.free))
        if top:
            order = order[::-1]
        for idx, (o, ln) in order:
            if top:
                oa = (o + ln - n) // align * align
                if oa < o:
                    continue
            else:
                oa = (o + align - 1) // align * align
                if oa + n > o + ln:
                    continue
            pre = (o, oa - o)
            post = (oa + n, o + ln - oa - n)
            repl = [p for p in (pre, post) if p[1] > 0]
            self.free[idx:idx + 1] = repl
            ap = self.t[:, oa:oa + n]
            self.meta[id(ap)] = (oa, n)
            return ap
        raise MemoryError(f"{self.name}: no slot for {n} (free={self.free})")

    def freeap(self, ap):
        o, n = self.meta.pop(id(ap))
        self.free.append((o, n))
        self.free.sort()
        merged = []
        for seg in self.free:
            if merged and merged[-1][0] + merged[-1][1] == seg[0]:
                merged[-1] = (merged[-1][0], merged[-1][1] + seg[1])
            else:
                merged.append(list(seg) and (seg[0], seg[1]))
        self.free = merged

    def used_bytes(self, esz):
        tot = self.t.shape[1] - sum(n for _, n in self.free)
        return tot * esz


def bcast_axis(ap, pos, n):
    pairs = [list(p) for p in list(ap.ap)]
    pairs.insert(pos, [0, n])
    return bass.AP(ap.tensor, ap.offset, pairs)


def _host_constants():
    c = {}
    c["ident"] = np.eye(128, dtype=BF)
    c["id8f"] = np.eye(8, dtype=np.float32)
    c["id1f"] = np.ones((1, 1), np.float32)
    kk = np.arange(128)[:, None]
    qq = np.arange(128)[None, :]
    c["m0t"] = np.where(kk > qq, 0.0, NEG).astype(BF)
    c["m4t"] = np.where(kk <= qq, 0.0, NEG).astype(BF)
    half = np.arange(0, DH, 2, dtype=np.float64)
    inv_freq = 1.0 / 10000.0 ** (half / DH)
    t = np.arange(T, dtype=np.float64)
    freqs = np.concatenate([t[:, None] * inv_freq[None, :]] * 2, axis=-1)
    base = (half + 0.4 * DH) / (1.4 * DH)
    power = (t - T // 2) / float(WIN // 2)
    scale = np.concatenate([base[None, :] ** power[:, None]] * 2, axis=-1)
    cos, sin = np.cos(freqs), np.sin(freqs)

    def pk(a):
        return np.ascontiguousarray(
            a.reshape(16, 128, DH).transpose(1, 0, 2).reshape(128, 16 * DH)
        ).astype(BF)
    sgn = np.concatenate([-np.ones(DH // 2), np.ones(DH // 2)])
    c["qcos"] = pk(cos * scale * 8.0)
    c["qsin"] = pk(sin * scale * 8.0 * sgn)
    c["kcos"] = pk(cos / scale)
    c["ksin"] = pk(sin / scale * sgn)
    e4 = np.zeros((128, 16), np.float32)
    for t in range(4):
        e4[:, t * 4 + t] = 1.0
    c["e4"] = e4.astype(BF)
    sd = np.zeros((128, 128), np.float32)
    sd[np.arange(127), np.arange(1, 128)] = 1.0
    c["shiftmat"] = sd
    ep = np.zeros((128, 128), np.float32)
    ep[127, 0] = 1.0
    c["prevmat"] = ep
    return c


def _w1_perm():
    idx = []
    for b in range(21):
        idx.extend(range(b * 128, b * 128 + 128))
        idx.extend(range(HID + b * 128, HID + b * 128 + 128))
    idx.extend(range(21 * 128, HID))
    idx.extend(range(HID + 21 * 128, FF2))
    return np.array(idx)


def _w1_chunks():
    out = []
    off = 0
    for b in range(21):
        out.append((off, 128, "a", b)); off += 128
        out.append((off, 128, "g", b)); off += 128
    out.append((off, 42, "a", 21)); off += 42
    out.append((off, 42, "g", 21)); off += 42
    return out


def build(dbg=(), stop_after=None):
    nc = bass.Bass("TRN2", target_bir_lowering=False)
    C = _host_constants()

    dx = nc.dram_tensor("x", [T, D], F32, kind="ExternalInput").ap()
    dout = nc.dram_tensor("out", [T, D], F32, kind="ExternalOutput").ap()
    park = nc.dram_tensor("park_x1", [T, D], F32, kind="Internal").ap()

    dWT = {w: nc.dram_tensor(w, [D, D], F32, kind="ExternalInput").ap()
           for w in ("WoutT",)}
    dssm = nc.dram_tensor("ssmprep", [128, 8 * 3 * 8 * 128], F32,
                          kind="ExternalInput").ap()
    dbig = nc.dram_tensor("b_ig", [1, D], F32, kind="ExternalInput").ap()
    dbrg = nc.dram_tensor("b_rg", [1, D], F32, kind="ExternalInput").ap()
    drows8 = {w: nc.dram_tensor(w, [8, 128], F32, kind="ExternalInput").ap()
              for w in ("lam8", "h08", "nw8", "lg8_0", "lb8_0", "lg8_1", "lb8_1")}
    dwqkv = [nc.dram_tensor(f"wqkv{l}", [D, 1536], F32, kind="ExternalInput").ap()
             for l in range(2)]
    dwo = [nc.dram_tensor(f"wo{l}", [512, D], F32, kind="ExternalInput").ap()
           for l in range(2)]
    dw1 = [nc.dram_tensor(f"w1prep{l}", [128, 8 * FF2], F32,
                          kind="ExternalInput").ap()
           for l in range(2)]
    db1 = [nc.dram_tensor(f"b1p{l}", [1, FF2], F32, kind="ExternalInput").ap()
           for l in range(2)]
    dw1b = [nc.dram_tensor(f"w1b{l}", [128, 8 * FF2], BF16, kind="Internal").ap()
            for l in range(2)]
    dw2b = [nc.dram_tensor(f"w2b{l}", [HID, D], BF16, kind="Internal").ap()
            for l in range(2)]
    dw2 = [nc.dram_tensor(f"w2_{l}", [HID, D], F32, kind="ExternalInput").ap()
           for l in range(2)]
    db2 = [nc.dram_tensor(f"b2_{l}", [1, D], F32, kind="ExternalInput").ap()
           for l in range(2)]
    dqs = [nc.dram_tensor(f"qs{l}", [1, DH], F32, kind="ExternalInput").ap()
           for l in range(2)]
    dks = [nc.dram_tensor(f"ks{l}", [1, DH], F32, kind="ExternalInput").ap()
           for l in range(2)]

    dconst = {k: nc.inline_tensor(C[k], "c_" + k).ap() for k in C}

    dbg_out = {}

    def dbgt(name, shape, dtype=F32):
        if name in dbg:
            dbg_out[name] = nc.dram_tensor("dbg_" + name, list(shape), dtype,
                                           kind="ExternalOutput").ap()
            return dbg_out[name]
        return None

    AR = FArena(nc, "ar", F32, 47360)

    class _A32:
        alloc = staticmethod(AR.alloc)
        freeap = staticmethod(AR.freeap)

    class _A16:
        alloc = staticmethod(AR.alloc16)
        freeap = staticmethod(AR.freeap)

    A32, A16 = _A32, _A16

    XY = A32.alloc(NT * D)
    XYv = XY.rearrange("p (i n) -> p i n", n=D)
    rstd_cols = A32.alloc(16)
    mean_cols = A32.alloc(16)
    cp = A32.alloc(64)
    hcol = A32.alloc(8)
    h0col = A32.alloc(8)
    nwcol = A32.alloc(8)
    lgcol = [A32.alloc(8) for _ in range(2)]
    lbcol = [A32.alloc(8) for _ in range(2)]
    fstat = A32.alloc(NT * 2)
    fstatv = fstat.rearrange("p (i k) -> p i k", k=2)
    id8f = A32.alloc(8)
    id1f = A32.alloc(1)
    ones128f = A32.alloc(128)
    qsrowb = [A32.alloc(64) for _ in range(2)]
    ksrowb = [A32.alloc(64) for _ in range(2)]
    ssqrow = A32.alloc(512)
    stage8 = A32.alloc(128)

    ident = A16.alloc(128)
    onescol = A16.alloc(16)
    e4b = A16.alloc(16)
    onesrow5 = A16.alloc(512)
    bigrow = A16.alloc(D)
    brgrow = A16.alloc(D)

    def row(ap, n=None, r0=0, rn=1):
        return ap[r0:r0 + rn, 0:(n if n else ap.shape[1])]

    PB01 = [nc.alloc_psum_tensor(f"pb{i}", [128, 512], F32) for i in range(2)]
    SIMP = [nc.alloc_psum_tensor(f"simp{i}", [128, 1024], F32) for i in range(2)]
    SIMa = [s[:] for s in SIMP]
    PBa = [PB01[0][:], PB01[1][:], SIMa[0][:, 0:512], SIMa[0][:, 512:1024],
           SIMa[1][:, 0:512], SIMa[1][:, 512:1024]]
    PB = PBa
    PT16 = nc.alloc_psum_tensor("pt16", [128, 1024], BF16)[:]
    PR = nc.alloc_psum_tensor("pr", [128, 512], F32)[:]
    PT16b = PR.bitcast(BF16)

    sem_h = {n: nc.semaphore(n).__enter__()
             for n in ("pe", "sc", "dv", "pl", "dmg", "dms", "dmx",
                       "xpa", "xpb", "upa", "upb", "sta", "stb", "w2a", "w2b",
                       "pia", "pib", "poa", "pob", "w2c")}
    sPE, sSC, sDV, sPL = (S(sem_h[n]) for n in ("pe", "sc", "dv", "pl"))
    sDG, sDS, sDX = (S(sem_h[n]) for n in ("dmg", "dms", "dmx"))
    sXP = [S(sem_h["xpa"]), S(sem_h["xpb"])]
    sW2 = [S(sem_h["w2a"]), S(sem_h["w2b"])]
    sPI = [S(sem_h["pia"]), S(sem_h["pib"])]
    sPO = [S(sem_h["poa"]), S(sem_h["pob"])]
    sW3 = [S(sem_h["w2a"]), S(sem_h["w2b"]), S(sem_h["w2c"])]
    sUP = [S(sem_h["upa"]), S(sem_h["upb"])]
    sST = [S(sem_h["sta"]), S(sem_h["stb"])]

    P = Phase(nc, selfsem={"vector": sDV, "scalar": sSC, "gpsimd": sPL})
    P.snap = lambda: {"pe": sPE.n, "dv": sDV.n, "sc": sSC.n}

    cregs = A32.alloc(16)
    _cvals = [1e-08, 4.0, 0.5, -0.5, -0.25, 0.25 * (1.0 + 1e-06), 1.0 / D, 1e-06]
    for _ci, _cv in enumerate(_cvals):
        nc.const_aps.register(F32, float(_cv), cregs[:, _ci:_ci + 1])
        P.gpsimd.memset(cregs[:, _ci:_ci + 1], float(_cv))

    P.sync.dma_start(XYv, dx.rearrange("(i p) n -> p i n", p=128)) \
        .then_inc(sDS.h, 16); sDS.inc(16)
    P.sync.dma_start(ident, dconst["ident"]).then_inc(sDS.h, 16); sDS.inc(16)
    P.sync.dma_start(e4b, dconst["e4"]).then_inc(sDS.h, 16); sDS.inc(16)
    P.sync.dma_start(id8f[0:8, :], dconst["id8f"]).then_inc(sDS.h, 16); sDS.inc(16)
    P.sync.dma_start(id1f[0:1, :], dconst["id1f"]).then_inc(sDS.h, 16); sDS.inc(16)
    P.gpsimd.dma_start(row(bigrow), dbig).then_inc(sDG.h, 16); sDG.inc(16)
    P.gpsimd.dma_start(row(brgrow), dbrg).then_inc(sDG.h, 16); sDG.inc(16)
    for l in range(2):
        P.sync.dma_start(row(qsrowb[l]), dqs[l]).then_inc(sDS.h, 16); sDS.inc(16)
        P.sync.dma_start(row(ksrowb[l]), dks[l]).then_inc(sDS.h, 16); sDS.inc(16)
    P.gpsimd.memset(row(onesrow5), 1.0)
    P.gpsimd.memset(onesrow5[32:33, :], 1.0)
    P.gpsimd.memset(onescol[:, 0:1], 1.0)
    P.gpsimd.memset(row(ones128f), 1.0).then_inc(sPL.h, 1); sPL.inc()
    P.run()

    def row_to_col(dname, dst_col, prep=False):
        P.sync.dma_start(stage8[0:8, :], drows8[dname]).then_inc(sDS.h, 16); sDS.inc(16)
        if prep:
            P.scalar.wait_ge(sDS.h, sDS.n)
            P.scalar.activation(stage8[0:8, :], stage8[0:8, :], AF.Sigmoid)
            P.scalar.activation(stage8[0:8, :], stage8[0:8, :], AF.Ln, bias=1e-08,
                                scale=1.0)
            P.scalar.mul(stage8[0:8, :], stage8[0:8, :], 4.0) \
                .then_inc(sSC.h, 1); sSC.inc()
            P.tensor.wait_ge(sSC.h, sSC.n)
        else:
            P.tensor.wait_ge(sDS.h, sDS.n)
        P.tensor.wait_ge(sDS.h, sDS.n)
        P.tensor.wait_ge(sDV.h, sDV.n)
        P.tensor.transpose(PR[:, 0:8], stage8[0:8, :], id8f[0:8, :]) \
            .then_inc(sPE.h, 1); sPE.inc()
        P.vector.wait_ge(sPE.h, sPE.n)
        P.vector.tensor_copy(dst_col, PR[:, 0:8]).then_inc(sDV.h, 1); sDV.inc()
        P.run()

    row_to_col("lam8", hcol, prep=True)
    row_to_col("h08", h0col)
    row_to_col("nw8", nwcol)
    for l in range(2):
        row_to_col(f"lg8_{l}", lgcol[l])
        row_to_col(f"lb8_{l}", lbcol[l])

    sq_scr = A32.alloc(D)

    def ln_stats(src_ap, mu, var, save_mean_col=None, known_mu=None):
        if known_mu is None:
            P.vector.reduce_sum(mu, src_ap, axis=AX.X)
            P.vector.tensor_scalar_mul(mu, mu, 1.0 / D)
        else:
            P.vector.tensor_copy(mu, known_mu)
        if save_mean_col is not None:
            P.vector.tensor_copy(save_mean_col, mu)
        P.vector.scalar_tensor_tensor(sq_scr, src_ap, 1.0, src_ap,
                                      op0=ALU.mult, op1=ALU.mult, accum_out=var)
        P.vector.tensor_scalar(cp[:, 63:64], mu, mu, -1.0, op0=ALU.mult, op1=ALU.mult)
        P.vector.scalar_tensor_tensor(var, var, 1.0 / D, cp[:, 63:64],
                                      op0=ALU.mult, op1=ALU.add)
        P.vector.tensor_scalar_add(var, var, 1e-05)

    P.mark("P0")
    xnT = A16.alloc(8 * T)
    xnTv = xnT.rearrange("p (c n) -> p c n", c=8)
    lntmp = A16.alloc(D)

    last_evac_sc = sSC.n
    for i in range(NT):
        xt = XYv[:, i, :]
        mu, var, rs, nb = (cp[:, k:k + 1] for k in (0, 1, 2, 3))
        ln_stats(xt, mu, var, save_mean_col=mean_cols[:, i:i + 1])
        P.vector.sem_inc(sDV.h, 1); sDV.inc()
        P.scalar.wait_ge(sDV.h, sDV.n)
        P.scalar.activation(rs, var, AF.Sqrt).then_inc(sSC.h, 1); sSC.inc()
        P.vector.wait_ge(sSC.h, sSC.n)
        P.vector.reciprocal(rs, rs)
        P.vector.wait_ge(sPE.h, sPE.n)
        P.vector.tensor_scalar(lntmp, xt, mu, rs, op0=ALU.subtract, op1=ALU.mult) \
            .then_inc(sDV.h, 1); sDV.inc()
        P.tensor.wait_ge(sDV.h, sDV.n)
        P.tensor.wait_ge(sSC.h, last_evac_sc)
        for c in range(8):
            op = P.tensor.transpose(PT16[:, c * 128:(c + 1) * 128],
                                    lntmp[:, c * 128:(c + 1) * 128], ident)
        op.then_inc(sPE.h, 1); sPE.inc()
        P.scalar.wait_ge(sPE.h, sPE.n)
        for c in range(8):
            op = P.scalar.copy(xnTv[:, c, i * 128:(i + 1) * 128],
                               PT16[:, c * 128:(c + 1) * 128])
        op.then_inc(sSC.h, 1); sSC.inc()
        last_evac_sc = sSC.n
    P.run()

    dbg_lnT = dbgt("ln1T", [128, 8 * T])
    if dbg_lnT is not None:
        scr = A32.alloc(T)
        for c in range(8):
            P.vector.tensor_copy(scr, xnTv[:, c, :]).then_inc(sDV.h, 1); sDV.inc()
            P.sync.wait_ge(sDV.h, sDV.n)
            P.sync.dma_start(dbg_lnT.rearrange("p (c n) -> p c n", c=8)[:, c, :], scr) \
                .then_inc(sDS.h, 16); sDS.inc(16)
            P.vector.wait_ge(sDS.h, sDS.n)
        P.run()
        A32.freeap(scr)

    if stop_after == "P1":
        _dev_finish(P, sDX, dout, XYv)
        return nc, dbg_out

    P.mark("P1_ln1")
    A16.freeap(lntmp)
    hs_bf = A16.alloc(8 * T)
    hs_bfv = hs_bf.rearrange("p (c n) -> p c n", c=8)
    HSQB = A16.alloc(T)
    Wst = A16.alloc(2 * 3 * 8 * 128)
    Wstv = Wst.rearrange("p (b j k n) -> p b j k n", b=2, j=3, k=8)
    Abuf = A32.alloc(T)
    Gbuf = A32.alloc(T)
    ASQ = A32.alloc(T)
    HS = ASQ

    dbg_a = dbgt("A0", [128, T]); dbg_g = dbgt("G0", [128, T])
    dbg_h = dbgt("HS0", [128, T])

    w_free_pe = [0, 0]
    w_ready = [0, 0]
    psum_free = {}
    scan_dv = 0
    pcjobs = []
    rsplit = [0, 684, 1368, 2052, HID]
    for _l in range(2):
        for k in range(8):
            pcjobs.append((dw1b[_l][:, k * FF2:(k + 1) * FF2],
                           dw1[_l][:, k * FF2:(k + 1) * FF2]))
        for k in range(4):
            pcjobs.append((dw2b[_l][rsplit[k]:rsplit[k + 1], :],
                           dw2[_l][rsplit[k]:rsplit[k + 1], :]))
    pc_gate = [0, 0]
    ssq_pe = 0
    for hc in range(8):
        wb = hc % 2
        P.gpsimd.wait_ge(sPE.h, w_free_pe[wb])
        P.gpsimd.dma_start(
            Wstv[:, wb], dssm[:, hc * 3072:(hc + 1) * 3072]
        ).then_inc(sDG.h, 16); sDG.inc(16)
        w_ready[wb] = sDG.n
        for _ in range(3):
            if pcjobs:
                _dst, _src = pcjobs.pop(0)
                P.gpsimd.dma_start(_dst, _src).then_inc(sDG.h, 16); sDG.inc(16)
                _ndone = 24 - len(pcjobs)
                if _ndone == 12:
                    pc_gate[0] = sDG.n
                elif _ndone == 24:
                    pc_gate[1] = sDG.n

        for tt in range(4):
            tsl = slice(tt * 512, tt * 512 + 512)
            pp = tt % 2
            Bj = [PBa[3 * pp + j] for j in range(3)]
            P.tensor.wait_ge(sDG.h, w_ready[wb])
            if psum_free.get(pp):
                P.tensor.wait_ge(sSC.h, psum_free[pp][0])
                P.tensor.wait_ge(sDV.h, psum_free[pp][1])
            for j in range(3):
                for kc in range(8):
                    op = P.tensor.matmul(Bj[j], Wstv[:, wb, j, kc, :],
                                         xnTv[:, kc, tsl],
                                         start=(kc == 0), stop=(j == 0 and kc == 7))
                if j == 1:
                    op = P.tensor.matmul(Bj[j], row(bigrow)[:, hc * 128:(hc + 1) * 128],
                                         row(onesrow5), start=False, stop=True)
                elif j == 2:
                    op = P.tensor.matmul(Bj[j], row(brgrow)[:, hc * 128:(hc + 1) * 128],
                                         row(onesrow5), start=False, stop=True)
            op.then_inc(sPE.h, 1); sPE.inc()
            mm_pe = sPE.n
            if tt == 3:
                w_free_pe[wb] = sPE.n

            hcc = hcol[:, hc:hc + 1]
            P.scalar.wait_ge(sPE.h, mm_pe)
            if hc > 0 and tt == 0:
                P.scalar.wait_ge(sDV.h, scan_dv)
            P.scalar.activation(Gbuf[:, tsl], Bj[1], AF.Tanh, scale=0.5)
            P.scalar.activation(Abuf[:, tsl], Bj[2], AF.Tanh, scale=0.5)
            P.scalar.activation(Abuf[:, tsl], Abuf[:, tsl], AF.Exp,
                                bias=hcc, scale=hcc) \
                .then_inc(sSC.h, 1); sSC.inc()
            P.vector.wait_ge(sSC.h, sSC.n)
            P.vector.scalar_tensor_tensor(Gbuf[:, tsl], Gbuf[:, tsl], 1.0, Bj[0],
                                          op0=ALU.add, op1=ALU.mult)
            P.vector.tensor_tensor(ASQ[:, tsl], Abuf[:, tsl], Abuf[:, tsl],
                                   op=ALU.mult).then_inc(sDV.h, 1); sDV.inc()
            psum_free[pp] = (sSC.n, sDV.n)

        P.scalar.wait_ge(sDV.h, psum_free[0][1])
        P.scalar.wait_ge(sDV.h, psum_free[1][1])
        for tt in range(4):
            tsl = slice(tt * 512, tt * 512 + 512)
            op = P.scalar.activation(ASQ[:, tsl], ASQ[:, tsl], AF.Sqrt,
                                     scale=-0.25, bias=0.25 * (1.0 + 1e-06))
        op.then_inc(sSC.h, 1); sSC.inc()
        P.vector.wait_ge(sSC.h, sSC.n)
        P.vector.tensor_tensor(Gbuf, Gbuf, ASQ, op=ALU.mult)
        P.vector.tensor_tensor_scan(HS, Abuf, Gbuf, h0col[:, hc:hc + 1],
                                    op0=ALU.mult, op1=ALU.add) \
            .then_inc(sDV.h, 1); sDV.inc()
        scan_dv = sDV.n
        P.vector.tensor_tensor(HSQB, HS, HS, op=ALU.mult)
        P.vector.tensor_scalar_mul(hs_bfv[:, hc, :], HS, nwcol[:, hc:hc + 1]) \
            .then_inc(sDV.h, 1); sDV.inc()
        hsq_dv = sDV.n

        P.tensor.wait_ge(sDV.h, hsq_dv)
        for tt in range(4):
            tsl = slice(tt * 512, tt * 512 + 512)
            op = P.tensor.matmul(PR[0:4, 0:512], e4b[:, tt * 4:(tt + 1) * 4],
                                 HSQB[:, tsl], start=(hc == 0), stop=(hc == 7))
        op.then_inc(sPE.h, 1); sPE.inc()
        ssq_pe = sPE.n

        if hc == 0 and (dbg_a is not None or dbg_g is not None or dbg_h is not None):
            P.sync.wait_ge(sDV.h, sDV.n)
            for o, b in ((dbg_a, Abuf), (dbg_g, Gbuf), (dbg_h, HS)):
                if o is not None:
                    P.sync.dma_start(o, b).then_inc(sDS.h, 16); sDS.inc(16)
            P.vector.wait_ge(sDS.h, sDS.n)
        P.run()

    P.mark("P2_ssm")
    P.scalar.wait_ge(sPE.h, ssq_pe)
    P.scalar.activation(ssqrow[0:4, 0:512], PR[0:4, 0:512], AF.Sqrt,
                        scale=1.0 / D, bias=1e-06).then_inc(sSC.h, 1); sSC.inc()
    P.vector.wait_ge(sSC.h, sSC.n)
    P.vector.reciprocal(ssqrow[0:4, 0:512], ssqrow[0:4, 0:512]) \
        .then_inc(sDV.h, 1); sDV.inc()
    P.tensor.wait_ge(sDV.h, sDV.n)
    for q in range(4):
        op = P.tensor.transpose(PR[:, 128 + q * 4:128 + (q + 1) * 4],
                                ssqrow[0:4, q * 128:(q + 1) * 128], id8f[0:4, 0:4])
    op.then_inc(sPE.h, 1); sPE.inc()
    P.vector.wait_ge(sPE.h, sPE.n)
    P.vector.tensor_copy(
        rstd_cols.rearrange("p (r q) -> p r q", r=4),
        PR[:, 128:144].rearrange("p (q r) -> p r q", q=4)) \
        .then_inc(sDV.h, 1); sDV.inc()
    P.run()

    for ap in (Abuf, Gbuf, ASQ, ssqrow):
        A32.freeap(ap)
    A16.freeap(Wst)
    A16.freeap(HSQB)
    A16.freeap(xnT)
    A16.freeap(bigrow)
    A16.freeap(brgrow)

    P.mark("P3")
    WoF = A16.alloc(8 * D)
    WoFv = WoF.rearrange("p (c n) -> p c n", c=8)
    x1b = A32.alloc(2 * D)
    x1bv = x1b.rearrange("p (b n) -> p b n", b=2)
    SSo = A32.alloc(D)
    P.gpsimd.dma_start(WoFv, dWT["WoutT"].rearrange("(c p) n -> p c n", p=128)) \
        .then_inc(sDG.h, 16); sDG.inc(16)
    wout_ready = sDG.n

    dbg_y0 = dbgt("y0", [T, D])
    park_done = {}
    x1b_free_sc = [0, 0]
    evac_free = 0
    for i in range(NT):
        tsl = slice(i * 128, (i + 1) * 128)
        P.tensor.wait_ge(sDG.h, wout_ready)
        if evac_free:
            P.tensor.wait_ge(sSC.h, evac_free)
        for half in range(2):
            for c in range(8):
                op = P.tensor.matmul(PBa[4 + half], hs_bfv[:, c, tsl],
                                     WoFv[:, c, half * 512:(half + 1) * 512],
                                     start=(c == 0), stop=(c == 7))
        op.then_inc(sPE.h, 1); sPE.inc()
        rc = rstd_cols[:, i:i + 1]
        P.scalar.wait_ge(sPE.h, sPE.n)
        P.scalar.activation(SSo[:, 0:512], PBa[4], AF.Copy, scale=rc)
        P.scalar.activation(SSo[:, 512:1024], PBa[5], AF.Copy, scale=rc) \
            .then_inc(sSC.h, 1); sSC.inc()
        evac_free = sSC.n
        mu, var, rs, nb = (cp[:, k:k + 1] for k in (0, 1, 2, 3))
        P.vector.wait_ge(sSC.h, sSC.n)
        ln_stats(SSo, mu, var)
        P.vector.sem_inc(sDV.h, 1); sDV.inc()
        P.scalar.wait_ge(sDV.h, sDV.n)
        P.scalar.activation(rs, var, AF.Sqrt).then_inc(sSC.h, 1); sSC.inc()
        P.vector.wait_ge(sSC.h, sSC.n)
        P.vector.reciprocal(rs, rs)
        P.vector.tensor_scalar(nb, mu, rs, -1.0, op0=ALU.mult, op1=ALU.mult) \
            .then_inc(sDV.h, 1); sDV.inc()
        P.scalar.wait_ge(sDV.h, sDV.n)
        P.scalar.activation(SSo, SSo, AF.Identity, scale=rs, bias=nb) \
            .then_inc(sSC.h, 1); sSC.inc()
        bpar = i % 2
        P.vector.wait_ge(sSC.h, sSC.n)
        if i >= 2:
            P.vector.wait_ge(sXP[bpar].h, sXP[bpar].n)
            P.vector.wait_ge(sSC.h, x1b_free_sc[bpar])
        P.vector.tensor_tensor(x1bv[:, bpar, :], XYv[:, i, :], SSo, op=ALU.add) \
            .then_inc(sDV.h, 1); sDV.inc()
        x1_dv = sDV.n
        P.scalar.wait_ge(sDV.h, x1_dv)
        P.scalar.dma_start(park.rearrange("(i p) n -> p i n", p=128)[:, i, :],
                           x1bv[:, bpar, :]).then_inc(sXP[bpar].h, 16)
        sXP[bpar].inc(16)
        mu2, var2, rs2, nb2 = (cp[:, k:k + 1] for k in (8, 9, 10, 11))
        ln_stats(x1bv[:, bpar, :], mu2, var2, known_mu=mean_cols[:, i:i + 1])
        P.vector.sem_inc(sDV.h, 1); sDV.inc()
        P.scalar.wait_ge(sDV.h, sDV.n)
        P.scalar.activation(rs2, var2, AF.Sqrt).then_inc(sSC.h, 1); sSC.inc()
        P.vector.wait_ge(sSC.h, sSC.n)
        P.vector.reciprocal(rs2, rs2)
        P.vector.tensor_scalar(nb2, mu2, rs2, -1.0, op0=ALU.mult, op1=ALU.mult) \
            .then_inc(sDV.h, 1); sDV.inc()
        P.scalar.wait_ge(sDV.h, sDV.n)
        P.scalar.activation(XYv[:, i, :], x1bv[:, bpar, :], AF.Identity,
                            scale=rs2, bias=nb2).then_inc(sSC.h, 1); sSC.inc()
        x1b_free_sc[bpar] = sSC.n
    P.run()

    if dbg_y0 is not None:
        P.sync.wait_ge(sSC.h, sSC.n)
        P.sync.dma_start(dbg_y0.rearrange("(i p) n -> p i n", p=128), XYv) \
            .then_inc(sDS.h, 16); sDS.inc(16)
        P.vector.wait_ge(sDS.h, sDS.n)
        P.run()

    A16.freeap(hs_bf)
    A16.freeap(WoF)
    A32.freeap(x1b)
    A32.freeap(SSo)

    if stop_after == "P4":
        _dev_finish(P, sDX, dout, XYv)
        return nc, dbg_out

    P.mark("P4_wout")
    dbg_qr = dbgt("qrot0", [T, 512]); dbg_kr = dbgt("krot0", [T, 512])
    dbg_o = dbgt("o0", [T, 512]); dbg_yn = dbgt("ynew0", [T, D])
    dbg_h2 = dbgt("h2ln0", [128, 8 * T]); dbg_p = dbgt("p0", [T, HID])
    dbg_y3 = dbgt("y3_0", [T, D])

    park_waited = False
    for l in range(2):
        tabs = {k: A16.alloc(NT * DH) for k in ("qcos", "qsin", "kcos", "ksin")}
        tabv = {k: v.rearrange("p (i n) -> p i n", n=DH) for k, v in tabs.items()}
        qsb = A16.alloc(DH)
        ksb = A16.alloc(DH)
        WQKV = A16.alloc(8 * 1536)
        WQKVv = WQKV.rearrange("p (c n) -> p c n", c=8)
        WO = A16.alloc(4 * D)
        WOv = WO.rearrange("p (c n) -> p c n", c=4)
        KT = A16.alloc(4 * T)
        KTv = KT.rearrange("p (c n) -> p c n", c=4)
        VB = A16.alloc(NT * 520)
        VB65v = VB.rearrange("p (i h e) -> p i h e", h=8, e=65)
        QT = A16.alloc(4 * 128)
        QTv = QT.rearrange("p (c n) -> p c n", c=4)
        yT = A16.alloc(2 * 8 * 132)
        yTv = yT.rearrange("p (b c w) -> p b c w", b=2, c=8)[:, :, :, 0:129]
        YC = A16.alloc(D)
        QRB = A16.alloc(512)
        KRB = A16.alloc(512)
        PTS = [A16.alloc(640) for _ in range(2)]
        ONB = A16.alloc(512)
        ONBv = ONB.rearrange("p (h e) -> p h e", h=8)
        OTB = A16.alloc(4 * 128)
        OTBv = OTB.rearrange("p (c n) -> p c n", c=4)
        m0t = A16.alloc(128)
        m4t = A16.alloc(128)
        QF = A16.alloc(512)
        KF = A16.alloc(512)
        SQ16 = A16.alloc(512)
        RH = A16.alloc(512)
        M2 = A16.alloc(512)
        RB8 = A16.alloc(16)
        if not park_waited:
            P.sync.wait_ge(sXP[0].h, sXP[0].n)
            P.sync.wait_ge(sXP[1].h, sXP[1].n)
            P.gpsimd.wait_ge(sXP[0].h, sXP[0].n)
            P.gpsimd.wait_ge(sXP[1].h, sXP[1].n)
            park_waited = True
        for k in ("qcos", "qsin", "kcos", "ksin"):
            P.sync.dma_start(tabs[k], dconst[k]).then_inc(sDS.h, 16); sDS.inc(16)
        P.sync.dma_start(m0t, dconst["m0t"]).then_inc(sDS.h, 16); sDS.inc(16)
        P.sync.dma_start(m4t, dconst["m4t"]).then_inc(sDS.h, 16); sDS.inc(16)
        tab_ready = sDS.n
        P.gpsimd.dma_start(WQKVv, dwqkv[l].rearrange("(c p) n -> p c n", p=128)) \
            .then_inc(sDG.h, 16); sDG.inc(16)
        P.gpsimd.dma_start(WOv, dwo[l].rearrange("(c p) n -> p c n", p=128)) \
            .then_inc(sDG.h, 16); sDG.inc(16)
        wqkv_ready = sDG.n
        P.tensor.wait_ge(sDS.h, tab_ready)
        P.tensor.wait_ge(sPL.h, 1)
        P.tensor.wait_ge(sDV.h, sDV.n)
        P.tensor.matmul(PR[:, 0:64], row(ones128f), row(qsrowb[l]),
                        start=True, stop=True)
        P.tensor.matmul(PR[:, 64:128], row(ones128f), row(ksrowb[l]),
                        start=True, stop=True).then_inc(sPE.h, 1); sPE.inc()
        P.vector.wait_ge(sPE.h, sPE.n)
        P.vector.tensor_copy(qsb, PR[:, 0:64])
        P.vector.tensor_copy(ksb, PR[:, 64:128]).then_inc(sDV.h, 1); sDV.inc()
        P.gpsimd.wait_ge(sDS.h, tab_ready)
        P.gpsimd.wait_ge(sDV.h, sDV.n)
        for k, sb in (("qcos", qsb), ("qsin", qsb), ("kcos", ksb), ("ksin", ksb)):
            for i in range(NT):
                op = P.gpsimd.tensor_tensor(tabv[k][:, i, :], tabv[k][:, i, :], sb,
                                            op=ALU.mult)
        op.then_inc(sPL.h, 1); sPL.inc()
        tabs_done = sPL.n
        P.gpsimd.memset(yTv[:, 0, :, 0:1], 0.0)
        P.gpsimd.memset(yTv[:, 1, :, 0:1], 0.0).then_inc(sPL.h, 1); sPL.inc()
        P.gpsimd.memset(VB65v[:, :, :, 64:65], 1.0).then_inc(sPL.h, 1); sPL.inc()
        vones_pl = sPL.n
        P.run()

        st = {"yc": 0, "qf": (0, 0), "qkt": 0, "wo": 0, "otb": 0, "norm": 0}
        ytr = {}
        qkvpe = {}
        qkvevac = {}
        rotp1 = {}
        rotrsq = {}
        rotdone = {}
        qkready = {}
        exp_done_sc = [0, 0]
        pts_rd_pe = [0, 0]

        def emitA_yroll(j):
            pj = j % 2
            P.vector.wait_ge(sPE.h, st["yc"])
            P.vector.tensor_copy(YC, XYv[:, j, :]).then_inc(sDV.h, 1); sDV.inc()
            P.tensor.wait_ge(sDV.h, sDV.n)
            P.tensor.wait_ge(sDV.h, qkready.get(j - 1, 0))
            for c in range(8):
                op = P.tensor.transpose(PT16b[:, c * 128:(c + 1) * 128],
                                        YC[:, c * 128:(c + 1) * 128], ident)
            op.then_inc(sPE.h, 1); sPE.inc()
            st["yc"] = sPE.n
            P.vector.wait_ge(sPE.h, sPE.n)
            P.vector.tensor_copy(yTv[:, pj, :, 1:129],
                                 PT16b.rearrange("p (c n) -> p c n", c=8))
            if j > 0:
                P.vector.tensor_copy(yTv[:, pj, :, 0:1],
                                     yTv[:, 1 - pj, :, 128:129]) \
                    .then_inc(sDV.h, 1); sDV.inc()
            ytr[j] = sDV.n

        def emitA_qkv(j, ca, cb):
            pj = j % 2
            if ca == 0:
                P.tensor.wait_ge(sDG.h, wqkv_ready)
                P.tensor.wait_ge(sDV.h, ytr[j])
                P.tensor.wait_ge(sPL.h, tabs_done + 1)
                if qkvevac.get(j - 1):
                    P.tensor.wait_ge(sDV.h, qkvevac[j - 1])
                if st["wo"]:
                    P.tensor.wait_ge(sDV.h, st["wo"])
            for c in range(ca, cb):
                lhs = yTv[:, pj, c, 1:129] if c < 4 else yTv[:, pj, c, 0:128]
                for n3 in range(3):
                    dst = PR if n3 == 2 else PBa[n3]
                    op = P.tensor.matmul(dst, lhs,
                                         WQKVv[:, c, n3 * 512:(n3 + 1) * 512],
                                         start=(c == 0), stop=(c == 7))
            op.then_inc(sPE.h, 1); sPE.inc()
            if cb == 8:
                qkvpe[j] = sPE.n

        def emitA_evac(j):
            P.vector.wait_ge(sPE.h, qkvpe[j])
            P.vector.tensor_copy(QF, PBa[0])
            P.vector.tensor_copy(KF, PBa[1])
            P.vector.tensor_copy(VB65v[:, j, :, 0:64],
                                 PR.rearrange("p (h e) -> p h e", h=8)) \
                .then_inc(sDV.h, 1); sDV.inc()
            qkvevac[j] = sDV.n

        rv3 = lambda a: a.rearrange("p (h e) -> p h e", h=8)
        rv4 = lambda a: a.rearrange("p (h b e) -> p h b e", h=8, b=2)

        def emitA_rotP1(j):
            ss_q = cp[:, 16:24]
            ss_k = cp[:, 48:56]
            P.vector.wait_ge(sPE.h, st["qkt"])
            P.vector.tensor_tensor(SQ16, QF, QF, op=ALU.mult)
            P.vector.tensor_reduce(ss_q, rv3(SQ16), axis=AX.X, op=ALU.add)
            P.vector.tensor_tensor(SQ16, KF, KF, op=ALU.mult)
            P.vector.tensor_reduce(ss_k, rv3(SQ16), axis=AX.X, op=ALU.add)
            P.vector.tensor_scalar_max(ss_q, ss_q, 1e-24)
            P.vector.tensor_scalar_max(ss_k, ss_k, 1e-24)
            P.vector.tensor_copy(rv4(RH)[:, :, 0, :], rv4(QF)[:, :, 1, :])
            P.vector.tensor_copy(rv4(RH)[:, :, 1, :], rv4(QF)[:, :, 0, :])
            P.vector.tensor_copy(rv4(M2)[:, :, 0, :], rv4(KF)[:, :, 1, :])
            P.vector.tensor_copy(rv4(M2)[:, :, 1, :], rv4(KF)[:, :, 0, :]) \
                .then_inc(sDV.h, 1); sDV.inc()
            rotp1[j] = sDV.n

        def emitA_rotRSQ(j):
            P.scalar.wait_ge(sDV.h, rotp1[j])
            P.scalar.activation(cp[:, 24:32], cp[:, 16:24], AF.Ln)
            P.scalar.activation(cp[:, 24:32], cp[:, 24:32], AF.Exp, scale=-0.5)
            P.scalar.activation(cp[:, 56:64], cp[:, 48:56], AF.Ln)
            P.scalar.activation(cp[:, 56:64], cp[:, 56:64], AF.Exp, scale=-0.5) \
                .then_inc(sSC.h, 1); sSC.inc()
            rotrsq[j] = sSC.n

        def emitA_rotP2(j):
            P.vector.wait_ge(sSC.h, rotrsq[j])
            rotq_dv = 0
            for SRC, RHx, OUTB, rn, ck, sk in (
                    (QF, RH, QRB, cp[:, 24:32], "qcos", "qsin"),
                    (KF, M2, KRB, cp[:, 56:64], "kcos", "ksin")):
                cosb = bcast_axis(tabv[ck][:, j, :], 1, 8)
                sinb = bcast_axis(tabv[sk][:, j, :], 1, 8)
                P.vector.tensor_tensor(rv3(SRC), rv3(SRC), cosb, op=ALU.mult)
                P.vector.tensor_tensor(rv3(RHx), rv3(RHx), sinb, op=ALU.mult)
                P.vector.tensor_tensor(SRC, SRC, RHx, op=ALU.add)
                rb = bcast_axis(rn, 2, 64)
                P.vector.tensor_tensor(rv3(OUTB), rv3(SRC), rb, op=ALU.mult) \
                    .then_inc(sDV.h, 1); sDV.inc()
                if SRC is QF:
                    rotq_dv = sDV.n
            rotdone[j] = (rotq_dv, sDV.n)
            if dbg_qr is not None and l == 0:
                P.gpsimd.wait_ge(sDV.h, rotdone[j][1])
                P.gpsimd.dma_start(dbg_qr.rearrange("(i p) n -> p i n", p=128)[:, j, :],
                                   QRB).then_inc(sDG.h, 16); sDG.inc(16)
                P.gpsimd.dma_start(dbg_kr.rearrange("(i p) n -> p i n", p=128)[:, j, :],
                                   KRB).then_inc(sDG.h, 16); sDG.inc(16)
                P.tensor.wait_ge(sDG.h, sDG.n)

        def emitA_qkT(j, half):
            if half == 0:
                P.tensor.wait_ge(sDV.h, rotdone[j][0])
                P.tensor.wait_ge(sDV.h, qkvevac[j])
                st["qkt"] = sPE.n + 1
                for c in range(4):
                    op = P.tensor.transpose(PT16b[:, c * 128:(c + 1) * 128],
                                            QRB[:, c * 128:(c + 1) * 128], ident)
                op.then_inc(sPE.h, 1); sPE.inc()
                return
            P.tensor.wait_ge(sDV.h, rotdone[j][1])
            for c in range(4):
                op = P.tensor.transpose(PT16b[:, 512 + c * 128:512 + (c + 1) * 128],
                                        KRB[:, c * 128:(c + 1) * 128], ident)
            op.then_inc(sPE.h, 1); sPE.inc()
            st["qkt"] = sPE.n
            P.vector.wait_ge(sPE.h, sPE.n)
            jsl = slice(j * 128, (j + 1) * 128)
            P.vector.tensor_copy(QT, PT16b[:, 0:512])
            P.vector.tensor_copy(
                KTv[:, :, jsl],
                PT16b[:, 512:1024].rearrange("p (c n) -> p c n", c=4)) \
                .then_inc(sDV.h, 1); sDV.inc()
            qkready[j] = sDV.n

        emitA_yroll(0)
        emitA_qkv(0, 0, 8)
        emitA_evac(0)
        emitA_rotP1(0)
        emitA_rotRSQ(0)
        emitA_rotP2(0)
        emitA_qkT(0, 0)
        emitA_qkT(0, 1)
        emitA_yroll(1)

        def tile_geo(j):
            if j < 4:
                return 128 * (j + 1), 0
            return 640, j * 128 - 512

        sim_pe = {}
        exp_sc = {}
        pv_pe = {}
        oev_sc = {}

        def emit_sim(j, h):
            nk, k0 = tile_geo(j)
            nch = nk // 128
            p = h % 2
            ch, pb = h // 2, 64 * (h % 2)
            Sb = SIMa[p]
            P.tensor.wait_ge(sDV.h, qkready[j])
            P.tensor.wait_ge(sSC.h, exp_done_sc[p])
            P.tensor.wait_ge(sDV.h, st["norm"])
            for c in range(nch):
                m = m4t if c == nch - 1 else \
                    (m0t if (c == 0 and nk == 640) else None)
                op = P.tensor.matmul(
                    Sb[:, c * 128:(c + 1) * 128],
                    KTv[:, ch, k0 + c * 128:k0 + (c + 1) * 128][pb:pb + 64, :],
                    QTv[:, ch, :][pb:pb + 64, :],
                    start=True, stop=(m is None))
                if m is not None:
                    op = P.tensor.matmul(Sb[:, c * 128:(c + 1) * 128],
                                         ident, m, start=False, stop=True)
            op.then_inc(sPE.h, 1); sPE.inc()
            sim_pe[(j, h)] = sPE.n

        def emit_exp(j, h):
            nk, k0 = tile_geo(j)
            p = h % 2
            P.scalar.wait_ge(sPE.h, sim_pe[(j, h)])
            P.scalar.wait_ge(sPE.h, pts_rd_pe[p])
            P.scalar.activation(PTS[p][:, 0:nk], SIMa[p][:, 0:nk], AF.Exp)
            exp_sc[(j, h)] = sSC.n
            exp_done_sc[p] = sSC.n

        emit_sim(0, 0)
        emit_sim(0, 1)
        emit_exp(0, 0)
        for i in range(NT):
            nk, k0 = tile_geo(i)
            nch = nk // 128
            tsl = slice(i * 128, (i + 1) * 128)
            if i + 1 < NT:
                emitA_qkv(i + 1, 0, 8)
            for h in range(HEADS):
                p = h % 2
                obank, ocol = SIMa[p], 640 + (h // 2) * 65
                if h + 2 < HEADS:
                    emit_sim(i, h + 2)
                P.tensor.wait_ge(sSC.h, exp_sc[(i, h)])
                if h == 0:
                    P.tensor.wait_ge(sPL.h, vones_pl)
                for c in range(nch):
                    vt = (k0 + c * 128) // 128
                    op = P.tensor.matmul(obank[:, ocol:ocol + 65],
                                         PTS[p][:, c * 128:(c + 1) * 128],
                                         VB65v[:, vt, h, :],
                                         start=(c == 0), stop=(c == nch - 1))
                op.then_inc(sPE.h, 1); sPE.inc()
                pv_pe[(i, h)] = sPE.n
                pts_rd_pe[p] = sPE.n
                if i + 1 < NT:
                    if h == 3:
                        emitA_evac(i + 1)
                        emitA_rotP1(i + 1)
                    elif h == 5:
                        emitA_rotRSQ(i + 1)
                        emitA_rotP2(i + 1)
                    elif h == 6:
                        emitA_qkT(i + 1, 0)
                    elif h == 7:
                        emitA_qkT(i + 1, 1)
                        if i + 2 < NT:
                            emitA_yroll(i + 2)
                if h + 1 < HEADS:
                    emit_exp(i, h + 1)
                P.scalar.wait_ge(sPE.h, pv_pe[(i, h)])
                P.scalar.copy(ONBv[:, h, :], obank[:, ocol:ocol + 64])
                oev_sc[h] = sSC.n
            tails0 = SIMa[0][:, 640:900].rearrange("p (k s) -> p k s", s=65)
            tails1 = SIMa[1][:, 640:900].rearrange("p (k s) -> p k s", s=65)
            rd8 = cp[:, 32:40]
            rd8v = rd8.rearrange("p (k e) -> p k e", e=2)
            P.vector.wait_ge(sPE.h, pv_pe[(i, HEADS - 1)])
            P.vector.reciprocal(rd8v[:, :, 0:1], tails0[:, :, 64:65])
            P.vector.reciprocal(rd8v[:, :, 1:2], tails1[:, :, 64:65])
            P.vector.wait_ge(sSC.h, oev_sc[HEADS - 1])
            P.vector.tensor_tensor(rv3(ONB), rv3(ONB), bcast_axis(rd8, 2, 64),
                                   op=ALU.mult).then_inc(sDV.h, 1); sDV.inc()
            st["norm"] = sDV.n
            P.tensor.wait_ge(sDV.h, st["norm"])
            P.tensor.wait_ge(sSC.h, st["otb"])
            for c in range(4):
                op = P.tensor.transpose(PT16[:, c * 128:(c + 1) * 128],
                                        ONB[:, c * 128:(c + 1) * 128], ident)
            op.then_inc(sPE.h, 1); sPE.inc()
            ot_pe = sPE.n
            P.scalar.wait_ge(sPE.h, ot_pe)
            P.scalar.copy(OTB, PT16[:, 0:512])
            st["otb"] = sSC.n
            ot_ready = sSC.n
            if i + 1 < NT:
                emit_sim(i + 1, 0)
                emit_sim(i + 1, 1)
                emit_exp(i + 1, 0)

            P.tensor.wait_ge(sSC.h, ot_ready)
            if i + 1 < NT:
                P.tensor.wait_ge(sDV.h, qkvevac[i + 1])
            if st["wo"]:
                P.tensor.wait_ge(sDV.h, st["wo"])
            for c in range(4):
                for half in range(2):
                    op = P.tensor.matmul(PBa[half],
                                         OTBv[:, c, :],
                                         WOv[:, c, half * 512:(half + 1) * 512],
                                         start=(c == 0), stop=(c == 3))
            op.then_inc(sPE.h, 1); sPE.inc()
            P.vector.wait_ge(sPE.h, sPE.n)
            P.vector.tensor_tensor(XYv[:, i, 0:512], XYv[:, i, 0:512], PBa[0],
                                   op=ALU.add)
            P.vector.tensor_tensor(XYv[:, i, 512:1024], XYv[:, i, 512:1024], PBa[1],
                                   op=ALU.add).then_inc(sDV.h, 1); sDV.inc()
            st["wo"] = sDV.n

            if dbg_o is not None and l == 0:
                P.gpsimd.wait_ge(sSC.h, ot_ready)
                P.gpsimd.dma_start(dbg_o.rearrange("(i p) n -> p i n", p=128)[:, i, :],
                                   OTB).then_inc(sDG.h, 16); sDG.inc(16)
                P.tensor.wait_ge(sDG.h, sDG.n)
        P.run()

        if dbg_yn is not None and l == 0:
            P.sync.wait_ge(sDV.h, sDV.n)
            P.sync.dma_start(dbg_yn.rearrange("(i p) n -> p i n", p=128), XYv) \
                .then_inc(sDS.h, 16); sDS.inc(16)
            P.vector.wait_ge(sDS.h, sDS.n)
            P.run()

        if stop_after == f"L{l}C":
            _dev_finish(P, sDX, dout, XYv)
            return nc, dbg_out

        for _b in (WQKV, WO, KT, VB, QT, yT, QRB, KRB, ONB,
                   PTS[0], PTS[1], OTB):
            A16.freeap(_b)
        for _b in (m0t, m4t, QF, KF, SQ16, RH, M2, RB8):
            A16.freeap(_b)
        for _k in tabs:
            A32.freeap(tabs[_k])
        A32.freeap(qsb)
        A32.freeap(ksb)

        P.mark(f"L{l}_attnABC")
        yTN = A16.alloc(8 * 2064, top=True)
        yTNv = yTN.rearrange("p (c n) -> p c n", c=8)[:, :, 0:2049]
        murow = A16.alloc(T)
        rsrow = A16.alloc(T)
        frow_t = A32.alloc(T)
        H2a = A16.alloc(4 * T)
        H2b = A16.alloc(4 * T)
        _h2av = H2a.rearrange("p (c n) -> p c n", c=4)
        _h2bv = H2b.rearrange("p (c n) -> p c n", c=4)

        class _H2V:
            def __getitem__(self, idx):
                _, c, sl = idx
                return (_h2av if c < 4 else _h2bv)[:, c % 4, sl]
        H2v = _H2V()
        SQB = A16.alloc(512)
        P.gpsimd.memset(yTNv[:, :, 0:1], 0.0).then_inc(sPL.h, 1); sPL.inc()

        for i in range(NT):
            P.vector.wait_ge(sPE.h, sPE.n)
            P.vector.tensor_copy(YC, XYv[:, i, :]).then_inc(sDV.h, 1); sDV.inc()
            P.tensor.wait_ge(sDV.h, sDV.n)
            P.tensor.wait_ge(sSC.h, sSC.n)
            for c in range(8):
                op = P.tensor.transpose(PT16[:, c * 128:(c + 1) * 128],
                                        YC[:, c * 128:(c + 1) * 128], ident)
            op.then_inc(sPE.h, 1); sPE.inc()
            P.scalar.wait_ge(sPE.h, sPE.n)
            for c in range(8):
                op = P.scalar.copy(yTNv[:, c, 1 + i * 128:1 + (i + 1) * 128],
                                   PT16[:, c * 128:(c + 1) * 128])
            op.then_inc(sSC.h, 1); sSC.inc()
        ytn_done = sSC.n
        P.run()

        def h2src(c, tsl):
            if c < 4:
                return yTNv[:, c, 1 + tsl.start:1 + tsl.stop]
            return yTNv[:, c, tsl.start:tsl.stop]

        for tt in range(4):
            tsl = slice(tt * 512, tt * 512 + 512)
            P.tensor.wait_ge(sSC.h, ytn_done)
            P.tensor.wait_ge(sPL.h, sPL.n)
            if tt > 0:
                P.tensor.wait_ge(sDV.h, sDV.n)
            for c in range(8):
                op = P.tensor.matmul(PB[0][0:1, 0:512], onescol[:, 0:1],
                                     h2src(c, tsl), start=(c == 0), stop=(c == 7))
            op.then_inc(sPE.h, 1); sPE.inc()
            for c in range(8):
                P.scalar.wait_ge(sPE.h, sPE.n)
                op = P.scalar.activation(SQB, h2src(c, tsl), AF.Square)
                op.then_inc(sSC.h, 1); sSC.inc()
                P.tensor.wait_ge(sSC.h, sSC.n)
                op = P.tensor.matmul(PB[1][0:1, 0:512], onescol[:, 0:1],
                                     SQB, start=(c == 0), stop=(c == 7))
                op.then_inc(sPE.h, 1); sPE.inc()
            P.vector.wait_ge(sPE.h, sPE.n)
            mrow = row(murow)[:, tsl]
            rrow = row(rsrow)[:, tsl]
            frow = row(frow_t)[:, tsl]
            P.vector.tensor_scalar_mul(frow, PB[0][0:1, 0:512], 1.0 / D)
            P.vector.tensor_copy(mrow, frow)
            P.vector.scalar_tensor_tensor(frow, frow, -1.0, frow,
                                          op0=ALU.mult, op1=ALU.mult)
            P.vector.scalar_tensor_tensor(frow, PB[1][0:1, 0:512], 1.0 / D, frow,
                                          op0=ALU.mult, op1=ALU.add)
            P.vector.tensor_scalar_add(frow, frow, 1e-05).then_inc(sDV.h, 1); sDV.inc()
            P.scalar.wait_ge(sDV.h, sDV.n)
            P.scalar.activation(frow, frow, AF.Sqrt).then_inc(sSC.h, 1); sSC.inc()
            P.vector.wait_ge(sSC.h, sSC.n)
            P.vector.reciprocal(frow, frow)
            P.vector.tensor_copy(rrow, frow).then_inc(sDV.h, 1); sDV.inc()
        P.run()

        MUB = A16.alloc(T)
        RSB = A16.alloc(T)
        for tt in range(4):
            tsl = slice(tt * 512, tt * 512 + 512)
            P.tensor.wait_ge(sDV.h, sDV.n)
            if tt > 0:
                P.tensor.wait_ge(sSC.h, sSC.n)
            P.tensor.matmul(PB[2][0:128, 0:512], onesrow5[0:1, 0:128],
                            row(murow)[:, tsl], start=True, stop=True)
            P.tensor.matmul(PB[3][0:128, 0:512], onesrow5[0:1, 0:128],
                            row(rsrow)[:, tsl], start=True, stop=True) \
                .then_inc(sPE.h, 1); sPE.inc()
            P.scalar.wait_ge(sPE.h, sPE.n)
            P.scalar.copy(MUB[:, tsl], PB[2][0:128, 0:512])
            P.scalar.copy(RSB[:, tsl], PB[3][0:128, 0:512]) \
                .then_inc(sSC.h, 1); sSC.inc()
        bc_done = sSC.n

        for c in range(8):
            src_ = yTNv[:, c, 1:2049] if c < 4 else yTNv[:, c, 0:2048]
            P.vector.wait_ge(sSC.h, bc_done)
            P.vector.tensor_tensor(H2v[:, c, :], src_, MUB, op=ALU.subtract)
            P.vector.tensor_tensor(H2v[:, c, :], H2v[:, c, :], RSB, op=ALU.mult) \
                .then_inc(sDV.h, 1); sDV.inc()
            P.scalar.wait_ge(sDV.h, sDV.n)
            P.scalar.activation(H2v[:, c, :], H2v[:, c, :], AF.Identity,
                                scale=lgcol[l][:, c:c + 1], bias=lbcol[l][:, c:c + 1]) \
                .then_inc(sSC.h, 1); sSC.inc()
        P.run()
        A16.freeap(yTN)
        A16.freeap(MUB)
        A16.freeap(RSB)
        A16.freeap(SQB)
        A16.freeap(murow)
        A16.freeap(rsrow)
        A32.freeap(frow_t)

        if dbg_h2 is not None and l == 0:
            scr = A32.alloc(T)
            for c in range(8):
                P.vector.tensor_copy(scr, H2v[:, c, :]).then_inc(sDV.h, 1); sDV.inc()
                P.sync.wait_ge(sDV.h, sDV.n)
                P.sync.dma_start(dbg_h2.rearrange("p (c n) -> p c n", c=8)[:, c, :],
                                 scr).then_inc(sDS.h, 16); sDS.inc(16)
                P.vector.wait_ge(sDS.h, sDS.n)
            P.run()
            A32.freeap(scr)

        if stop_after == f"L{l}D":
            _dev_finish(P, sDX, dout, XYv)
            return nc, dbg_out

        P.mark(f"L{l}_D")
        A16.freeap(YC)
        PP = A16.alloc(22 * 512, top=True)
        W1S = A16.alloc(2 * 2 * 8 * 128)
        W2C = [[A16.alloc(512) for _ in range(22)] for _ in range(2)]
        W1Sv = W1S.rearrange("p (b a k n) -> p b a k n", b=2, a=2, k=8)
        B1L = A16.alloc(2816)
        B2L = A16.alloc(512)

        def b1ap(off, n):
            if off < 2816:
                return B1L[0:1, off:off + n]
            return B1L[32:33, off - 2816:off - 2816 + n]
        P.gpsimd.dma_start(B1L[0:1, :], db1[l][:, 0:2816]).then_inc(sDG.h, 16)
        sDG.inc(16)
        P.gpsimd.dma_start(B1L[32:33, 0:FF2 - 2816], db1[l][:, 2816:FF2]) \
            .then_inc(sDG.h, 16); sDG.inc(16)
        P.gpsimd.dma_start(B2L[0:1, :], db2[l][:, 0:512]).then_inc(sDG.h, 16)
        sDG.inc(16)
        P.gpsimd.dma_start(B2L[32:33, :], db2[l][:, 512:1024]) \
            .then_inc(sDG.h, 16); sDG.inc(16)
        P.sync.wait_ge(sDG.h, pc_gate[l])
        P.run()

        w1ch = _w1_chunks()
        w1pairs = [(w1ch[2 * b], w1ch[2 * b + 1]) for b in range(22)]
        w1off = []
        off = 0
        for coff, npart, kind, pch in w1ch:
            w1off.append(off)
            off += 8 * npart
        if l == 1:
            fstat_dv = {}
        w2h_pe = [0, 0]
        w2h_ready = [0, 0]
        pair_pe = {}
        pair_rd = {}

        def load_w2h(colh):
            csl = slice(colh * 512, colh * 512 + 512)
            P.sync.wait_ge(sPE.h, w2h_pe[colh])
            for ic in range(22):
                npart = 128 if ic < 21 else 42
                P.sync.dma_start(W2C[colh][ic][0:npart, :],
                                 dw2b[l][ic * 128:ic * 128 + npart, csl]) \
                    .then_inc(sDS.h, 16); sDS.inc(16)
            w2h_ready[colh] = sDS.n

        for q in range(4):
            qsl = slice(q * 512, q * 512 + 512)
            load_w2h(0)

            pa_free = {}
            gel_free = {}
            gelp = [None, None]
            for b, (ach, gch) in enumerate(w1pairs):
                npart = ach[1]
                par = b % 2
                weng = P.sync
                weng.wait_ge(sPE.h, pair_pe.get((q, b - 2), 0 if q == 0 else
                                                pair_pe.get((q - 1, 20 + b), 0)))
                boff = w1off[b * 2]
                weng.dma_start(
                    W1Sv[:, par, :, :, 0:npart] if npart == 128 else
                    W1Sv[:, par, :, :, 0:npart],
                    dw1b[l][:, boff:boff + 16 * npart]
                    .rearrange("p (a k n) -> p a k n", a=2, k=8)) \
                    .then_inc(sW3[par].h, 16)
                sW3[par].inc(16)
                pair_rd[(q, b)] = sW3[par].n
                P.tensor.wait_ge(sW3[par].h, pair_rd[(q, b)])
                P.tensor.wait_ge(sDV.h, pa_free.get(b % 2, 0))
                psl_a = PB[2 + 2 * (b % 2)][0:npart, 0:512]
                psl_g = PB[3 if b % 2 == 0 else 5][0:npart, 0:512]
                for kc in range(8):
                    P.tensor.matmul(psl_a, W1Sv[:, par, 0, kc, 0:npart],
                                    H2v[:, kc, qsl], start=(kc == 0), stop=False)
                P.tensor.matmul(psl_a, b1ap(ach[0], npart),
                                onesrow5[0:1, :] if ach[0] < 2816 else
                                onesrow5[32:33, :], start=False, stop=True)
                P.tensor.wait_ge(sSC.h, gel_free.get(b % 2, 0))
                gel_free[b % 2] = sSC.n + 1
                for kc in range(8):
                    P.tensor.matmul(psl_g, W1Sv[:, par, 1, kc, 0:npart],
                                    H2v[:, kc, qsl], start=(kc == 0), stop=False)
                op = P.tensor.matmul(psl_g, b1ap(gch[0], npart),
                                     onesrow5[0:1, :] if gch[0] < 2816 else
                                     onesrow5[32:33, :], start=False, stop=True)
                op.then_inc(sPE.h, 1); sPE.inc()
                pair_pe[(q, b)] = sPE.n
                P.scalar.wait_ge(sPE.h, sPE.n)
                dst = PP[0:npart, ach[3] * 512:ach[3] * 512 + 512]
                P.scalar.activation(dst, psl_g, AF.Gelu)
                P.vector.wait_ge(sSC.h, sSC.n)
                P.vector.tensor_tensor(dst, dst, psl_a, op=ALU.mult)
                pa_free[b % 2] = sDV.n
            pp_done_dv = sDV.n
            load_w2h(1)
            P.run()

            for colh in range(2):
                csl = slice(colh * 512, colh * 512 + 512)
                w2ps_free = [0, 0]
                for it in range(4):
                    i = q * 4 + it
                    bank = 4 + (it % 2)
                    P.tensor.wait_ge(sDS.h, w2h_ready[colh])
                    if it == 0:
                        P.tensor.wait_ge(sDV.h, pp_done_dv)
                    P.tensor.wait_ge(sDV.h, w2ps_free[it % 2])
                    for ic in range(22):
                        npart = 128 if ic < 21 else 42
                        lhs = PP[0:npart, ic * 512 + it * 128:ic * 512 + (it + 1) * 128]
                        rhs = W2C[colh][ic][0:npart, :]
                        P.tensor.matmul(PB[bank][0:128, 0:512], lhs, rhs,
                                        start=(ic == 0), stop=False)
                    _r = 0 if colh == 0 else 32
                    op = P.tensor.matmul(PB[bank][0:128, 0:512],
                                         onesrow5[_r:_r + 1, 0:128],
                                         B2L[_r:_r + 1, :], start=False, stop=True)
                    op.then_inc(sPE.h, 1); sPE.inc()
                    w2h_pe[colh] = sPE.n
                    P.vector.wait_ge(sPE.h, sPE.n)
                    P.vector.tensor_tensor(XYv[:, i, csl], XYv[:, i, csl],
                                           PB[bank][0:128, 0:512], op=ALU.add)
                    w2ps_free[it % 2] = sDV.n
                    if l == 1 and colh == 1:
                        ln_stats(XYv[:, i, :], fstatv[:, i, 0:1], fstatv[:, i, 1:2])
                        fstat_dv[i] = sDV.n
            P.run()

        A16.freeap(PP)
        for _cl in W2C:
            for _b in _cl:
                A16.freeap(_b)
        A16.freeap(W1S)
        A16.freeap(B1L)
        A16.freeap(B2L)
        A16.freeap(H2a)
        A16.freeap(H2b)

        if dbg_y3 is not None and l == 0:
            P.sync.wait_ge(sDV.h, sDV.n)
            P.sync.dma_start(dbg_y3.rearrange("(i p) n -> p i n", p=128), XYv) \
                .then_inc(sDS.h, 16); sDS.inc(16)
            P.vector.wait_ge(sDS.h, sDS.n)
            P.run()

        if stop_after == f"L{l}E":
            _dev_finish(P, sDX, dout, XYv)
            return nc, dbg_out

    P.mark("L_E_last")
    x1r = A32.alloc(2 * D)
    x1rv = x1r.rearrange("p (b n) -> p b n", b=2)
    OT = A32.alloc(2 * D)
    OTv = OT.rearrange("p (b n) -> p b n", b=2)
    for i in range(min(2, NT)):
        P.scalar.dma_start(x1rv[:, i % 2, :],
                           park.rearrange("(i p) n -> p i n", p=128)[:, i, :]) \
            .then_inc(sUP[i % 2].h, 16)
        sUP[i % 2].inc(16)
    for i in range(NT):
        bpar = i % 2
        mu, var, rs, nb = (cp[:, k:k + 1] for k in (40, 41, 42, 43))
        P.scalar.wait_ge(sDV.h, sDV.n)
        P.scalar.activation(rs, fstatv[:, i, 1:2], AF.Sqrt).then_inc(sSC.h, 1)
        sSC.inc()
        P.vector.wait_ge(sSC.h, sSC.n)
        P.vector.reciprocal(rs, rs)
        P.vector.tensor_scalar(nb, fstatv[:, i, 0:1], rs, -1.0, op0=ALU.mult,
                               op1=ALU.mult).then_inc(sDV.h, 1); sDV.inc()
        P.scalar.wait_ge(sDV.h, sDV.n)
        P.scalar.activation(sq_scr, XYv[:, i, :], AF.Identity, scale=rs, bias=nb) \
            .then_inc(sSC.h, 1); sSC.inc()
        P.vector.wait_ge(sSC.h, sSC.n)
        P.vector.wait_ge(sUP[bpar].h, sUP[bpar].n)
        if i >= 2:
            P.vector.wait_ge(sST[bpar].h, sST[bpar].n)
        P.vector.tensor_tensor(OTv[:, bpar, :], x1rv[:, bpar, :], sq_scr,
                               op=ALU.add).then_inc(sDV.h, 1); sDV.inc()
        P.scalar.wait_ge(sDV.h, sDV.n)
        P.scalar.dma_start(dout.rearrange("(i p) n -> p i n", p=128)[:, i, :],
                           OTv[:, bpar, :]).then_inc(sST[bpar].h, 16)
        sST[bpar].inc(16)
        if i + 2 < NT:
            P.scalar.wait_ge(sDV.h, sDV.n)
            P.scalar.dma_start(x1rv[:, (i + 2) % 2, :],
                               park.rearrange("(i p) n -> p i n", p=128)[:, i + 2, :]) \
                .then_inc(sUP[(i + 2) % 2].h, 16)
            sUP[(i + 2) % 2].inc(16)
    P.scalar.wait_ge(sST[0].h, sST[0].n)
    P.scalar.wait_ge(sST[1].h, sST[1].n)
    P.run()

    P.mark("P6")
    return nc, dbg_out


def _dev_finish(P, sDX, dout, XYv):
    P.scalar.dma_start(dout.rearrange("(i p) n -> p i n", p=128), XYv) \
        .then_inc(sDX.h, 16); sDX.inc(16)
    P.scalar.wait_ge(sDX.h, sDX.n)
    P.run()


def _prep_inputs(inputs, core):
    f = np.float32
    perm = _w1_perm()
    d = {}
    d["x"] = np.ascontiguousarray(np.asarray(inputs["L0_attn_x"], f)[core])
    d["WoutT"] = np.ascontiguousarray(np.asarray(inputs["L0_attn_W_out"], f).T)
    ssm = []
    for key in ("L0_attn_W_in", "L0_attn_W_ig", "L0_attn_W_rg"):
        Wt = np.asarray(inputs[key], f).T
        ssm.append(Wt.reshape(8, 128, 8, 128).transpose(1, 2, 0, 3))
    ssm = np.stack(ssm, axis=2)
    d["ssmprep"] = np.ascontiguousarray(ssm.reshape(128, 8 * 3 * 8 * 128))
    d["b_ig"] = np.asarray(inputs["L0_attn_b_ig"], f).reshape(1, D)
    d["b_rg"] = np.asarray(inputs["L0_attn_b_rg"], f).reshape(1, D)
    d["lam8"] = np.ascontiguousarray(np.asarray(inputs["L0_attn_lam"], f).reshape(8, 128))
    d["h08"] = np.ascontiguousarray(np.asarray(inputs["L0_attn_h0"], f).reshape(8, 128))
    d["nw8"] = np.ascontiguousarray(np.asarray(inputs["L0_attn_norm_w"], f).reshape(8, 128))
    wqkv = np.asarray(inputs["L0_ffn_wqkv"], f)
    wo = np.asarray(inputs["L0_ffn_wo"], f)
    w1 = np.asarray(inputs["L0_ffn_w1"], f)
    b1 = np.asarray(inputs["L0_ffn_b1"], f)
    w2 = np.asarray(inputs["L0_ffn_w2"], f)
    b2 = np.asarray(inputs["L0_ffn_b2"], f)
    qs = np.asarray(inputs["L0_ffn_q_scale"], f)
    ks = np.asarray(inputs["L0_ffn_k_scale"], f)
    lg = np.asarray(inputs["L0_ffn_ln_g"], f)
    lb = np.asarray(inputs["L0_ffn_ln_b"], f)
    for l in range(2):
        d[f"wqkv{l}"] = np.ascontiguousarray(wqkv[l])
        d[f"wo{l}"] = np.ascontiguousarray(wo[l])
        wp3 = w1[l][:, perm].reshape(8, 128, FF2)
        segs = []
        off = 0
        for coff, npart, kind, pch in _w1_chunks():
            segs.append(wp3[:, :, coff:coff + npart].transpose(1, 0, 2)
                        .reshape(128, 8 * npart))
        d[f"w1prep{l}"] = np.ascontiguousarray(np.concatenate(segs, axis=1))
        d[f"b1p{l}"] = np.ascontiguousarray(b1[l][perm].reshape(1, FF2))
        d[f"w2_{l}"] = np.ascontiguousarray(w2[l])
        d[f"b2_{l}"] = np.ascontiguousarray(b2[l].reshape(1, D))
        d[f"qs{l}"] = np.ascontiguousarray(qs[l].reshape(1, DH))
        d[f"ks{l}"] = np.ascontiguousarray(ks[l].reshape(1, DH))
        d[f"lg8_{l}"] = np.ascontiguousarray(lg[l].reshape(8, 128))
        d[f"lb8_{l}"] = np.ascontiguousarray(lb[l].reshape(8, 128))
    return d


_CACHE = {}


def kernel(**inputs):
    key = "main"
    if key not in _CACHE:
        _CACHE[key] = build()
    nc, _ = _CACHE[key]
    in_maps = [_prep_inputs(inputs, c) for c in range(8)]
    res = run_bass_kernel_spmd(nc, in_maps, core_ids=list(range(8)))
    out = np.stack([res.results[c]["out"] for c in range(8)], axis=0)
    return out.astype(np.float32)


# version 25
# speedup vs baseline: 1.1614x; 1.0038x over previous
import numpy as np
import ml_dtypes
import concourse.bass as bass
import concourse.mybir as mybir
from concourse.bass_utils import run_bass_kernel_spmd

F32 = mybir.dt.float32
BF16 = mybir.dt.bfloat16
AF = mybir.ActivationFunctionType
ALU = mybir.AluOpType
AX = mybir.AxisListType

T, D, NT = 2048, 1024, 16
HEADS, DH, WIN = 8, 64, 512
HID, FF2 = 2730, 5460
NEG = -1.0e30
BF = ml_dtypes.bfloat16
ENGINES = ("tensor", "vector", "scalar", "gpsimd", "sync")


class RecOp:
    __slots__ = ("meth", "a", "k", "incs")

    def __init__(self, meth, a, k):
        self.meth, self.a, self.k = meth, a, k
        self.incs = []

    def then_inc(self, sem, k=1):
        self.incs.append((sem, k))
        return self

    def replay(self, eng):
        try:
            r = getattr(eng, self.meth)(*self.a, **self.k)
        except Exception as e:
            shapes = [tuple(x.shape) if isinstance(x, bass.AP) else x
                      for x in self.a]
            raise RuntimeError(f"replay {self.meth} {shapes} {self.k}: {e}") from e
        merged = {}
        for sem, k in self.incs:
            merged[sem] = merged.get(sem, 0) + k
        for sem, k in merged.items():
            r = r.then_inc(sem, k)


def _apspan(ap):
    try:
        W = ap.tensor.shape[1]
        off = ap.offset
        if not isinstance(off, int):
            return None
        c0 = off % W
        pairs = list(ap.ap)[1:]
        span = 1
        for st, num in pairs:
            span += abs(st) * (num - 1)
        return (ap.tensor.name, c0, c0 + span)
    except Exception:
        return None


def _overlap(a, b):
    return a and b and a[0] == b[0] and a[1] < b[2] and b[1] < a[2]


class Rec:

    def __init__(self, phase, engname):
        self._phase, self._eng = phase, engname

    def __getattr__(self, meth):
        def call(*a, **k):
            ph = self._phase
            ss = ph.selfsem.get(self._eng)
            if ss is None or meth in ("wait_ge", "sem_inc"):
                op = RecOp(meth, a, k)
                ph.ops.append((self._eng, op))
                return op
            st = ph.dep.setdefault(self._eng, {"writes": [], "waited": 0})
            outs, ins = [], []
            aps = [x for x in a if isinstance(x, bass.AP)]
            if meth in ("dma_start",):
                outs = aps[:1]
                ins = aps[1:]
            elif meth in ("memset", "memzero"):
                outs = aps[:1]
            else:
                outs = aps[:1]
                ins = aps[1:]
            for kk in ("bias", "scale", "initial", "scalar1", "scalar2", "scalar"):
                v = k.get(kk)
                if isinstance(v, bass.AP):
                    ins.append(v)
            ao = k.get("accum_out")
            if isinstance(ao, bass.AP):
                outs.append(ao)
            need = 0
            spans_in = [_apspan(x) for x in ins]
            spans_out = [_apspan(x) for x in outs]
            for w_span, cnt in st["writes"]:
                if cnt <= st["waited"]:
                    continue
                for sp in spans_in + spans_out:
                    if _overlap(sp, w_span):
                        need = max(need, cnt)
            if need > st["waited"]:
                ph.ops.append((self._eng, RecOp("wait_ge", (ss.h, need), {})))
                st["waited"] = need
            op = RecOp(meth, a, k)
            if meth != "dma_start":
                op.incs.append((ss.h, 1))
                ss.inc()
                for sp in spans_out:
                    if sp:
                        st["writes"].append((sp, ss.n))
                if len(st["writes"]) > 24:
                    st["writes"] = st["writes"][-24:]
            ph.ops.append((self._eng, op))
            return op
        return call


class Phase:
    def __init__(self, nc, selfsem=None):
        self.nc = nc
        self.ops = []
        self.selfsem = selfsem or {}
        self.nblocks = 0
        self.marks = []
        self.dep = {}
        for e in ENGINES:
            setattr(self, e, Rec(self, e))

    def mark(self, label):
        snap = self.snap() if hasattr(self, "snap") else {}
        self.marks.append((label, self.nblocks, snap))

    def run(self):
        if not self.ops:
            return
        used = [e for e in ENGINES if any(o[0] == e for o in self.ops)]
        with self.nc.Block() as blk:
            for e in used:
                eops = [op for en, op in self.ops if en == e]
                getattr(blk, e)(lambda eng, eops=eops: [op.replay(eng) for op in eops])
        self.ops = []
        self.nblocks += 1


class S:
    def __init__(self, h):
        self.h, self.n = h, 0

    def inc(self, k=1):
        self.n += k
        return self.n


class FArena:

    def __init__(self, nc, name, dtype, cols):
        self.t = nc.alloc_sbuf_tensor(name, [128, cols], dtype)
        self.free = [(0, cols)]
        self.meta = {}
        self.name = name

    def alloc16(self, n, align=16, top=False):
        assert n % 2 == 0
        ap = self.alloc(n // 2, align, top)
        b = ap.bitcast(BF16)
        self.meta[id(b)] = self.meta.pop(id(ap))
        return b

    def alloc(self, n, align=16, top=False):
        order = list(enumerate(self.free))
        if top:
            order = order[::-1]
        for idx, (o, ln) in order:
            if top:
                oa = (o + ln - n) // align * align
                if oa < o:
                    continue
            else:
                oa = (o + align - 1) // align * align
                if oa + n > o + ln:
                    continue
            pre = (o, oa - o)
            post = (oa + n, o + ln - oa - n)
            repl = [p for p in (pre, post) if p[1] > 0]
            self.free[idx:idx + 1] = repl
            ap = self.t[:, oa:oa + n]
            self.meta[id(ap)] = (oa, n)
            return ap
        raise MemoryError(f"{self.name}: no slot for {n} (free={self.free})")

    def freeap(self, ap):
        o, n = self.meta.pop(id(ap))
        self.free.append((o, n))
        self.free.sort()
        merged = []
        for seg in self.free:
            if merged and merged[-1][0] + merged[-1][1] == seg[0]:
                merged[-1] = (merged[-1][0], merged[-1][1] + seg[1])
            else:
                merged.append(list(seg) and (seg[0], seg[1]))
        self.free = merged

    def used_bytes(self, esz):
        tot = self.t.shape[1] - sum(n for _, n in self.free)
        return tot * esz


def bcast_axis(ap, pos, n):
    pairs = [list(p) for p in list(ap.ap)]
    pairs.insert(pos, [0, n])
    return bass.AP(ap.tensor, ap.offset, pairs)


def _host_constants():
    c = {}
    c["ident"] = np.eye(128, dtype=BF)
    c["id8f"] = np.eye(8, dtype=np.float32)
    c["id1f"] = np.ones((1, 1), np.float32)
    kk = np.arange(128)[:, None]
    qq = np.arange(128)[None, :]
    c["m0t"] = np.where(kk > qq, 0.0, NEG).astype(BF)
    c["m4t"] = np.where(kk <= qq, 0.0, NEG).astype(BF)
    half = np.arange(0, DH, 2, dtype=np.float64)
    inv_freq = 1.0 / 10000.0 ** (half / DH)
    t = np.arange(T, dtype=np.float64)
    freqs = np.concatenate([t[:, None] * inv_freq[None, :]] * 2, axis=-1)
    base = (half + 0.4 * DH) / (1.4 * DH)
    power = (t - T // 2) / float(WIN // 2)
    scale = np.concatenate([base[None, :] ** power[:, None]] * 2, axis=-1)
    cos, sin = np.cos(freqs), np.sin(freqs)

    def pk(a):
        return np.ascontiguousarray(
            a.reshape(16, 128, DH).transpose(1, 0, 2).reshape(128, 16 * DH)
        ).astype(BF)
    sgn = np.concatenate([-np.ones(DH // 2), np.ones(DH // 2)])
    c["qcos"] = pk(cos * scale * 8.0)
    c["qsin"] = pk(sin * scale * 8.0 * sgn)
    c["kcos"] = pk(cos / scale)
    c["ksin"] = pk(sin / scale * sgn)
    e4 = np.zeros((128, 16), np.float32)
    for t in range(4):
        e4[:, t * 4 + t] = 1.0
    c["e4"] = e4.astype(BF)
    sd = np.zeros((128, 128), np.float32)
    sd[np.arange(127), np.arange(1, 128)] = 1.0
    c["shiftmat"] = sd
    ep = np.zeros((128, 128), np.float32)
    ep[127, 0] = 1.0
    c["prevmat"] = ep
    return c


def _w1_perm():
    idx = []
    for b in range(21):
        idx.extend(range(b * 128, b * 128 + 128))
        idx.extend(range(HID + b * 128, HID + b * 128 + 128))
    idx.extend(range(21 * 128, HID))
    idx.extend(range(HID + 21 * 128, FF2))
    return np.array(idx)


def _w1_chunks():
    out = []
    off = 0
    for b in range(21):
        out.append((off, 128, "a", b)); off += 128
        out.append((off, 128, "g", b)); off += 128
    out.append((off, 42, "a", 21)); off += 42
    out.append((off, 42, "g", 21)); off += 42
    return out


def build(dbg=(), stop_after=None):
    nc = bass.Bass("TRN2", target_bir_lowering=False)
    C = _host_constants()

    dx = nc.dram_tensor("x", [T, D], F32, kind="ExternalInput").ap()
    dout = nc.dram_tensor("out", [T, D], F32, kind="ExternalOutput").ap()
    park = nc.dram_tensor("park_x1", [T, D], F32, kind="Internal").ap()

    dWT = {w: nc.dram_tensor(w, [D, D], F32, kind="ExternalInput").ap()
           for w in ("WoutT",)}
    dssm = nc.dram_tensor("ssmprep", [128, 8 * 3 * 8 * 128], F32,
                          kind="ExternalInput").ap()
    dbig = nc.dram_tensor("b_ig", [1, D], F32, kind="ExternalInput").ap()
    dbrg = nc.dram_tensor("b_rg", [1, D], F32, kind="ExternalInput").ap()
    drows8 = {w: nc.dram_tensor(w, [8, 128], F32, kind="ExternalInput").ap()
              for w in ("lam8", "h08", "nw8", "lg8_0", "lb8_0", "lg8_1", "lb8_1")}
    dwqkv = [nc.dram_tensor(f"wqkv{l}", [D, 1536], F32, kind="ExternalInput").ap()
             for l in range(2)]
    dwo = [nc.dram_tensor(f"wo{l}", [512, D], F32, kind="ExternalInput").ap()
           for l in range(2)]
    dw1 = [nc.dram_tensor(f"w1prep{l}", [128, 8 * FF2], F32,
                          kind="ExternalInput").ap()
           for l in range(2)]
    db1 = [nc.dram_tensor(f"b1p{l}", [1, FF2], F32, kind="ExternalInput").ap()
           for l in range(2)]
    dw1b = [nc.dram_tensor(f"w1b{l}", [128, 8 * FF2], BF16, kind="Internal").ap()
            for l in range(2)]
    dw2b = [nc.dram_tensor(f"w2b{l}", [HID, D], BF16, kind="Internal").ap()
            for l in range(2)]
    dw2 = [nc.dram_tensor(f"w2_{l}", [HID, D], F32, kind="ExternalInput").ap()
           for l in range(2)]
    db2 = [nc.dram_tensor(f"b2_{l}", [1, D], F32, kind="ExternalInput").ap()
           for l in range(2)]
    dqs = [nc.dram_tensor(f"qs{l}", [1, DH], F32, kind="ExternalInput").ap()
           for l in range(2)]
    dks = [nc.dram_tensor(f"ks{l}", [1, DH], F32, kind="ExternalInput").ap()
           for l in range(2)]

    dconst = {k: nc.inline_tensor(C[k], "c_" + k).ap() for k in C}

    dbg_out = {}

    def dbgt(name, shape, dtype=F32):
        if name in dbg:
            dbg_out[name] = nc.dram_tensor("dbg_" + name, list(shape), dtype,
                                           kind="ExternalOutput").ap()
            return dbg_out[name]
        return None

    AR = FArena(nc, "ar", F32, 47360)

    class _A32:
        alloc = staticmethod(AR.alloc)
        freeap = staticmethod(AR.freeap)

    class _A16:
        alloc = staticmethod(AR.alloc16)
        freeap = staticmethod(AR.freeap)

    A32, A16 = _A32, _A16

    XY = A32.alloc(NT * D)
    XYv = XY.rearrange("p (i n) -> p i n", n=D)
    rstd_cols = A32.alloc(16)
    mean_cols = A32.alloc(16)
    cp = A32.alloc(64)
    hcol = A32.alloc(8)
    h0col = A32.alloc(8)
    nwcol = A32.alloc(8)
    lgcol = [A32.alloc(8) for _ in range(2)]
    lbcol = [A32.alloc(8) for _ in range(2)]
    fstat = A32.alloc(NT * 2)
    fstatv = fstat.rearrange("p (i k) -> p i k", k=2)
    id8f = A32.alloc(8)
    id1f = A32.alloc(1)
    ones128f = A32.alloc(128)
    qsrowb = [A32.alloc(64) for _ in range(2)]
    ksrowb = [A32.alloc(64) for _ in range(2)]
    ssqrow = A32.alloc(512)
    stage8 = A32.alloc(128)

    ident = A16.alloc(128)
    onescol = A16.alloc(16)
    e4b = A16.alloc(16)
    onesrow5 = A16.alloc(512)
    bigrow = A16.alloc(D)
    brgrow = A16.alloc(D)

    def row(ap, n=None, r0=0, rn=1):
        return ap[r0:r0 + rn, 0:(n if n else ap.shape[1])]

    PB01 = [nc.alloc_psum_tensor(f"pb{i}", [128, 512], F32) for i in range(2)]
    SIMP = [nc.alloc_psum_tensor(f"simp{i}", [128, 1024], F32) for i in range(2)]
    SIMa = [s[:] for s in SIMP]
    PBa = [PB01[0][:], PB01[1][:], SIMa[0][:, 0:512], SIMa[0][:, 512:1024],
           SIMa[1][:, 0:512], SIMa[1][:, 512:1024]]
    PB = PBa
    PT16 = nc.alloc_psum_tensor("pt16", [128, 1024], BF16)[:]
    PR = nc.alloc_psum_tensor("pr", [128, 512], F32)[:]
    PT16b = PR.bitcast(BF16)

    sem_h = {n: nc.semaphore(n).__enter__()
             for n in ("pe", "sc", "dv", "pl", "dmg", "dms", "dmx",
                       "xpa", "xpb", "upa", "upb", "sta", "stb", "w2a", "w2b",
                       "pia", "pib", "poa", "pob", "w2c")}
    sPE, sSC, sDV, sPL = (S(sem_h[n]) for n in ("pe", "sc", "dv", "pl"))
    sDG, sDS, sDX = (S(sem_h[n]) for n in ("dmg", "dms", "dmx"))
    sXP = [S(sem_h["xpa"]), S(sem_h["xpb"])]
    sW2 = [S(sem_h["w2a"]), S(sem_h["w2b"])]
    sPI = [S(sem_h["pia"]), S(sem_h["pib"])]
    sPO = [S(sem_h["poa"]), S(sem_h["pob"])]
    sW3 = [S(sem_h["w2a"]), S(sem_h["w2b"]), S(sem_h["w2c"])]
    sUP = [S(sem_h["upa"]), S(sem_h["upb"])]
    sST = [S(sem_h["sta"]), S(sem_h["stb"])]

    P = Phase(nc, selfsem={"vector": sDV, "scalar": sSC, "gpsimd": sPL})
    P.snap = lambda: {"pe": sPE.n, "dv": sDV.n, "sc": sSC.n}

    cregs = A32.alloc(16)
    _cvals = [1e-08, 4.0, 0.5, -0.5, -0.25, 0.25 * (1.0 + 1e-06), 1.0 / D, 1e-06]
    for _ci, _cv in enumerate(_cvals):
        nc.const_aps.register(F32, float(_cv), cregs[:, _ci:_ci + 1])
        P.gpsimd.memset(cregs[:, _ci:_ci + 1], float(_cv))

    P.sync.dma_start(XYv, dx.rearrange("(i p) n -> p i n", p=128)) \
        .then_inc(sDS.h, 16); sDS.inc(16)
    P.sync.dma_start(ident, dconst["ident"]).then_inc(sDS.h, 16); sDS.inc(16)
    P.sync.dma_start(e4b, dconst["e4"]).then_inc(sDS.h, 16); sDS.inc(16)
    P.sync.dma_start(id8f[0:8, :], dconst["id8f"]).then_inc(sDS.h, 16); sDS.inc(16)
    P.sync.dma_start(id1f[0:1, :], dconst["id1f"]).then_inc(sDS.h, 16); sDS.inc(16)
    P.gpsimd.dma_start(row(bigrow), dbig).then_inc(sDG.h, 16); sDG.inc(16)
    P.gpsimd.dma_start(row(brgrow), dbrg).then_inc(sDG.h, 16); sDG.inc(16)
    for l in range(2):
        P.sync.dma_start(row(qsrowb[l]), dqs[l]).then_inc(sDS.h, 16); sDS.inc(16)
        P.sync.dma_start(row(ksrowb[l]), dks[l]).then_inc(sDS.h, 16); sDS.inc(16)
    P.gpsimd.memset(row(onesrow5), 1.0)
    P.gpsimd.memset(onesrow5[32:33, :], 1.0)
    P.gpsimd.memset(onescol[:, 0:1], 1.0)
    P.gpsimd.memset(row(ones128f), 1.0).then_inc(sPL.h, 1); sPL.inc()
    P.run()

    def row_to_col(dname, dst_col, prep=False):
        P.sync.dma_start(stage8[0:8, :], drows8[dname]).then_inc(sDS.h, 16); sDS.inc(16)
        if prep:
            P.scalar.wait_ge(sDS.h, sDS.n)
            P.scalar.activation(stage8[0:8, :], stage8[0:8, :], AF.Sigmoid)
            P.scalar.activation(stage8[0:8, :], stage8[0:8, :], AF.Ln, bias=1e-08,
                                scale=1.0)
            P.scalar.mul(stage8[0:8, :], stage8[0:8, :], 4.0) \
                .then_inc(sSC.h, 1); sSC.inc()
            P.tensor.wait_ge(sSC.h, sSC.n)
        else:
            P.tensor.wait_ge(sDS.h, sDS.n)
        P.tensor.wait_ge(sDS.h, sDS.n)
        P.tensor.wait_ge(sDV.h, sDV.n)
        P.tensor.transpose(PR[:, 0:8], stage8[0:8, :], id8f[0:8, :]) \
            .then_inc(sPE.h, 1); sPE.inc()
        P.vector.wait_ge(sPE.h, sPE.n)
        P.vector.tensor_copy(dst_col, PR[:, 0:8]).then_inc(sDV.h, 1); sDV.inc()
        P.run()

    row_to_col("lam8", hcol, prep=True)
    row_to_col("h08", h0col)
    row_to_col("nw8", nwcol)
    for l in range(2):
        row_to_col(f"lg8_{l}", lgcol[l])
        row_to_col(f"lb8_{l}", lbcol[l])

    sq_scr = A32.alloc(D)

    def ln_stats(src_ap, mu, var, save_mean_col=None, known_mu=None):
        if known_mu is None:
            P.vector.reduce_sum(mu, src_ap, axis=AX.X)
            P.vector.tensor_scalar_mul(mu, mu, 1.0 / D)
        else:
            P.vector.tensor_copy(mu, known_mu)
        if save_mean_col is not None:
            P.vector.tensor_copy(save_mean_col, mu)
        P.vector.scalar_tensor_tensor(sq_scr, src_ap, 1.0, src_ap,
                                      op0=ALU.mult, op1=ALU.mult, accum_out=var)
        P.vector.tensor_scalar(cp[:, 63:64], mu, mu, -1.0, op0=ALU.mult, op1=ALU.mult)
        P.vector.scalar_tensor_tensor(var, var, 1.0 / D, cp[:, 63:64],
                                      op0=ALU.mult, op1=ALU.add)
        P.vector.tensor_scalar_add(var, var, 1e-05)

    P.mark("P0")
    xnT = A16.alloc(8 * T)
    xnTv = xnT.rearrange("p (c n) -> p c n", c=8)
    lntmp = A16.alloc(D)

    last_evac_sc = sSC.n
    for i in range(NT):
        xt = XYv[:, i, :]
        mu, var, rs, nb = (cp[:, k:k + 1] for k in (0, 1, 2, 3))
        ln_stats(xt, mu, var, save_mean_col=mean_cols[:, i:i + 1])
        P.vector.sem_inc(sDV.h, 1); sDV.inc()
        P.scalar.wait_ge(sDV.h, sDV.n)
        P.scalar.activation(rs, var, AF.Sqrt).then_inc(sSC.h, 1); sSC.inc()
        P.vector.wait_ge(sSC.h, sSC.n)
        P.vector.reciprocal(rs, rs)
        P.vector.wait_ge(sPE.h, sPE.n)
        P.vector.tensor_scalar(lntmp, xt, mu, rs, op0=ALU.subtract, op1=ALU.mult) \
            .then_inc(sDV.h, 1); sDV.inc()
        P.tensor.wait_ge(sDV.h, sDV.n)
        P.tensor.wait_ge(sSC.h, last_evac_sc)
        for c in range(8):
            op = P.tensor.transpose(PT16[:, c * 128:(c + 1) * 128],
                                    lntmp[:, c * 128:(c + 1) * 128], ident)
        op.then_inc(sPE.h, 1); sPE.inc()
        P.scalar.wait_ge(sPE.h, sPE.n)
        for c in range(8):
            op = P.scalar.copy(xnTv[:, c, i * 128:(i + 1) * 128],
                               PT16[:, c * 128:(c + 1) * 128])
        op.then_inc(sSC.h, 1); sSC.inc()
        last_evac_sc = sSC.n
    P.run()

    dbg_lnT = dbgt("ln1T", [128, 8 * T])
    if dbg_lnT is not None:
        scr = A32.alloc(T)
        for c in range(8):
            P.vector.tensor_copy(scr, xnTv[:, c, :]).then_inc(sDV.h, 1); sDV.inc()
            P.sync.wait_ge(sDV.h, sDV.n)
            P.sync.dma_start(dbg_lnT.rearrange("p (c n) -> p c n", c=8)[:, c, :], scr) \
                .then_inc(sDS.h, 16); sDS.inc(16)
            P.vector.wait_ge(sDS.h, sDS.n)
        P.run()
        A32.freeap(scr)

    if stop_after == "P1":
        _dev_finish(P, sDX, dout, XYv)
        return nc, dbg_out

    P.mark("P1_ln1")
    A16.freeap(lntmp)
    hs_bf = A16.alloc(8 * T)
    hs_bfv = hs_bf.rearrange("p (c n) -> p c n", c=8)
    HSQB = A16.alloc(T)
    Wst = A16.alloc(2 * 3 * 8 * 128)
    Wstv = Wst.rearrange("p (b j k n) -> p b j k n", b=2, j=3, k=8)
    Abuf = A32.alloc(T)
    Gbuf = A32.alloc(T)
    ASQ = A32.alloc(T)
    HS = ASQ

    dbg_a = dbgt("A0", [128, T]); dbg_g = dbgt("G0", [128, T])
    dbg_h = dbgt("HS0", [128, T])

    w_free_pe = [0, 0]
    w_ready = [0, 0]
    psum_free = {}
    scan_dv = 0
    pcjobs = []
    rsplit = [0, 684, 1368, 2052, HID]
    for _l in range(2):
        for k in range(8):
            pcjobs.append((dw1b[_l][:, k * FF2:(k + 1) * FF2],
                           dw1[_l][:, k * FF2:(k + 1) * FF2]))
        for k in range(4):
            pcjobs.append((dw2b[_l][rsplit[k]:rsplit[k + 1], :],
                           dw2[_l][rsplit[k]:rsplit[k + 1], :]))
    pc_gate = [0, 0]
    ssq_pe = 0
    for hc in range(8):
        wb = hc % 2
        P.gpsimd.wait_ge(sPE.h, w_free_pe[wb])
        P.gpsimd.dma_start(
            Wstv[:, wb], dssm[:, hc * 3072:(hc + 1) * 3072]
        ).then_inc(sDG.h, 16); sDG.inc(16)
        w_ready[wb] = sDG.n
        for _ in range(3):
            if pcjobs:
                _dst, _src = pcjobs.pop(0)
                P.gpsimd.dma_start(_dst, _src).then_inc(sDG.h, 16); sDG.inc(16)
                _ndone = 24 - len(pcjobs)
                if _ndone == 12:
                    pc_gate[0] = sDG.n
                elif _ndone == 24:
                    pc_gate[1] = sDG.n

        for tt in range(4):
            tsl = slice(tt * 512, tt * 512 + 512)
            pp = tt % 2
            Bj = [PBa[3 * pp + j] for j in range(3)]
            P.tensor.wait_ge(sDG.h, w_ready[wb])
            if psum_free.get(pp):
                P.tensor.wait_ge(sSC.h, psum_free[pp][0])
                P.tensor.wait_ge(sDV.h, psum_free[pp][1])
            for j in range(3):
                for kc in range(8):
                    op = P.tensor.matmul(Bj[j], Wstv[:, wb, j, kc, :],
                                         xnTv[:, kc, tsl],
                                         start=(kc == 0), stop=(j == 0 and kc == 7))
                if j == 1:
                    op = P.tensor.matmul(Bj[j], row(bigrow)[:, hc * 128:(hc + 1) * 128],
                                         row(onesrow5), start=False, stop=True)
                elif j == 2:
                    op = P.tensor.matmul(Bj[j], row(brgrow)[:, hc * 128:(hc + 1) * 128],
                                         row(onesrow5), start=False, stop=True)
            op.then_inc(sPE.h, 1); sPE.inc()
            mm_pe = sPE.n
            if tt == 3:
                w_free_pe[wb] = sPE.n

            hcc = hcol[:, hc:hc + 1]
            P.scalar.wait_ge(sPE.h, mm_pe)
            if hc > 0 and tt == 0:
                P.scalar.wait_ge(sDV.h, scan_dv)
            P.scalar.activation(Gbuf[:, tsl], Bj[1], AF.Tanh, scale=0.5)
            P.scalar.activation(Abuf[:, tsl], Bj[2], AF.Tanh, scale=0.5)
            P.scalar.activation(Abuf[:, tsl], Abuf[:, tsl], AF.Exp,
                                bias=hcc, scale=hcc) \
                .then_inc(sSC.h, 1); sSC.inc()
            P.vector.wait_ge(sSC.h, sSC.n)
            P.vector.scalar_tensor_tensor(Gbuf[:, tsl], Gbuf[:, tsl], 1.0, Bj[0],
                                          op0=ALU.add, op1=ALU.mult)
            P.vector.tensor_tensor(ASQ[:, tsl], Abuf[:, tsl], Abuf[:, tsl],
                                   op=ALU.mult).then_inc(sDV.h, 1); sDV.inc()
            psum_free[pp] = (sSC.n, sDV.n)

        P.scalar.wait_ge(sDV.h, psum_free[0][1])
        P.scalar.wait_ge(sDV.h, psum_free[1][1])
        for tt in range(4):
            tsl = slice(tt * 512, tt * 512 + 512)
            op = P.scalar.activation(ASQ[:, tsl], ASQ[:, tsl], AF.Sqrt,
                                     scale=-0.25, bias=0.25 * (1.0 + 1e-06))
        op.then_inc(sSC.h, 1); sSC.inc()
        P.vector.wait_ge(sSC.h, sSC.n)
        P.vector.tensor_tensor(Gbuf, Gbuf, ASQ, op=ALU.mult)
        P.vector.tensor_tensor_scan(HS, Abuf, Gbuf, h0col[:, hc:hc + 1],
                                    op0=ALU.mult, op1=ALU.add) \
            .then_inc(sDV.h, 1); sDV.inc()
        scan_dv = sDV.n
        P.vector.tensor_tensor(HSQB, HS, HS, op=ALU.mult)
        P.vector.tensor_scalar_mul(hs_bfv[:, hc, :], HS, nwcol[:, hc:hc + 1]) \
            .then_inc(sDV.h, 1); sDV.inc()
        hsq_dv = sDV.n

        P.tensor.wait_ge(sDV.h, hsq_dv)
        for tt in range(4):
            tsl = slice(tt * 512, tt * 512 + 512)
            op = P.tensor.matmul(PR[0:4, 0:512], e4b[:, tt * 4:(tt + 1) * 4],
                                 HSQB[:, tsl], start=(hc == 0), stop=(hc == 7))
        op.then_inc(sPE.h, 1); sPE.inc()
        ssq_pe = sPE.n

        if hc == 0 and (dbg_a is not None or dbg_g is not None or dbg_h is not None):
            P.sync.wait_ge(sDV.h, sDV.n)
            for o, b in ((dbg_a, Abuf), (dbg_g, Gbuf), (dbg_h, HS)):
                if o is not None:
                    P.sync.dma_start(o, b).then_inc(sDS.h, 16); sDS.inc(16)
            P.vector.wait_ge(sDS.h, sDS.n)
        P.run()

    P.mark("P2_ssm")
    P.scalar.wait_ge(sPE.h, ssq_pe)
    P.scalar.activation(ssqrow[0:4, 0:512], PR[0:4, 0:512], AF.Sqrt,
                        scale=1.0 / D, bias=1e-06).then_inc(sSC.h, 1); sSC.inc()
    P.vector.wait_ge(sSC.h, sSC.n)
    P.vector.reciprocal(ssqrow[0:4, 0:512], ssqrow[0:4, 0:512]) \
        .then_inc(sDV.h, 1); sDV.inc()
    P.tensor.wait_ge(sDV.h, sDV.n)
    for q in range(4):
        op = P.tensor.transpose(PR[:, 128 + q * 4:128 + (q + 1) * 4],
                                ssqrow[0:4, q * 128:(q + 1) * 128], id8f[0:4, 0:4])
    op.then_inc(sPE.h, 1); sPE.inc()
    P.vector.wait_ge(sPE.h, sPE.n)
    P.vector.tensor_copy(
        rstd_cols.rearrange("p (r q) -> p r q", r=4),
        PR[:, 128:144].rearrange("p (q r) -> p r q", q=4)) \
        .then_inc(sDV.h, 1); sDV.inc()
    P.run()

    for ap in (Abuf, Gbuf, ASQ, ssqrow):
        A32.freeap(ap)
    A16.freeap(Wst)
    A16.freeap(HSQB)
    A16.freeap(xnT)
    A16.freeap(bigrow)
    A16.freeap(brgrow)

    P.mark("P3")
    WoF = A16.alloc(8 * D)
    WoFv = WoF.rearrange("p (c n) -> p c n", c=8)
    x1b = A32.alloc(2 * D)
    x1bv = x1b.rearrange("p (b n) -> p b n", b=2)
    SSo = A32.alloc(D)
    P.gpsimd.dma_start(WoFv, dWT["WoutT"].rearrange("(c p) n -> p c n", p=128)) \
        .then_inc(sDG.h, 16); sDG.inc(16)
    wout_ready = sDG.n

    dbg_y0 = dbgt("y0", [T, D])
    park_done = {}
    x1b_free_sc = [0, 0]
    evac_free = 0
    for i in range(NT):
        tsl = slice(i * 128, (i + 1) * 128)
        P.tensor.wait_ge(sDG.h, wout_ready)
        if evac_free:
            P.tensor.wait_ge(sSC.h, evac_free)
        for half in range(2):
            for c in range(8):
                op = P.tensor.matmul(PBa[4 + half], hs_bfv[:, c, tsl],
                                     WoFv[:, c, half * 512:(half + 1) * 512],
                                     start=(c == 0), stop=(c == 7))
        op.then_inc(sPE.h, 1); sPE.inc()
        rc = rstd_cols[:, i:i + 1]
        P.scalar.wait_ge(sPE.h, sPE.n)
        P.scalar.activation(SSo[:, 0:512], PBa[4], AF.Copy, scale=rc)
        P.scalar.activation(SSo[:, 512:1024], PBa[5], AF.Copy, scale=rc) \
            .then_inc(sSC.h, 1); sSC.inc()
        evac_free = sSC.n
        mu, var, rs, nb = (cp[:, k:k + 1] for k in (0, 1, 2, 3))
        P.vector.wait_ge(sSC.h, sSC.n)
        ln_stats(SSo, mu, var)
        P.vector.sem_inc(sDV.h, 1); sDV.inc()
        P.scalar.wait_ge(sDV.h, sDV.n)
        P.scalar.activation(rs, var, AF.Sqrt).then_inc(sSC.h, 1); sSC.inc()
        P.vector.wait_ge(sSC.h, sSC.n)
        P.vector.reciprocal(rs, rs)
        P.vector.tensor_scalar(nb, mu, rs, -1.0, op0=ALU.mult, op1=ALU.mult) \
            .then_inc(sDV.h, 1); sDV.inc()
        P.scalar.wait_ge(sDV.h, sDV.n)
        P.scalar.activation(SSo, SSo, AF.Identity, scale=rs, bias=nb) \
            .then_inc(sSC.h, 1); sSC.inc()
        bpar = i % 2
        P.vector.wait_ge(sSC.h, sSC.n)
        if i >= 2:
            P.vector.wait_ge(sXP[bpar].h, sXP[bpar].n)
            P.vector.wait_ge(sSC.h, x1b_free_sc[bpar])
        P.vector.tensor_tensor(x1bv[:, bpar, :], XYv[:, i, :], SSo, op=ALU.add) \
            .then_inc(sDV.h, 1); sDV.inc()
        x1_dv = sDV.n
        P.scalar.wait_ge(sDV.h, x1_dv)
        P.scalar.dma_start(park.rearrange("(i p) n -> p i n", p=128)[:, i, :],
                           x1bv[:, bpar, :]).then_inc(sXP[bpar].h, 16)
        sXP[bpar].inc(16)
        mu2, var2, rs2, nb2 = (cp[:, k:k + 1] for k in (8, 9, 10, 11))
        ln_stats(x1bv[:, bpar, :], mu2, var2, known_mu=mean_cols[:, i:i + 1])
        P.vector.sem_inc(sDV.h, 1); sDV.inc()
        P.scalar.wait_ge(sDV.h, sDV.n)
        P.scalar.activation(rs2, var2, AF.Sqrt).then_inc(sSC.h, 1); sSC.inc()
        P.vector.wait_ge(sSC.h, sSC.n)
        P.vector.reciprocal(rs2, rs2)
        P.vector.tensor_scalar(nb2, mu2, rs2, -1.0, op0=ALU.mult, op1=ALU.mult) \
            .then_inc(sDV.h, 1); sDV.inc()
        P.scalar.wait_ge(sDV.h, sDV.n)
        P.scalar.activation(XYv[:, i, :], x1bv[:, bpar, :], AF.Identity,
                            scale=rs2, bias=nb2).then_inc(sSC.h, 1); sSC.inc()
        x1b_free_sc[bpar] = sSC.n
    P.run()

    if dbg_y0 is not None:
        P.sync.wait_ge(sSC.h, sSC.n)
        P.sync.dma_start(dbg_y0.rearrange("(i p) n -> p i n", p=128), XYv) \
            .then_inc(sDS.h, 16); sDS.inc(16)
        P.vector.wait_ge(sDS.h, sDS.n)
        P.run()

    A16.freeap(hs_bf)
    A16.freeap(WoF)
    A32.freeap(x1b)
    A32.freeap(SSo)

    if stop_after == "P4":
        _dev_finish(P, sDX, dout, XYv)
        return nc, dbg_out

    P.mark("P4_wout")
    dbg_qr = dbgt("qrot0", [T, 512]); dbg_kr = dbgt("krot0", [T, 512])
    dbg_o = dbgt("o0", [T, 512]); dbg_yn = dbgt("ynew0", [T, D])
    dbg_h2 = dbgt("h2ln0", [128, 8 * T]); dbg_p = dbgt("p0", [T, HID])
    dbg_y3 = dbgt("y3_0", [T, D])

    park_waited = False
    for l in range(2):
        tabs = {k: A16.alloc(NT * DH) for k in ("qcos", "qsin", "kcos", "ksin")}
        tabv = {k: v.rearrange("p (i n) -> p i n", n=DH) for k, v in tabs.items()}
        qsb = A16.alloc(DH)
        ksb = A16.alloc(DH)
        WQKV = A16.alloc(8 * 1536)
        WQKVv = WQKV.rearrange("p (c n) -> p c n", c=8)
        WO = A16.alloc(4 * D)
        WOv = WO.rearrange("p (c n) -> p c n", c=4)
        KT = A16.alloc(4 * T)
        KTv = KT.rearrange("p (c n) -> p c n", c=4)
        VB = A16.alloc(NT * 520)
        VB65v = VB.rearrange("p (i h e) -> p i h e", h=8, e=65)
        QT = A16.alloc(4 * 128)
        QTv = QT.rearrange("p (c n) -> p c n", c=4)
        yT = A16.alloc(2 * 8 * 132)
        yTv = yT.rearrange("p (b c w) -> p b c w", b=2, c=8)[:, :, :, 0:129]
        YC = A16.alloc(D)
        QRB = A16.alloc(512)
        KRB = A16.alloc(512)
        PTS = [A16.alloc(640) for _ in range(2)]
        ONB = A16.alloc(512)
        ONBv = ONB.rearrange("p (h e) -> p h e", h=8)
        OTB = A16.alloc(4 * 128)
        OTBv = OTB.rearrange("p (c n) -> p c n", c=4)
        m0t = A16.alloc(128)
        m4t = A16.alloc(128)
        QF = A16.alloc(512)
        KF = A16.alloc(512)
        SQ16 = A16.alloc(512)
        RH = A16.alloc(512)
        M2 = A16.alloc(512)
        RB8 = A16.alloc(16)
        if not park_waited:
            P.sync.wait_ge(sXP[0].h, sXP[0].n)
            P.sync.wait_ge(sXP[1].h, sXP[1].n)
            P.gpsimd.wait_ge(sXP[0].h, sXP[0].n)
            P.gpsimd.wait_ge(sXP[1].h, sXP[1].n)
            park_waited = True
        for k in ("qcos", "qsin", "kcos", "ksin"):
            P.sync.dma_start(tabs[k], dconst[k]).then_inc(sDS.h, 16); sDS.inc(16)
        P.sync.dma_start(m0t, dconst["m0t"]).then_inc(sDS.h, 16); sDS.inc(16)
        P.sync.dma_start(m4t, dconst["m4t"]).then_inc(sDS.h, 16); sDS.inc(16)
        tab_ready = sDS.n
        P.gpsimd.dma_start(WQKVv, dwqkv[l].rearrange("(c p) n -> p c n", p=128)) \
            .then_inc(sDG.h, 16); sDG.inc(16)
        P.gpsimd.dma_start(WOv, dwo[l].rearrange("(c p) n -> p c n", p=128)) \
            .then_inc(sDG.h, 16); sDG.inc(16)
        wqkv_ready = sDG.n
        P.tensor.wait_ge(sDS.h, tab_ready)
        P.tensor.wait_ge(sPL.h, 1)
        P.tensor.wait_ge(sDV.h, sDV.n)
        P.tensor.matmul(PR[:, 0:64], row(ones128f), row(qsrowb[l]),
                        start=True, stop=True)
        P.tensor.matmul(PR[:, 64:128], row(ones128f), row(ksrowb[l]),
                        start=True, stop=True).then_inc(sPE.h, 1); sPE.inc()
        P.vector.wait_ge(sPE.h, sPE.n)
        P.vector.tensor_copy(qsb, PR[:, 0:64])
        P.vector.tensor_copy(ksb, PR[:, 64:128]).then_inc(sDV.h, 1); sDV.inc()
        P.gpsimd.wait_ge(sDS.h, tab_ready)
        P.gpsimd.wait_ge(sDV.h, sDV.n)
        for k, sb in (("qcos", qsb), ("qsin", qsb), ("kcos", ksb), ("ksin", ksb)):
            for i in range(NT):
                op = P.gpsimd.tensor_tensor(tabv[k][:, i, :], tabv[k][:, i, :], sb,
                                            op=ALU.mult)
        op.then_inc(sPL.h, 1); sPL.inc()
        tabs_done = sPL.n
        P.gpsimd.memset(yTv[:, 0, :, 0:1], 0.0)
        P.gpsimd.memset(yTv[:, 1, :, 0:1], 0.0).then_inc(sPL.h, 1); sPL.inc()
        P.gpsimd.memset(VB65v[:, :, :, 64:65], 1.0).then_inc(sPL.h, 1); sPL.inc()
        vones_pl = sPL.n
        P.run()

        st = {"yc": 0, "qf": (0, 0), "qkt": 0, "wo": 0, "otb": 0, "norm": 0}
        ytr = {}
        qkvpe = {}
        qkvevac = {}
        rotp1 = {}
        rotrsq = {}
        rotdone = {}
        qkready = {}
        exp_done_sc = [0, 0]
        pts_rd_pe = [0, 0]

        def emitA_yroll(j):
            pj = j % 2
            P.vector.wait_ge(sPE.h, st["yc"])
            P.vector.tensor_copy(YC, XYv[:, j, :]).then_inc(sDV.h, 1); sDV.inc()
            P.tensor.wait_ge(sDV.h, sDV.n)
            P.tensor.wait_ge(sDV.h, qkready.get(j - 1, 0))
            for c in range(8):
                op = P.tensor.transpose(PT16b[:, c * 128:(c + 1) * 128],
                                        YC[:, c * 128:(c + 1) * 128], ident)
            op.then_inc(sPE.h, 1); sPE.inc()
            st["yc"] = sPE.n
            P.vector.wait_ge(sPE.h, sPE.n)
            P.vector.tensor_copy(yTv[:, pj, :, 1:129],
                                 PT16b.rearrange("p (c n) -> p c n", c=8))
            if j > 0:
                P.vector.tensor_copy(yTv[:, pj, :, 0:1],
                                     yTv[:, 1 - pj, :, 128:129]) \
                    .then_inc(sDV.h, 1); sDV.inc()
            ytr[j] = sDV.n

        def emitA_qkv(j, ca, cb):
            pj = j % 2
            if ca == 0:
                P.tensor.wait_ge(sDG.h, wqkv_ready)
                P.tensor.wait_ge(sDV.h, ytr[j])
                P.tensor.wait_ge(sPL.h, tabs_done + 1)
                if qkvevac.get(j - 1):
                    P.tensor.wait_ge(sDV.h, qkvevac[j - 1])
                if st["wo"]:
                    P.tensor.wait_ge(sDV.h, st["wo"])
            for c in range(ca, cb):
                lhs = yTv[:, pj, c, 1:129] if c < 4 else yTv[:, pj, c, 0:128]
                for n3 in range(3):
                    dst = PR if n3 == 2 else PBa[n3]
                    op = P.tensor.matmul(dst, lhs,
                                         WQKVv[:, c, n3 * 512:(n3 + 1) * 512],
                                         start=(c == 0), stop=(c == 7))
            op.then_inc(sPE.h, 1); sPE.inc()
            if cb == 8:
                qkvpe[j] = sPE.n

        def emitA_evac(j):
            P.vector.wait_ge(sPE.h, qkvpe[j])
            P.vector.tensor_copy(QF, PBa[0])
            P.vector.tensor_copy(KF, PBa[1])
            P.vector.tensor_copy(VB65v[:, j, :, 0:64],
                                 PR.rearrange("p (h e) -> p h e", h=8)) \
                .then_inc(sDV.h, 1); sDV.inc()
            qkvevac[j] = sDV.n

        rv3 = lambda a: a.rearrange("p (h e) -> p h e", h=8)
        rv4 = lambda a: a.rearrange("p (h b e) -> p h b e", h=8, b=2)

        def emitA_rotP1(j):
            ss_q = cp[:, 16:24]
            ss_k = cp[:, 48:56]
            P.vector.wait_ge(sPE.h, st["qkt"])
            P.vector.tensor_tensor(SQ16, QF, QF, op=ALU.mult)
            P.vector.tensor_reduce(ss_q, rv3(SQ16), axis=AX.X, op=ALU.add)
            P.vector.tensor_tensor(SQ16, KF, KF, op=ALU.mult)
            P.vector.tensor_reduce(ss_k, rv3(SQ16), axis=AX.X, op=ALU.add)
            P.vector.tensor_scalar_max(ss_q, ss_q, 1e-24)
            P.vector.tensor_scalar_max(ss_k, ss_k, 1e-24)
            P.vector.tensor_copy(rv4(RH)[:, :, 0, :], rv4(QF)[:, :, 1, :])
            P.vector.tensor_copy(rv4(RH)[:, :, 1, :], rv4(QF)[:, :, 0, :])
            P.vector.tensor_copy(rv4(M2)[:, :, 0, :], rv4(KF)[:, :, 1, :])
            P.vector.tensor_copy(rv4(M2)[:, :, 1, :], rv4(KF)[:, :, 0, :]) \
                .then_inc(sDV.h, 1); sDV.inc()
            rotp1[j] = sDV.n

        def emitA_rotRSQ(j):
            P.scalar.wait_ge(sDV.h, rotp1[j])
            P.scalar.activation(cp[:, 24:32], cp[:, 16:24], AF.Ln)
            P.scalar.activation(cp[:, 24:32], cp[:, 24:32], AF.Exp, scale=-0.5)
            P.scalar.activation(cp[:, 56:64], cp[:, 48:56], AF.Ln)
            P.scalar.activation(cp[:, 56:64], cp[:, 56:64], AF.Exp, scale=-0.5) \
                .then_inc(sSC.h, 1); sSC.inc()
            rotrsq[j] = sSC.n

        def emitA_rotP2(j):
            P.vector.wait_ge(sSC.h, rotrsq[j])
            rotq_dv = 0
            for SRC, RHx, OUTB, rn, ck, sk in (
                    (QF, RH, QRB, cp[:, 24:32], "qcos", "qsin"),
                    (KF, M2, KRB, cp[:, 56:64], "kcos", "ksin")):
                cosb = bcast_axis(tabv[ck][:, j, :], 1, 8)
                sinb = bcast_axis(tabv[sk][:, j, :], 1, 8)
                P.vector.tensor_tensor(rv3(SRC), rv3(SRC), cosb, op=ALU.mult)
                P.vector.tensor_tensor(rv3(RHx), rv3(RHx), sinb, op=ALU.mult)
                P.vector.tensor_tensor(SRC, SRC, RHx, op=ALU.add)
                rb = bcast_axis(rn, 2, 64)
                P.vector.tensor_tensor(rv3(OUTB), rv3(SRC), rb, op=ALU.mult) \
                    .then_inc(sDV.h, 1); sDV.inc()
                if SRC is QF:
                    rotq_dv = sDV.n
            rotdone[j] = (rotq_dv, sDV.n)
            if dbg_qr is not None and l == 0:
                P.gpsimd.wait_ge(sDV.h, rotdone[j][1])
                P.gpsimd.dma_start(dbg_qr.rearrange("(i p) n -> p i n", p=128)[:, j, :],
                                   QRB).then_inc(sDG.h, 16); sDG.inc(16)
                P.gpsimd.dma_start(dbg_kr.rearrange("(i p) n -> p i n", p=128)[:, j, :],
                                   KRB).then_inc(sDG.h, 16); sDG.inc(16)
                P.tensor.wait_ge(sDG.h, sDG.n)

        def emitA_qkT(j, half):
            if half == 0:
                P.tensor.wait_ge(sDV.h, rotdone[j][0])
                P.tensor.wait_ge(sDV.h, qkvevac[j])
                st["qkt"] = sPE.n + 1
                for c in range(4):
                    op = P.tensor.transpose(PT16b[:, c * 128:(c + 1) * 128],
                                            QRB[:, c * 128:(c + 1) * 128], ident)
                op.then_inc(sPE.h, 1); sPE.inc()
                return
            P.tensor.wait_ge(sDV.h, rotdone[j][1])
            for c in range(4):
                op = P.tensor.transpose(PT16b[:, 512 + c * 128:512 + (c + 1) * 128],
                                        KRB[:, c * 128:(c + 1) * 128], ident)
            op.then_inc(sPE.h, 1); sPE.inc()
            st["qkt"] = sPE.n
            P.vector.wait_ge(sPE.h, sPE.n)
            jsl = slice(j * 128, (j + 1) * 128)
            P.vector.tensor_copy(QT, PT16b[:, 0:512])
            P.vector.tensor_copy(
                KTv[:, :, jsl],
                PT16b[:, 512:1024].rearrange("p (c n) -> p c n", c=4)) \
                .then_inc(sDV.h, 1); sDV.inc()
            qkready[j] = sDV.n

        emitA_yroll(0)
        emitA_qkv(0, 0, 8)
        emitA_evac(0)
        emitA_rotP1(0)
        emitA_rotRSQ(0)
        emitA_rotP2(0)
        emitA_qkT(0, 0)
        emitA_qkT(0, 1)
        emitA_yroll(1)

        def tile_geo(j):
            if j < 4:
                return 128 * (j + 1), 0
            return 640, j * 128 - 512

        sim_pe = {}
        exp_sc = {}
        pv_pe = {}
        oev_sc = {}

        def emit_sim(j, h):
            nk, k0 = tile_geo(j)
            nch = nk // 128
            p = h % 2
            ch, pb = h // 2, 64 * (h % 2)
            Sb = SIMa[p]
            P.tensor.wait_ge(sDV.h, qkready[j])
            P.tensor.wait_ge(sSC.h, exp_done_sc[p])
            P.tensor.wait_ge(sDV.h, st["norm"])
            for c in range(nch):
                m = m4t if c == nch - 1 else \
                    (m0t if (c == 0 and nk == 640) else None)
                op = P.tensor.matmul(
                    Sb[:, c * 128:(c + 1) * 128],
                    KTv[:, ch, k0 + c * 128:k0 + (c + 1) * 128][pb:pb + 64, :],
                    QTv[:, ch, :][pb:pb + 64, :],
                    start=True, stop=(m is None))
                if m is not None:
                    op = P.tensor.matmul(Sb[:, c * 128:(c + 1) * 128],
                                         ident, m, start=False, stop=True)
            op.then_inc(sPE.h, 1); sPE.inc()
            sim_pe[(j, h)] = sPE.n

        def emit_exp(j, h):
            nk, k0 = tile_geo(j)
            p = h % 2
            P.scalar.wait_ge(sPE.h, sim_pe[(j, h)])
            P.scalar.wait_ge(sPE.h, pts_rd_pe[p])
            P.scalar.activation(PTS[p][:, 0:nk], SIMa[p][:, 0:nk], AF.Exp)
            exp_sc[(j, h)] = sSC.n
            exp_done_sc[p] = sSC.n

        emit_sim(0, 0)
        emit_sim(0, 1)
        emit_exp(0, 0)
        for i in range(NT):
            nk, k0 = tile_geo(i)
            nch = nk // 128
            tsl = slice(i * 128, (i + 1) * 128)
            if i + 1 < NT:
                emitA_qkv(i + 1, 0, 8)
            for h in range(HEADS):
                p = h % 2
                obank, ocol = SIMa[p], 640 + (h // 2) * 65
                if h + 2 < HEADS:
                    emit_sim(i, h + 2)
                P.tensor.wait_ge(sSC.h, exp_sc[(i, h)])
                if h == 0:
                    P.tensor.wait_ge(sPL.h, vones_pl)
                for c in range(nch):
                    vt = (k0 + c * 128) // 128
                    op = P.tensor.matmul(obank[:, ocol:ocol + 65],
                                         PTS[p][:, c * 128:(c + 1) * 128],
                                         VB65v[:, vt, h, :],
                                         start=(c == 0), stop=(c == nch - 1))
                op.then_inc(sPE.h, 1); sPE.inc()
                pv_pe[(i, h)] = sPE.n
                pts_rd_pe[p] = sPE.n
                if i + 1 < NT:
                    if h == 3:
                        emitA_evac(i + 1)
                        emitA_rotP1(i + 1)
                    elif h == 5:
                        emitA_rotRSQ(i + 1)
                        emitA_rotP2(i + 1)
                    elif h == 6:
                        emitA_qkT(i + 1, 0)
                    elif h == 7:
                        emitA_qkT(i + 1, 1)
                        if i + 2 < NT:
                            emitA_yroll(i + 2)
                if h + 1 < HEADS:
                    emit_exp(i, h + 1)
                P.scalar.wait_ge(sPE.h, pv_pe[(i, h)])
                P.scalar.copy(ONBv[:, h, :], obank[:, ocol:ocol + 64])
                oev_sc[h] = sSC.n
            tails0 = SIMa[0][:, 640:900].rearrange("p (k s) -> p k s", s=65)
            tails1 = SIMa[1][:, 640:900].rearrange("p (k s) -> p k s", s=65)
            rd8 = cp[:, 32:40]
            rd8v = rd8.rearrange("p (k e) -> p k e", e=2)
            P.vector.wait_ge(sPE.h, pv_pe[(i, HEADS - 1)])
            P.vector.reciprocal(rd8v[:, :, 0:1], tails0[:, :, 64:65])
            P.vector.reciprocal(rd8v[:, :, 1:2], tails1[:, :, 64:65])
            P.vector.wait_ge(sSC.h, oev_sc[HEADS - 1])
            P.vector.tensor_tensor(rv3(ONB), rv3(ONB), bcast_axis(rd8, 2, 64),
                                   op=ALU.mult).then_inc(sDV.h, 1); sDV.inc()
            st["norm"] = sDV.n
            P.tensor.wait_ge(sDV.h, st["norm"])
            P.tensor.wait_ge(sSC.h, st["otb"])
            for c in range(4):
                op = P.tensor.transpose(PT16[:, c * 128:(c + 1) * 128],
                                        ONB[:, c * 128:(c + 1) * 128], ident)
            op.then_inc(sPE.h, 1); sPE.inc()
            ot_pe = sPE.n
            P.scalar.wait_ge(sPE.h, ot_pe)
            P.scalar.copy(OTB, PT16[:, 0:512])
            st["otb"] = sSC.n
            ot_ready = sSC.n
            if i + 1 < NT:
                emit_sim(i + 1, 0)
                emit_sim(i + 1, 1)
                emit_exp(i + 1, 0)

            P.tensor.wait_ge(sSC.h, ot_ready)
            if i + 1 < NT:
                P.tensor.wait_ge(sDV.h, qkvevac[i + 1])
            if st["wo"]:
                P.tensor.wait_ge(sDV.h, st["wo"])
            for c in range(4):
                for half in range(2):
                    op = P.tensor.matmul(PBa[half],
                                         OTBv[:, c, :],
                                         WOv[:, c, half * 512:(half + 1) * 512],
                                         start=(c == 0), stop=(c == 3))
            op.then_inc(sPE.h, 1); sPE.inc()
            P.vector.wait_ge(sPE.h, sPE.n)
            P.vector.tensor_tensor(XYv[:, i, 0:512], XYv[:, i, 0:512], PBa[0],
                                   op=ALU.add)
            P.vector.tensor_tensor(XYv[:, i, 512:1024], XYv[:, i, 512:1024], PBa[1],
                                   op=ALU.add).then_inc(sDV.h, 1); sDV.inc()
            st["wo"] = sDV.n

            if dbg_o is not None and l == 0:
                P.gpsimd.wait_ge(sSC.h, ot_ready)
                P.gpsimd.dma_start(dbg_o.rearrange("(i p) n -> p i n", p=128)[:, i, :],
                                   OTB).then_inc(sDG.h, 16); sDG.inc(16)
                P.tensor.wait_ge(sDG.h, sDG.n)
        P.run()

        if dbg_yn is not None and l == 0:
            P.sync.wait_ge(sDV.h, sDV.n)
            P.sync.dma_start(dbg_yn.rearrange("(i p) n -> p i n", p=128), XYv) \
                .then_inc(sDS.h, 16); sDS.inc(16)
            P.vector.wait_ge(sDS.h, sDS.n)
            P.run()

        if stop_after == f"L{l}C":
            _dev_finish(P, sDX, dout, XYv)
            return nc, dbg_out

        for _b in (WQKV, WO, KT, VB, QT, yT, QRB, KRB, ONB,
                   PTS[0], PTS[1], OTB):
            A16.freeap(_b)
        for _b in (m0t, m4t, QF, KF, SQ16, RH, M2, RB8):
            A16.freeap(_b)
        for _k in tabs:
            A32.freeap(tabs[_k])
        A32.freeap(qsb)
        A32.freeap(ksb)

        P.mark(f"L{l}_attnABC")
        yTN = A16.alloc(8 * 2064, top=True)
        yTNv = yTN.rearrange("p (c n) -> p c n", c=8)[:, :, 0:2049]
        murow = A16.alloc(T)
        rsrow = A16.alloc(T)
        frow_t = A32.alloc(T)
        H2a = A16.alloc(4 * T)
        H2b = A16.alloc(4 * T)
        _h2av = H2a.rearrange("p (c n) -> p c n", c=4)
        _h2bv = H2b.rearrange("p (c n) -> p c n", c=4)

        class _H2V:
            def __getitem__(self, idx):
                _, c, sl = idx
                return (_h2av if c < 4 else _h2bv)[:, c % 4, sl]
        H2v = _H2V()
        SQB = A16.alloc(512)
        P.gpsimd.memset(yTNv[:, :, 0:1], 0.0).then_inc(sPL.h, 1); sPL.inc()

        for i in range(NT):
            P.vector.wait_ge(sPE.h, sPE.n)
            P.vector.tensor_copy(YC, XYv[:, i, :]).then_inc(sDV.h, 1); sDV.inc()
            P.tensor.wait_ge(sDV.h, sDV.n)
            P.tensor.wait_ge(sSC.h, sSC.n)
            for c in range(8):
                op = P.tensor.transpose(PT16[:, c * 128:(c + 1) * 128],
                                        YC[:, c * 128:(c + 1) * 128], ident)
            op.then_inc(sPE.h, 1); sPE.inc()
            P.scalar.wait_ge(sPE.h, sPE.n)
            for c in range(8):
                op = P.scalar.copy(yTNv[:, c, 1 + i * 128:1 + (i + 1) * 128],
                                   PT16[:, c * 128:(c + 1) * 128])
            op.then_inc(sSC.h, 1); sSC.inc()
        ytn_done = sSC.n
        P.run()

        def h2src(c, tsl):
            if c < 4:
                return yTNv[:, c, 1 + tsl.start:1 + tsl.stop]
            return yTNv[:, c, tsl.start:tsl.stop]

        for tt in range(4):
            tsl = slice(tt * 512, tt * 512 + 512)
            P.tensor.wait_ge(sSC.h, ytn_done)
            P.tensor.wait_ge(sPL.h, sPL.n)
            if tt > 0:
                P.tensor.wait_ge(sDV.h, sDV.n)
            for c in range(8):
                op = P.tensor.matmul(PB[0][0:1, 0:512], onescol[:, 0:1],
                                     h2src(c, tsl), start=(c == 0), stop=(c == 7))
            op.then_inc(sPE.h, 1); sPE.inc()
            for c in range(8):
                P.scalar.wait_ge(sPE.h, sPE.n)
                op = P.scalar.activation(SQB, h2src(c, tsl), AF.Square)
                op.then_inc(sSC.h, 1); sSC.inc()
                P.tensor.wait_ge(sSC.h, sSC.n)
                op = P.tensor.matmul(PB[1][0:1, 0:512], onescol[:, 0:1],
                                     SQB, start=(c == 0), stop=(c == 7))
                op.then_inc(sPE.h, 1); sPE.inc()
            P.vector.wait_ge(sPE.h, sPE.n)
            mrow = row(murow)[:, tsl]
            rrow = row(rsrow)[:, tsl]
            frow = row(frow_t)[:, tsl]
            P.vector.tensor_scalar_mul(frow, PB[0][0:1, 0:512], 1.0 / D)
            P.vector.tensor_copy(mrow, frow)
            P.vector.scalar_tensor_tensor(frow, frow, -1.0, frow,
                                          op0=ALU.mult, op1=ALU.mult)
            P.vector.scalar_tensor_tensor(frow, PB[1][0:1, 0:512], 1.0 / D, frow,
                                          op0=ALU.mult, op1=ALU.add)
            P.vector.tensor_scalar_add(frow, frow, 1e-05).then_inc(sDV.h, 1); sDV.inc()
            P.scalar.wait_ge(sDV.h, sDV.n)
            P.scalar.activation(frow, frow, AF.Sqrt).then_inc(sSC.h, 1); sSC.inc()
            P.vector.wait_ge(sSC.h, sSC.n)
            P.vector.reciprocal(frow, frow)
            P.vector.tensor_copy(rrow, frow).then_inc(sDV.h, 1); sDV.inc()
        P.run()

        MUB = A16.alloc(T)
        RSB = A16.alloc(T)
        for tt in range(4):
            tsl = slice(tt * 512, tt * 512 + 512)
            P.tensor.wait_ge(sDV.h, sDV.n)
            if tt > 0:
                P.tensor.wait_ge(sSC.h, sSC.n)
            P.tensor.matmul(PB[2][0:128, 0:512], onesrow5[0:1, 0:128],
                            row(murow)[:, tsl], start=True, stop=True)
            P.tensor.matmul(PB[3][0:128, 0:512], onesrow5[0:1, 0:128],
                            row(rsrow)[:, tsl], start=True, stop=True) \
                .then_inc(sPE.h, 1); sPE.inc()
            P.scalar.wait_ge(sPE.h, sPE.n)
            P.scalar.copy(MUB[:, tsl], PB[2][0:128, 0:512])
            P.scalar.copy(RSB[:, tsl], PB[3][0:128, 0:512]) \
                .then_inc(sSC.h, 1); sSC.inc()
        bc_done = sSC.n

        for c in range(8):
            src_ = yTNv[:, c, 1:2049] if c < 4 else yTNv[:, c, 0:2048]
            P.vector.wait_ge(sSC.h, bc_done)
            P.vector.tensor_tensor(H2v[:, c, :], src_, MUB, op=ALU.subtract)
            P.vector.tensor_tensor(H2v[:, c, :], H2v[:, c, :], RSB, op=ALU.mult) \
                .then_inc(sDV.h, 1); sDV.inc()
            P.scalar.wait_ge(sDV.h, sDV.n)
            P.scalar.activation(H2v[:, c, :], H2v[:, c, :], AF.Identity,
                                scale=lgcol[l][:, c:c + 1], bias=lbcol[l][:, c:c + 1]) \
                .then_inc(sSC.h, 1); sSC.inc()
        P.run()
        A16.freeap(yTN)
        A16.freeap(MUB)
        A16.freeap(RSB)
        A16.freeap(SQB)
        A16.freeap(murow)
        A16.freeap(rsrow)
        A32.freeap(frow_t)

        if dbg_h2 is not None and l == 0:
            scr = A32.alloc(T)
            for c in range(8):
                P.vector.tensor_copy(scr, H2v[:, c, :]).then_inc(sDV.h, 1); sDV.inc()
                P.sync.wait_ge(sDV.h, sDV.n)
                P.sync.dma_start(dbg_h2.rearrange("p (c n) -> p c n", c=8)[:, c, :],
                                 scr).then_inc(sDS.h, 16); sDS.inc(16)
                P.vector.wait_ge(sDS.h, sDS.n)
            P.run()
            A32.freeap(scr)

        if stop_after == f"L{l}D":
            _dev_finish(P, sDX, dout, XYv)
            return nc, dbg_out

        P.mark(f"L{l}_D")
        A16.freeap(YC)
        PP = A16.alloc(22 * 512, top=True)
        W1S = A16.alloc(2 * 2 * 8 * 128)
        W2C = [[A16.alloc(512) for _ in range(22)] for _ in range(2)]
        W1Sv = W1S.rearrange("p (b a k n) -> p b a k n", b=2, a=2, k=8)
        B1L = A16.alloc(2816)
        B2L = A16.alloc(512)

        def b1ap(off, n):
            if off < 2816:
                return B1L[0:1, off:off + n]
            return B1L[32:33, off - 2816:off - 2816 + n]
        P.gpsimd.dma_start(B1L[0:1, :], db1[l][:, 0:2816]).then_inc(sDG.h, 16)
        sDG.inc(16)
        P.gpsimd.dma_start(B1L[32:33, 0:FF2 - 2816], db1[l][:, 2816:FF2]) \
            .then_inc(sDG.h, 16); sDG.inc(16)
        P.gpsimd.dma_start(B2L[0:1, :], db2[l][:, 0:512]).then_inc(sDG.h, 16)
        sDG.inc(16)
        P.gpsimd.dma_start(B2L[32:33, :], db2[l][:, 512:1024]) \
            .then_inc(sDG.h, 16); sDG.inc(16)
        P.sync.wait_ge(sDG.h, pc_gate[l])
        P.run()

        w1ch = _w1_chunks()
        w1pairs = [(w1ch[2 * b], w1ch[2 * b + 1]) for b in range(22)]
        w1off = []
        off = 0
        for coff, npart, kind, pch in w1ch:
            w1off.append(off)
            off += 8 * npart
        if l == 1:
            fstat_dv = {}
        w2h_pe = [0, 0]
        w2h_ready = [0, 0]
        pair_pe = {}
        pair_rd = {}

        def load_w2h(colh):
            csl = slice(colh * 512, colh * 512 + 512)
            eng = P.sync if colh == 0 else P.gpsimd
            eng.wait_ge(sPE.h, w2h_pe[colh])
            for ic in range(22):
                npart = 128 if ic < 21 else 42
                eng.dma_start(W2C[colh][ic][0:npart, :],
                              dw2b[l][ic * 128:ic * 128 + npart, csl]) \
                    .then_inc(sDS.h, 16); sDS.inc(16)
            w2h_ready[colh] = sDS.n

        for q in range(4):
            qsl = slice(q * 512, q * 512 + 512)
            load_w2h(0)

            pa_free = {}
            gel_free = {}
            gelp = [None, None]
            for b, (ach, gch) in enumerate(w1pairs):
                npart = ach[1]
                par = b % 2
                weng = P.sync if par == 0 else P.gpsimd
                weng.wait_ge(sPE.h, pair_pe.get((q, b - 2), 0 if q == 0 else
                                                pair_pe.get((q - 1, 20 + b), 0)))
                boff = w1off[b * 2]
                weng.dma_start(
                    W1Sv[:, par, :, :, 0:npart] if npart == 128 else
                    W1Sv[:, par, :, :, 0:npart],
                    dw1b[l][:, boff:boff + 16 * npart]
                    .rearrange("p (a k n) -> p a k n", a=2, k=8)) \
                    .then_inc(sW3[par].h, 16)
                sW3[par].inc(16)
                pair_rd[(q, b)] = sW3[par].n
                P.tensor.wait_ge(sW3[par].h, pair_rd[(q, b)])
                P.tensor.wait_ge(sDV.h, pa_free.get(b % 2, 0))
                psl_a = PB[2 + 2 * (b % 2)][0:npart, 0:512]
                psl_g = PB[3 if b % 2 == 0 else 5][0:npart, 0:512]
                for kc in range(8):
                    P.tensor.matmul(psl_a, W1Sv[:, par, 0, kc, 0:npart],
                                    H2v[:, kc, qsl], start=(kc == 0), stop=False)
                P.tensor.matmul(psl_a, b1ap(ach[0], npart),
                                onesrow5[0:1, :] if ach[0] < 2816 else
                                onesrow5[32:33, :], start=False, stop=True)
                P.tensor.wait_ge(sSC.h, gel_free.get(b % 2, 0))
                gel_free[b % 2] = sSC.n + 1
                for kc in range(8):
                    P.tensor.matmul(psl_g, W1Sv[:, par, 1, kc, 0:npart],
                                    H2v[:, kc, qsl], start=(kc == 0), stop=False)
                op = P.tensor.matmul(psl_g, b1ap(gch[0], npart),
                                     onesrow5[0:1, :] if gch[0] < 2816 else
                                     onesrow5[32:33, :], start=False, stop=True)
                op.then_inc(sPE.h, 1); sPE.inc()
                pair_pe[(q, b)] = sPE.n
                P.scalar.wait_ge(sPE.h, sPE.n)
                dst = PP[0:npart, ach[3] * 512:ach[3] * 512 + 512]
                P.scalar.activation(dst, psl_g, AF.Gelu)
                P.vector.wait_ge(sSC.h, sSC.n)
                P.vector.tensor_tensor(dst, dst, psl_a, op=ALU.mult)
                pa_free[b % 2] = sDV.n
            pp_done_dv = sDV.n
            load_w2h(1)
            P.run()

            for colh in range(2):
                csl = slice(colh * 512, colh * 512 + 512)
                w2ps_free = [0, 0]
                for it in range(4):
                    i = q * 4 + it
                    bank = 4 + (it % 2)
                    P.tensor.wait_ge(sDS.h, w2h_ready[colh])
                    if it == 0:
                        P.tensor.wait_ge(sDV.h, pp_done_dv)
                    P.tensor.wait_ge(sDV.h, w2ps_free[it % 2])
                    for ic in range(22):
                        npart = 128 if ic < 21 else 42
                        lhs = PP[0:npart, ic * 512 + it * 128:ic * 512 + (it + 1) * 128]
                        rhs = W2C[colh][ic][0:npart, :]
                        P.tensor.matmul(PB[bank][0:128, 0:512], lhs, rhs,
                                        start=(ic == 0), stop=False)
                    _r = 0 if colh == 0 else 32
                    op = P.tensor.matmul(PB[bank][0:128, 0:512],
                                         onesrow5[_r:_r + 1, 0:128],
                                         B2L[_r:_r + 1, :], start=False, stop=True)
                    op.then_inc(sPE.h, 1); sPE.inc()
                    w2h_pe[colh] = sPE.n
                    P.vector.wait_ge(sPE.h, sPE.n)
                    P.vector.tensor_tensor(XYv[:, i, csl], XYv[:, i, csl],
                                           PB[bank][0:128, 0:512], op=ALU.add)
                    w2ps_free[it % 2] = sDV.n
                    if l == 1 and colh == 1:
                        ln_stats(XYv[:, i, :], fstatv[:, i, 0:1], fstatv[:, i, 1:2])
                        fstat_dv[i] = sDV.n
            P.run()

        A16.freeap(PP)
        for _cl in W2C:
            for _b in _cl:
                A16.freeap(_b)
        A16.freeap(W1S)
        A16.freeap(B1L)
        A16.freeap(B2L)
        A16.freeap(H2a)
        A16.freeap(H2b)

        if dbg_y3 is not None and l == 0:
            P.sync.wait_ge(sDV.h, sDV.n)
            P.sync.dma_start(dbg_y3.rearrange("(i p) n -> p i n", p=128), XYv) \
                .then_inc(sDS.h, 16); sDS.inc(16)
            P.vector.wait_ge(sDS.h, sDS.n)
            P.run()

        if stop_after == f"L{l}E":
            _dev_finish(P, sDX, dout, XYv)
            return nc, dbg_out

    P.mark("L_E_last")
    x1r = A32.alloc(2 * D)
    x1rv = x1r.rearrange("p (b n) -> p b n", b=2)
    OT = A32.alloc(2 * D)
    OTv = OT.rearrange("p (b n) -> p b n", b=2)
    for i in range(min(2, NT)):
        P.scalar.dma_start(x1rv[:, i % 2, :],
                           park.rearrange("(i p) n -> p i n", p=128)[:, i, :]) \
            .then_inc(sUP[i % 2].h, 16)
        sUP[i % 2].inc(16)
    for i in range(NT):
        bpar = i % 2
        mu, var, rs, nb = (cp[:, k:k + 1] for k in (40, 41, 42, 43))
        P.scalar.wait_ge(sDV.h, sDV.n)
        P.scalar.activation(rs, fstatv[:, i, 1:2], AF.Sqrt).then_inc(sSC.h, 1)
        sSC.inc()
        P.vector.wait_ge(sSC.h, sSC.n)
        P.vector.reciprocal(rs, rs)
        P.vector.tensor_scalar(nb, fstatv[:, i, 0:1], rs, -1.0, op0=ALU.mult,
                               op1=ALU.mult).then_inc(sDV.h, 1); sDV.inc()
        P.scalar.wait_ge(sDV.h, sDV.n)
        P.scalar.activation(sq_scr, XYv[:, i, :], AF.Identity, scale=rs, bias=nb) \
            .then_inc(sSC.h, 1); sSC.inc()
        P.vector.wait_ge(sSC.h, sSC.n)
        P.vector.wait_ge(sUP[bpar].h, sUP[bpar].n)
        if i >= 2:
            P.vector.wait_ge(sST[bpar].h, sST[bpar].n)
        P.vector.tensor_tensor(OTv[:, bpar, :], x1rv[:, bpar, :], sq_scr,
                               op=ALU.add).then_inc(sDV.h, 1); sDV.inc()
        P.scalar.wait_ge(sDV.h, sDV.n)
        P.scalar.dma_start(dout.rearrange("(i p) n -> p i n", p=128)[:, i, :],
                           OTv[:, bpar, :]).then_inc(sST[bpar].h, 16)
        sST[bpar].inc(16)
        if i + 2 < NT:
            P.scalar.wait_ge(sDV.h, sDV.n)
            P.scalar.dma_start(x1rv[:, (i + 2) % 2, :],
                               park.rearrange("(i p) n -> p i n", p=128)[:, i + 2, :]) \
                .then_inc(sUP[(i + 2) % 2].h, 16)
            sUP[(i + 2) % 2].inc(16)
    P.scalar.wait_ge(sST[0].h, sST[0].n)
    P.scalar.wait_ge(sST[1].h, sST[1].n)
    P.run()

    P.mark("P6")
    return nc, dbg_out


def _dev_finish(P, sDX, dout, XYv):
    P.scalar.dma_start(dout.rearrange("(i p) n -> p i n", p=128), XYv) \
        .then_inc(sDX.h, 16); sDX.inc(16)
    P.scalar.wait_ge(sDX.h, sDX.n)
    P.run()


def _prep_inputs(inputs, core):
    f = np.float32
    perm = _w1_perm()
    d = {}
    d["x"] = np.ascontiguousarray(np.asarray(inputs["L0_attn_x"], f)[core])
    d["WoutT"] = np.ascontiguousarray(np.asarray(inputs["L0_attn_W_out"], f).T)
    ssm = []
    for key in ("L0_attn_W_in", "L0_attn_W_ig", "L0_attn_W_rg"):
        Wt = np.asarray(inputs[key], f).T
        ssm.append(Wt.reshape(8, 128, 8, 128).transpose(1, 2, 0, 3))
    ssm = np.stack(ssm, axis=2)
    d["ssmprep"] = np.ascontiguousarray(ssm.reshape(128, 8 * 3 * 8 * 128))
    d["b_ig"] = np.asarray(inputs["L0_attn_b_ig"], f).reshape(1, D)
    d["b_rg"] = np.asarray(inputs["L0_attn_b_rg"], f).reshape(1, D)
    d["lam8"] = np.ascontiguousarray(np.asarray(inputs["L0_attn_lam"], f).reshape(8, 128))
    d["h08"] = np.ascontiguousarray(np.asarray(inputs["L0_attn_h0"], f).reshape(8, 128))
    d["nw8"] = np.ascontiguousarray(np.asarray(inputs["L0_attn_norm_w"], f).reshape(8, 128))
    wqkv = np.asarray(inputs["L0_ffn_wqkv"], f)
    wo = np.asarray(inputs["L0_ffn_wo"], f)
    w1 = np.asarray(inputs["L0_ffn_w1"], f)
    b1 = np.asarray(inputs["L0_ffn_b1"], f)
    w2 = np.asarray(inputs["L0_ffn_w2"], f)
    b2 = np.asarray(inputs["L0_ffn_b2"], f)
    qs = np.asarray(inputs["L0_ffn_q_scale"], f)
    ks = np.asarray(inputs["L0_ffn_k_scale"], f)
    lg = np.asarray(inputs["L0_ffn_ln_g"], f)
    lb = np.asarray(inputs["L0_ffn_ln_b"], f)
    for l in range(2):
        d[f"wqkv{l}"] = np.ascontiguousarray(wqkv[l])
        d[f"wo{l}"] = np.ascontiguousarray(wo[l])
        wp3 = w1[l][:, perm].reshape(8, 128, FF2)
        segs = []
        off = 0
        for coff, npart, kind, pch in _w1_chunks():
            segs.append(wp3[:, :, coff:coff + npart].transpose(1, 0, 2)
                        .reshape(128, 8 * npart))
        d[f"w1prep{l}"] = np.ascontiguousarray(np.concatenate(segs, axis=1))
        d[f"b1p{l}"] = np.ascontiguousarray(b1[l][perm].reshape(1, FF2))
        d[f"w2_{l}"] = np.ascontiguousarray(w2[l])
        d[f"b2_{l}"] = np.ascontiguousarray(b2[l].reshape(1, D))
        d[f"qs{l}"] = np.ascontiguousarray(qs[l].reshape(1, DH))
        d[f"ks{l}"] = np.ascontiguousarray(ks[l].reshape(1, DH))
        d[f"lg8_{l}"] = np.ascontiguousarray(lg[l].reshape(8, 128))
        d[f"lb8_{l}"] = np.ascontiguousarray(lb[l].reshape(8, 128))
    return d


_CACHE = {}


def kernel(**inputs):
    key = "main"
    if key not in _CACHE:
        _CACHE[key] = build()
    nc, _ = _CACHE[key]
    in_maps = [_prep_inputs(inputs, c) for c in range(8)]
    res = run_bass_kernel_spmd(nc, in_maps, core_ids=list(range(8)))
    out = np.stack([res.results[c]["out"] for c in range(8)], axis=0)
    return out.astype(np.float32)


# version 26
# speedup vs baseline: 1.1872x; 1.0222x over previous
import numpy as np
import ml_dtypes
import concourse.bass as bass
import concourse.mybir as mybir
from concourse.bass_utils import run_bass_kernel_spmd

F32 = mybir.dt.float32
BF16 = mybir.dt.bfloat16
AF = mybir.ActivationFunctionType
ALU = mybir.AluOpType
AX = mybir.AxisListType

T, D, NT = 2048, 1024, 16
HEADS, DH, WIN = 8, 64, 512
HID, FF2 = 2730, 5460
NEG = -1.0e30
BF = ml_dtypes.bfloat16
ENGINES = ("tensor", "vector", "scalar", "gpsimd", "sync")


class RecOp:
    __slots__ = ("meth", "a", "k", "incs")

    def __init__(self, meth, a, k):
        self.meth, self.a, self.k = meth, a, k
        self.incs = []

    def then_inc(self, sem, k=1):
        self.incs.append((sem, k))
        return self

    def replay(self, eng):
        try:
            r = getattr(eng, self.meth)(*self.a, **self.k)
        except Exception as e:
            shapes = [tuple(x.shape) if isinstance(x, bass.AP) else x
                      for x in self.a]
            raise RuntimeError(f"replay {self.meth} {shapes} {self.k}: {e}") from e
        merged = {}
        for sem, k in self.incs:
            merged[sem] = merged.get(sem, 0) + k
        for sem, k in merged.items():
            r = r.then_inc(sem, k)


def _apspan(ap):
    try:
        W = ap.tensor.shape[1]
        off = ap.offset
        if not isinstance(off, int):
            return None
        c0 = off % W
        pairs = list(ap.ap)[1:]
        span = 1
        for st, num in pairs:
            span += abs(st) * (num - 1)
        return (ap.tensor.name, c0, c0 + span)
    except Exception:
        return None


def _overlap(a, b):
    return a and b and a[0] == b[0] and a[1] < b[2] and b[1] < a[2]


class Rec:

    def __init__(self, phase, engname):
        self._phase, self._eng = phase, engname

    def __getattr__(self, meth):
        def call(*a, **k):
            ph = self._phase
            ss = ph.selfsem.get(self._eng)
            if ss is None or meth in ("wait_ge", "sem_inc"):
                op = RecOp(meth, a, k)
                ph.ops.append((self._eng, op))
                return op
            st = ph.dep.setdefault(self._eng, {"writes": [], "waited": 0})
            outs, ins = [], []
            aps = [x for x in a if isinstance(x, bass.AP)]
            if meth in ("dma_start",):
                outs = aps[:1]
                ins = aps[1:]
            elif meth in ("memset", "memzero"):
                outs = aps[:1]
            else:
                outs = aps[:1]
                ins = aps[1:]
            for kk in ("bias", "scale", "initial", "scalar1", "scalar2", "scalar"):
                v = k.get(kk)
                if isinstance(v, bass.AP):
                    ins.append(v)
            ao = k.get("accum_out")
            if isinstance(ao, bass.AP):
                outs.append(ao)
            need = 0
            spans_in = [_apspan(x) for x in ins]
            spans_out = [_apspan(x) for x in outs]
            for w_span, cnt in st["writes"]:
                if cnt <= st["waited"]:
                    continue
                for sp in spans_in + spans_out:
                    if _overlap(sp, w_span):
                        need = max(need, cnt)
            if need > st["waited"]:
                ph.ops.append((self._eng, RecOp("wait_ge", (ss.h, need), {})))
                st["waited"] = need
            op = RecOp(meth, a, k)
            if meth != "dma_start":
                op.incs.append((ss.h, 1))
                ss.inc()
                for sp in spans_out:
                    if sp:
                        st["writes"].append((sp, ss.n))
                if len(st["writes"]) > 24:
                    st["writes"] = st["writes"][-24:]
            ph.ops.append((self._eng, op))
            return op
        return call


class Phase:
    def __init__(self, nc, selfsem=None):
        self.nc = nc
        self.ops = []
        self.selfsem = selfsem or {}
        self.nblocks = 0
        self.marks = []
        self.dep = {}
        for e in ENGINES:
            setattr(self, e, Rec(self, e))

    def mark(self, label):
        snap = self.snap() if hasattr(self, "snap") else {}
        self.marks.append((label, self.nblocks, snap))

    def run(self):
        if not self.ops:
            return
        used = [e for e in ENGINES if any(o[0] == e for o in self.ops)]
        with self.nc.Block() as blk:
            for e in used:
                eops = [op for en, op in self.ops if en == e]
                getattr(blk, e)(lambda eng, eops=eops: [op.replay(eng) for op in eops])
        self.ops = []
        self.nblocks += 1


class S:
    def __init__(self, h):
        self.h, self.n = h, 0

    def inc(self, k=1):
        self.n += k
        return self.n


class FArena:

    def __init__(self, nc, name, dtype, cols):
        self.t = nc.alloc_sbuf_tensor(name, [128, cols], dtype)
        self.free = [(0, cols)]
        self.meta = {}
        self.name = name

    def alloc16(self, n, align=16, top=False):
        assert n % 2 == 0
        ap = self.alloc(n // 2, align, top)
        b = ap.bitcast(BF16)
        self.meta[id(b)] = self.meta.pop(id(ap))
        return b

    def alloc(self, n, align=16, top=False):
        order = list(enumerate(self.free))
        if top:
            order = order[::-1]
        for idx, (o, ln) in order:
            if top:
                oa = (o + ln - n) // align * align
                if oa < o:
                    continue
            else:
                oa = (o + align - 1) // align * align
                if oa + n > o + ln:
                    continue
            pre = (o, oa - o)
            post = (oa + n, o + ln - oa - n)
            repl = [p for p in (pre, post) if p[1] > 0]
            self.free[idx:idx + 1] = repl
            ap = self.t[:, oa:oa + n]
            self.meta[id(ap)] = (oa, n)
            return ap
        raise MemoryError(f"{self.name}: no slot for {n} (free={self.free})")

    def freeap(self, ap):
        o, n = self.meta.pop(id(ap))
        self.free.append((o, n))
        self.free.sort()
        merged = []
        for seg in self.free:
            if merged and merged[-1][0] + merged[-1][1] == seg[0]:
                merged[-1] = (merged[-1][0], merged[-1][1] + seg[1])
            else:
                merged.append(list(seg) and (seg[0], seg[1]))
        self.free = merged

    def used_bytes(self, esz):
        tot = self.t.shape[1] - sum(n for _, n in self.free)
        return tot * esz


def bcast_axis(ap, pos, n):
    pairs = [list(p) for p in list(ap.ap)]
    pairs.insert(pos, [0, n])
    return bass.AP(ap.tensor, ap.offset, pairs)


def _host_constants():
    c = {}
    c["ident"] = np.eye(128, dtype=BF)
    c["id8f"] = np.eye(8, dtype=np.float32)
    c["id1f"] = np.ones((1, 1), np.float32)
    kk = np.arange(128)[:, None]
    qq = np.arange(128)[None, :]
    c["m0t"] = np.where(kk > qq, 0.0, NEG).astype(BF)
    c["m4t"] = np.where(kk <= qq, 0.0, NEG).astype(BF)
    half = np.arange(0, DH, 2, dtype=np.float64)
    inv_freq = 1.0 / 10000.0 ** (half / DH)
    t = np.arange(T, dtype=np.float64)
    freqs = np.concatenate([t[:, None] * inv_freq[None, :]] * 2, axis=-1)
    base = (half + 0.4 * DH) / (1.4 * DH)
    power = (t - T // 2) / float(WIN // 2)
    scale = np.concatenate([base[None, :] ** power[:, None]] * 2, axis=-1)
    cos, sin = np.cos(freqs), np.sin(freqs)

    def pk(a):
        return np.ascontiguousarray(
            a.reshape(16, 128, DH).transpose(1, 0, 2).reshape(128, 16 * DH)
        ).astype(BF)
    sgn = np.concatenate([-np.ones(DH // 2), np.ones(DH // 2)])
    c["qcos"] = pk(cos * scale * 8.0)
    c["qsin"] = pk(sin * scale * 8.0 * sgn)
    c["kcos"] = pk(cos / scale)
    c["ksin"] = pk(sin / scale * sgn)
    e4 = np.zeros((128, 16), np.float32)
    for t in range(4):
        e4[:, t * 4 + t] = 1.0
    c["e4"] = e4.astype(BF)
    sd = np.zeros((128, 128), np.float32)
    sd[np.arange(127), np.arange(1, 128)] = 1.0
    c["shiftmat"] = sd
    ep = np.zeros((128, 128), np.float32)
    ep[127, 0] = 1.0
    c["prevmat"] = ep
    return c


def _w1_perm():
    idx = []
    for b in range(21):
        idx.extend(range(b * 128, b * 128 + 128))
        idx.extend(range(HID + b * 128, HID + b * 128 + 128))
    idx.extend(range(21 * 128, HID))
    idx.extend(range(HID + 21 * 128, FF2))
    return np.array(idx)


def _w1_chunks():
    out = []
    off = 0
    for b in range(21):
        out.append((off, 128, "a", b)); off += 128
        out.append((off, 128, "g", b)); off += 128
    out.append((off, 42, "a", 21)); off += 42
    out.append((off, 42, "g", 21)); off += 42
    return out


def build(dbg=(), stop_after=None):
    nc = bass.Bass("TRN2", target_bir_lowering=False)
    C = _host_constants()

    dx = nc.dram_tensor("x", [T, D], F32, kind="ExternalInput").ap()
    dout = nc.dram_tensor("out", [T, D], F32, kind="ExternalOutput").ap()
    park = nc.dram_tensor("park_x1", [T, D], F32, kind="Internal").ap()

    dWT = {w: nc.dram_tensor(w, [D, D], F32, kind="ExternalInput").ap()
           for w in ("WoutT",)}
    dssm = nc.dram_tensor("ssmprep", [128, 8 * 3 * 8 * 128], F32,
                          kind="ExternalInput").ap()
    dbig = nc.dram_tensor("b_ig", [1, D], F32, kind="ExternalInput").ap()
    dbrg = nc.dram_tensor("b_rg", [1, D], F32, kind="ExternalInput").ap()
    drows8 = {w: nc.dram_tensor(w, [8, 128], F32, kind="ExternalInput").ap()
              for w in ("lam8", "h08", "nw8", "lg8_0", "lb8_0", "lg8_1", "lb8_1")}
    dwqkv = [nc.dram_tensor(f"wqkv{l}", [D, 1536], F32, kind="ExternalInput").ap()
             for l in range(2)]
    dwo = [nc.dram_tensor(f"wo{l}", [512, D], F32, kind="ExternalInput").ap()
           for l in range(2)]
    dw1 = [nc.dram_tensor(f"w1prep{l}", [128, 8 * FF2], F32,
                          kind="ExternalInput").ap()
           for l in range(2)]
    db1 = [nc.dram_tensor(f"b1p{l}", [1, FF2], F32, kind="ExternalInput").ap()
           for l in range(2)]
    dw1b = [nc.dram_tensor(f"w1b{l}", [128, 8 * FF2], BF16, kind="Internal").ap()
            for l in range(2)]
    dw2b = [nc.dram_tensor(f"w2b{l}", [HID, D], BF16, kind="Internal").ap()
            for l in range(2)]
    dw2 = [nc.dram_tensor(f"w2_{l}", [HID, D], F32, kind="ExternalInput").ap()
           for l in range(2)]
    db2 = [nc.dram_tensor(f"b2_{l}", [1, D], F32, kind="ExternalInput").ap()
           for l in range(2)]
    dqs = [nc.dram_tensor(f"qs{l}", [1, DH], F32, kind="ExternalInput").ap()
           for l in range(2)]
    dks = [nc.dram_tensor(f"ks{l}", [1, DH], F32, kind="ExternalInput").ap()
           for l in range(2)]

    dconst = {k: nc.inline_tensor(C[k], "c_" + k).ap() for k in C}

    dbg_out = {}

    def dbgt(name, shape, dtype=F32):
        if name in dbg:
            dbg_out[name] = nc.dram_tensor("dbg_" + name, list(shape), dtype,
                                           kind="ExternalOutput").ap()
            return dbg_out[name]
        return None

    AR = FArena(nc, "ar", F32, 47360)

    class _A32:
        alloc = staticmethod(AR.alloc)
        freeap = staticmethod(AR.freeap)

    class _A16:
        alloc = staticmethod(AR.alloc16)
        freeap = staticmethod(AR.freeap)

    A32, A16 = _A32, _A16

    XY = A32.alloc(NT * D)
    XYv = XY.rearrange("p (i n) -> p i n", n=D)
    rstd_cols = A32.alloc(16)
    mean_cols = A32.alloc(16)
    cp = A32.alloc(64)
    hcol = A32.alloc(8)
    h0col = A32.alloc(8)
    nwcol = A32.alloc(8)
    lgcol = [A32.alloc(8) for _ in range(2)]
    lbcol = [A32.alloc(8) for _ in range(2)]
    fstat = A32.alloc(NT * 2)
    fstatv = fstat.rearrange("p (i k) -> p i k", k=2)
    id8f = A32.alloc(8)
    id1f = A32.alloc(1)
    ones128f = A32.alloc(128)
    qsrowb = [A32.alloc(64) for _ in range(2)]
    ksrowb = [A32.alloc(64) for _ in range(2)]
    ssqrow = A32.alloc(512)
    stage8 = A32.alloc(128)

    ident = A16.alloc(128)
    onescol = A16.alloc(16)
    e4b = A16.alloc(16)
    onesrow5 = A16.alloc(512)
    bigrow = A16.alloc(D)
    brgrow = A16.alloc(D)

    def row(ap, n=None, r0=0, rn=1):
        return ap[r0:r0 + rn, 0:(n if n else ap.shape[1])]

    PB01 = [nc.alloc_psum_tensor(f"pb{i}", [128, 512], F32) for i in range(2)]
    SIMP = [nc.alloc_psum_tensor(f"simp{i}", [128, 1024], F32) for i in range(2)]
    SIMa = [s[:] for s in SIMP]
    PBa = [PB01[0][:], PB01[1][:], SIMa[0][:, 0:512], SIMa[0][:, 512:1024],
           SIMa[1][:, 0:512], SIMa[1][:, 512:1024]]
    PB = PBa
    PT16 = nc.alloc_psum_tensor("pt16", [128, 1024], BF16)[:]
    PR = nc.alloc_psum_tensor("pr", [128, 512], F32)[:]
    PT16b = PR.bitcast(BF16)

    sem_h = {n: nc.semaphore(n).__enter__()
             for n in ("pe", "sc", "dv", "pl", "dmg", "dms", "dmx",
                       "xpa", "xpb", "upa", "upb", "sta", "stb", "w2a", "w2b",
                       "pia", "pib", "poa", "pob", "w2c")}
    sPE, sSC, sDV, sPL = (S(sem_h[n]) for n in ("pe", "sc", "dv", "pl"))
    sDG, sDS, sDX = (S(sem_h[n]) for n in ("dmg", "dms", "dmx"))
    sXP = [S(sem_h["xpa"]), S(sem_h["xpb"])]
    sW2 = [S(sem_h["w2a"]), S(sem_h["w2b"])]
    sPI = [S(sem_h["pia"]), S(sem_h["pib"])]
    sPO = [S(sem_h["poa"]), S(sem_h["pob"])]
    sW3 = [S(sem_h["w2a"]), S(sem_h["w2b"]), S(sem_h["w2c"])]
    sUP = [S(sem_h["upa"]), S(sem_h["upb"])]
    sST = [S(sem_h["sta"]), S(sem_h["stb"])]

    P = Phase(nc, selfsem={"vector": sDV, "scalar": sSC, "gpsimd": sPL})
    P.snap = lambda: {"pe": sPE.n, "dv": sDV.n, "sc": sSC.n}

    cregs = A32.alloc(16)
    _cvals = [1e-08, 4.0, 0.5, -0.5, -0.25, 0.25 * (1.0 + 1e-06), 1.0 / D, 1e-06]
    for _ci, _cv in enumerate(_cvals):
        nc.const_aps.register(F32, float(_cv), cregs[:, _ci:_ci + 1])
        P.gpsimd.memset(cregs[:, _ci:_ci + 1], float(_cv))

    P.sync.dma_start(XYv, dx.rearrange("(i p) n -> p i n", p=128)) \
        .then_inc(sDS.h, 16); sDS.inc(16)
    P.sync.dma_start(ident, dconst["ident"]).then_inc(sDS.h, 16); sDS.inc(16)
    P.sync.dma_start(e4b, dconst["e4"]).then_inc(sDS.h, 16); sDS.inc(16)
    P.sync.dma_start(id8f[0:8, :], dconst["id8f"]).then_inc(sDS.h, 16); sDS.inc(16)
    P.sync.dma_start(id1f[0:1, :], dconst["id1f"]).then_inc(sDS.h, 16); sDS.inc(16)
    P.gpsimd.dma_start(row(bigrow), dbig).then_inc(sDG.h, 16); sDG.inc(16)
    P.gpsimd.dma_start(row(brgrow), dbrg).then_inc(sDG.h, 16); sDG.inc(16)
    for l in range(2):
        P.sync.dma_start(row(qsrowb[l]), dqs[l]).then_inc(sDS.h, 16); sDS.inc(16)
        P.sync.dma_start(row(ksrowb[l]), dks[l]).then_inc(sDS.h, 16); sDS.inc(16)
    P.gpsimd.memset(row(onesrow5), 1.0)
    P.gpsimd.memset(onesrow5[32:33, :], 1.0)
    P.gpsimd.memset(onescol[:, 0:1], 1.0)
    P.gpsimd.memset(row(ones128f), 1.0).then_inc(sPL.h, 1); sPL.inc()
    P.run()

    def row_to_col(dname, dst_col, prep=False):
        P.sync.dma_start(stage8[0:8, :], drows8[dname]).then_inc(sDS.h, 16); sDS.inc(16)
        if prep:
            P.scalar.wait_ge(sDS.h, sDS.n)
            P.scalar.activation(stage8[0:8, :], stage8[0:8, :], AF.Sigmoid)
            P.scalar.activation(stage8[0:8, :], stage8[0:8, :], AF.Ln, bias=1e-08,
                                scale=1.0)
            P.scalar.mul(stage8[0:8, :], stage8[0:8, :], 4.0) \
                .then_inc(sSC.h, 1); sSC.inc()
            P.tensor.wait_ge(sSC.h, sSC.n)
        else:
            P.tensor.wait_ge(sDS.h, sDS.n)
        P.tensor.wait_ge(sDS.h, sDS.n)
        P.tensor.wait_ge(sDV.h, sDV.n)
        P.tensor.transpose(PR[:, 0:8], stage8[0:8, :], id8f[0:8, :]) \
            .then_inc(sPE.h, 1); sPE.inc()
        P.vector.wait_ge(sPE.h, sPE.n)
        P.vector.tensor_copy(dst_col, PR[:, 0:8]).then_inc(sDV.h, 1); sDV.inc()
        P.run()

    row_to_col("lam8", hcol, prep=True)
    row_to_col("h08", h0col)
    row_to_col("nw8", nwcol)
    for l in range(2):
        row_to_col(f"lg8_{l}", lgcol[l])
        row_to_col(f"lb8_{l}", lbcol[l])

    sq_scr = A32.alloc(D)

    def ln_stats(src_ap, mu, var, save_mean_col=None, known_mu=None):
        if known_mu is None:
            P.vector.reduce_sum(mu, src_ap, axis=AX.X)
            P.vector.tensor_scalar_mul(mu, mu, 1.0 / D)
        else:
            P.vector.tensor_copy(mu, known_mu)
        if save_mean_col is not None:
            P.vector.tensor_copy(save_mean_col, mu)
        P.vector.scalar_tensor_tensor(sq_scr, src_ap, 1.0, src_ap,
                                      op0=ALU.mult, op1=ALU.mult, accum_out=var)
        P.vector.tensor_scalar(cp[:, 63:64], mu, mu, -1.0, op0=ALU.mult, op1=ALU.mult)
        P.vector.scalar_tensor_tensor(var, var, 1.0 / D, cp[:, 63:64],
                                      op0=ALU.mult, op1=ALU.add)
        P.vector.tensor_scalar_add(var, var, 1e-05)

    P.mark("P0")
    xnT = A16.alloc(8 * T)
    xnTv = xnT.rearrange("p (c n) -> p c n", c=8)
    lntmp = A16.alloc(D)

    last_evac_sc = sSC.n
    for i in range(NT):
        xt = XYv[:, i, :]
        mu, var, rs, nb = (cp[:, k:k + 1] for k in (0, 1, 2, 3))
        ln_stats(xt, mu, var, save_mean_col=mean_cols[:, i:i + 1])
        P.vector.sem_inc(sDV.h, 1); sDV.inc()
        P.scalar.wait_ge(sDV.h, sDV.n)
        P.scalar.activation(rs, var, AF.Sqrt).then_inc(sSC.h, 1); sSC.inc()
        P.vector.wait_ge(sSC.h, sSC.n)
        P.vector.reciprocal(rs, rs)
        P.vector.wait_ge(sPE.h, sPE.n)
        P.vector.tensor_scalar(lntmp, xt, mu, rs, op0=ALU.subtract, op1=ALU.mult) \
            .then_inc(sDV.h, 1); sDV.inc()
        P.tensor.wait_ge(sDV.h, sDV.n)
        P.tensor.wait_ge(sSC.h, last_evac_sc)
        for c in range(8):
            op = P.tensor.transpose(PT16[:, c * 128:(c + 1) * 128],
                                    lntmp[:, c * 128:(c + 1) * 128], ident)
        op.then_inc(sPE.h, 1); sPE.inc()
        P.scalar.wait_ge(sPE.h, sPE.n)
        for c in range(8):
            op = P.scalar.copy(xnTv[:, c, i * 128:(i + 1) * 128],
                               PT16[:, c * 128:(c + 1) * 128])
        op.then_inc(sSC.h, 1); sSC.inc()
        last_evac_sc = sSC.n
    P.run()

    dbg_lnT = dbgt("ln1T", [128, 8 * T])
    if dbg_lnT is not None:
        scr = A32.alloc(T)
        for c in range(8):
            P.vector.tensor_copy(scr, xnTv[:, c, :]).then_inc(sDV.h, 1); sDV.inc()
            P.sync.wait_ge(sDV.h, sDV.n)
            P.sync.dma_start(dbg_lnT.rearrange("p (c n) -> p c n", c=8)[:, c, :], scr) \
                .then_inc(sDS.h, 16); sDS.inc(16)
            P.vector.wait_ge(sDS.h, sDS.n)
        P.run()
        A32.freeap(scr)

    if stop_after == "P1":
        _dev_finish(P, sDX, dout, XYv)
        return nc, dbg_out

    P.mark("P1_ln1")
    A16.freeap(lntmp)
    hs_bf = A16.alloc(8 * T)
    hs_bfv = hs_bf.rearrange("p (c n) -> p c n", c=8)
    HSQB = A16.alloc(T)
    Wst = A16.alloc(2 * 3 * 8 * 128)
    Wstv = Wst.rearrange("p (b j k n) -> p b j k n", b=2, j=3, k=8)
    Abuf = A32.alloc(T)
    Gbuf = A32.alloc(T)
    ASQ = A32.alloc(T)
    HS = ASQ

    dbg_a = dbgt("A0", [128, T]); dbg_g = dbgt("G0", [128, T])
    dbg_h = dbgt("HS0", [128, T])

    w_free_pe = [0, 0]
    w_ready = [0, 0]
    psum_free = {}
    scan_dv = 0
    pcjobs = []
    rsplit = [0, 684, 1368, 2052, HID]
    for _l in range(2):
        for k in range(8):
            pcjobs.append((dw1b[_l][:, k * FF2:(k + 1) * FF2],
                           dw1[_l][:, k * FF2:(k + 1) * FF2]))
        for k in range(4):
            pcjobs.append((dw2b[_l][rsplit[k]:rsplit[k + 1], :],
                           dw2[_l][rsplit[k]:rsplit[k + 1], :]))
    pc_gate = [0, 0]
    ssq_pe = 0
    for hc in range(8):
        wb = hc % 2
        P.gpsimd.wait_ge(sPE.h, w_free_pe[wb])
        P.gpsimd.dma_start(
            Wstv[:, wb], dssm[:, hc * 3072:(hc + 1) * 3072]
        ).then_inc(sDG.h, 16); sDG.inc(16)
        w_ready[wb] = sDG.n
        for _ in range(3):
            if pcjobs:
                _dst, _src = pcjobs.pop(0)
                P.gpsimd.dma_start(_dst, _src).then_inc(sDG.h, 16); sDG.inc(16)
                _ndone = 24 - len(pcjobs)
                if _ndone == 12:
                    pc_gate[0] = sDG.n
                elif _ndone == 24:
                    pc_gate[1] = sDG.n

        for tt in range(4):
            tsl = slice(tt * 512, tt * 512 + 512)
            pp = tt % 2
            Bj = [PBa[3 * pp + j] for j in range(3)]
            P.tensor.wait_ge(sDG.h, w_ready[wb])
            if psum_free.get(pp):
                P.tensor.wait_ge(sSC.h, psum_free[pp][0])
                P.tensor.wait_ge(sDV.h, psum_free[pp][1])
            for j in range(3):
                for kc in range(8):
                    op = P.tensor.matmul(Bj[j], Wstv[:, wb, j, kc, :],
                                         xnTv[:, kc, tsl],
                                         start=(kc == 0), stop=(j == 0 and kc == 7))
                if j == 1:
                    op = P.tensor.matmul(Bj[j], row(bigrow)[:, hc * 128:(hc + 1) * 128],
                                         row(onesrow5), start=False, stop=True)
                elif j == 2:
                    op = P.tensor.matmul(Bj[j], row(brgrow)[:, hc * 128:(hc + 1) * 128],
                                         row(onesrow5), start=False, stop=True)
            op.then_inc(sPE.h, 1); sPE.inc()
            mm_pe = sPE.n
            if tt == 3:
                w_free_pe[wb] = sPE.n

            hcc = hcol[:, hc:hc + 1]
            P.scalar.wait_ge(sPE.h, mm_pe)
            if hc > 0 and tt == 0:
                P.scalar.wait_ge(sDV.h, scan_dv)
            P.scalar.activation(Gbuf[:, tsl], Bj[1], AF.Tanh, scale=0.5)
            P.scalar.activation(Abuf[:, tsl], Bj[2], AF.Tanh, scale=0.5)
            P.scalar.activation(Abuf[:, tsl], Abuf[:, tsl], AF.Exp,
                                bias=hcc, scale=hcc) \
                .then_inc(sSC.h, 1); sSC.inc()
            P.vector.wait_ge(sSC.h, sSC.n)
            P.vector.scalar_tensor_tensor(Gbuf[:, tsl], Gbuf[:, tsl], 1.0, Bj[0],
                                          op0=ALU.add, op1=ALU.mult)
            P.vector.tensor_tensor(ASQ[:, tsl], Abuf[:, tsl], Abuf[:, tsl],
                                   op=ALU.mult).then_inc(sDV.h, 1); sDV.inc()
            psum_free[pp] = (sSC.n, sDV.n)

        P.scalar.wait_ge(sDV.h, psum_free[0][1])
        P.scalar.wait_ge(sDV.h, psum_free[1][1])
        for tt in range(4):
            tsl = slice(tt * 512, tt * 512 + 512)
            op = P.scalar.activation(ASQ[:, tsl], ASQ[:, tsl], AF.Sqrt,
                                     scale=-0.25, bias=0.25 * (1.0 + 1e-06))
        op.then_inc(sSC.h, 1); sSC.inc()
        P.vector.wait_ge(sSC.h, sSC.n)
        P.vector.tensor_tensor(Gbuf, Gbuf, ASQ, op=ALU.mult)
        P.vector.tensor_tensor_scan(HS, Abuf, Gbuf, h0col[:, hc:hc + 1],
                                    op0=ALU.mult, op1=ALU.add) \
            .then_inc(sDV.h, 1); sDV.inc()
        scan_dv = sDV.n
        P.vector.tensor_tensor(HSQB, HS, HS, op=ALU.mult)
        P.vector.tensor_scalar_mul(hs_bfv[:, hc, :], HS, nwcol[:, hc:hc + 1]) \
            .then_inc(sDV.h, 1); sDV.inc()
        hsq_dv = sDV.n

        P.tensor.wait_ge(sDV.h, hsq_dv)
        for tt in range(4):
            tsl = slice(tt * 512, tt * 512 + 512)
            op = P.tensor.matmul(PR[0:4, 0:512], e4b[:, tt * 4:(tt + 1) * 4],
                                 HSQB[:, tsl], start=(hc == 0), stop=(hc == 7))
        op.then_inc(sPE.h, 1); sPE.inc()
        ssq_pe = sPE.n

        if hc == 0 and (dbg_a is not None or dbg_g is not None or dbg_h is not None):
            P.sync.wait_ge(sDV.h, sDV.n)
            for o, b in ((dbg_a, Abuf), (dbg_g, Gbuf), (dbg_h, HS)):
                if o is not None:
                    P.sync.dma_start(o, b).then_inc(sDS.h, 16); sDS.inc(16)
            P.vector.wait_ge(sDS.h, sDS.n)
        P.run()

    P.mark("P2_ssm")
    P.scalar.wait_ge(sPE.h, ssq_pe)
    P.scalar.activation(ssqrow[0:4, 0:512], PR[0:4, 0:512], AF.Sqrt,
                        scale=1.0 / D, bias=1e-06).then_inc(sSC.h, 1); sSC.inc()
    P.vector.wait_ge(sSC.h, sSC.n)
    P.vector.reciprocal(ssqrow[0:4, 0:512], ssqrow[0:4, 0:512]) \
        .then_inc(sDV.h, 1); sDV.inc()
    P.tensor.wait_ge(sDV.h, sDV.n)
    for q in range(4):
        op = P.tensor.transpose(PR[:, 128 + q * 4:128 + (q + 1) * 4],
                                ssqrow[0:4, q * 128:(q + 1) * 128], id8f[0:4, 0:4])
    op.then_inc(sPE.h, 1); sPE.inc()
    P.vector.wait_ge(sPE.h, sPE.n)
    P.vector.tensor_copy(
        rstd_cols.rearrange("p (r q) -> p r q", r=4),
        PR[:, 128:144].rearrange("p (q r) -> p r q", q=4)) \
        .then_inc(sDV.h, 1); sDV.inc()
    P.run()

    for ap in (Abuf, Gbuf, ASQ, ssqrow):
        A32.freeap(ap)
    A16.freeap(Wst)
    A16.freeap(HSQB)
    A16.freeap(xnT)
    A16.freeap(bigrow)
    A16.freeap(brgrow)

    P.mark("P3")
    WoF = A16.alloc(8 * D)
    WoFv = WoF.rearrange("p (c n) -> p c n", c=8)
    x1b = A32.alloc(2 * D)
    x1bv = x1b.rearrange("p (b n) -> p b n", b=2)
    SSo = A32.alloc(D)
    P.gpsimd.dma_start(WoFv, dWT["WoutT"].rearrange("(c p) n -> p c n", p=128)) \
        .then_inc(sDG.h, 16); sDG.inc(16)
    wout_ready = sDG.n

    dbg_y0 = dbgt("y0", [T, D])
    park_done = {}
    x1b_free_sc = [0, 0]
    evac_free = 0
    for i in range(NT):
        tsl = slice(i * 128, (i + 1) * 128)
        P.tensor.wait_ge(sDG.h, wout_ready)
        if evac_free:
            P.tensor.wait_ge(sSC.h, evac_free)
        for half in range(2):
            for c in range(8):
                op = P.tensor.matmul(PBa[4 + half], hs_bfv[:, c, tsl],
                                     WoFv[:, c, half * 512:(half + 1) * 512],
                                     start=(c == 0), stop=(c == 7))
        op.then_inc(sPE.h, 1); sPE.inc()
        rc = rstd_cols[:, i:i + 1]
        P.scalar.wait_ge(sPE.h, sPE.n)
        P.scalar.activation(SSo[:, 0:512], PBa[4], AF.Copy, scale=rc)
        P.scalar.activation(SSo[:, 512:1024], PBa[5], AF.Copy, scale=rc) \
            .then_inc(sSC.h, 1); sSC.inc()
        evac_free = sSC.n
        mu, var, rs, nb = (cp[:, k:k + 1] for k in (0, 1, 2, 3))
        P.vector.wait_ge(sSC.h, sSC.n)
        ln_stats(SSo, mu, var)
        P.vector.sem_inc(sDV.h, 1); sDV.inc()
        P.scalar.wait_ge(sDV.h, sDV.n)
        P.scalar.activation(rs, var, AF.Sqrt).then_inc(sSC.h, 1); sSC.inc()
        P.vector.wait_ge(sSC.h, sSC.n)
        P.vector.reciprocal(rs, rs)
        P.vector.tensor_scalar(nb, mu, rs, -1.0, op0=ALU.mult, op1=ALU.mult) \
            .then_inc(sDV.h, 1); sDV.inc()
        P.scalar.wait_ge(sDV.h, sDV.n)
        P.scalar.activation(SSo, SSo, AF.Identity, scale=rs, bias=nb) \
            .then_inc(sSC.h, 1); sSC.inc()
        bpar = i % 2
        P.vector.wait_ge(sSC.h, sSC.n)
        if i >= 2:
            P.vector.wait_ge(sXP[bpar].h, sXP[bpar].n)
            P.vector.wait_ge(sSC.h, x1b_free_sc[bpar])
        P.vector.tensor_tensor(x1bv[:, bpar, :], XYv[:, i, :], SSo, op=ALU.add) \
            .then_inc(sDV.h, 1); sDV.inc()
        x1_dv = sDV.n
        P.scalar.wait_ge(sDV.h, x1_dv)
        P.scalar.dma_start(park.rearrange("(i p) n -> p i n", p=128)[:, i, :],
                           x1bv[:, bpar, :]).then_inc(sXP[bpar].h, 16)
        sXP[bpar].inc(16)
        mu2, var2, rs2, nb2 = (cp[:, k:k + 1] for k in (8, 9, 10, 11))
        ln_stats(x1bv[:, bpar, :], mu2, var2, known_mu=mean_cols[:, i:i + 1])
        P.vector.sem_inc(sDV.h, 1); sDV.inc()
        P.scalar.wait_ge(sDV.h, sDV.n)
        P.scalar.activation(rs2, var2, AF.Sqrt).then_inc(sSC.h, 1); sSC.inc()
        P.vector.wait_ge(sSC.h, sSC.n)
        P.vector.reciprocal(rs2, rs2)
        P.vector.tensor_scalar(nb2, mu2, rs2, -1.0, op0=ALU.mult, op1=ALU.mult) \
            .then_inc(sDV.h, 1); sDV.inc()
        P.scalar.wait_ge(sDV.h, sDV.n)
        P.scalar.activation(XYv[:, i, :], x1bv[:, bpar, :], AF.Identity,
                            scale=rs2, bias=nb2).then_inc(sSC.h, 1); sSC.inc()
        x1b_free_sc[bpar] = sSC.n
    P.run()

    if dbg_y0 is not None:
        P.sync.wait_ge(sSC.h, sSC.n)
        P.sync.dma_start(dbg_y0.rearrange("(i p) n -> p i n", p=128), XYv) \
            .then_inc(sDS.h, 16); sDS.inc(16)
        P.vector.wait_ge(sDS.h, sDS.n)
        P.run()

    A16.freeap(hs_bf)
    A16.freeap(WoF)
    A32.freeap(x1b)
    A32.freeap(SSo)

    if stop_after == "P4":
        _dev_finish(P, sDX, dout, XYv)
        return nc, dbg_out

    P.mark("P4_wout")
    dbg_qr = dbgt("qrot0", [T, 512]); dbg_kr = dbgt("krot0", [T, 512])
    dbg_o = dbgt("o0", [T, 512]); dbg_yn = dbgt("ynew0", [T, D])
    dbg_h2 = dbgt("h2ln0", [128, 8 * T]); dbg_p = dbgt("p0", [T, HID])
    dbg_y3 = dbgt("y3_0", [T, D])

    park_waited = False
    for l in range(2):
        tabs = {k: A16.alloc(NT * DH) for k in ("qcos", "qsin", "kcos", "ksin")}
        tabv = {k: v.rearrange("p (i n) -> p i n", n=DH) for k, v in tabs.items()}
        qsb = A16.alloc(DH)
        ksb = A16.alloc(DH)
        WQKV = A16.alloc(8 * 1536)
        WQKVv = WQKV.rearrange("p (c n) -> p c n", c=8)
        WO = A16.alloc(4 * D)
        WOv = WO.rearrange("p (c n) -> p c n", c=4)
        KT = A16.alloc(4 * T)
        KTv = KT.rearrange("p (c n) -> p c n", c=4)
        VB = A16.alloc(NT * 520)
        VB65v = VB.rearrange("p (i h e) -> p i h e", h=8, e=65)
        QT = A16.alloc(4 * 128)
        QTv = QT.rearrange("p (c n) -> p c n", c=4)
        yT = A16.alloc(2 * 8 * 132)
        yTv = yT.rearrange("p (b c w) -> p b c w", b=2, c=8)[:, :, :, 0:129]
        YC = A16.alloc(D)
        QRB = A16.alloc(512)
        KRB = A16.alloc(512)
        PTS = [A16.alloc(640) for _ in range(2)]
        ONB = A16.alloc(512)
        ONBv = ONB.rearrange("p (h e) -> p h e", h=8)
        OTB = A16.alloc(4 * 128)
        OTBv = OTB.rearrange("p (c n) -> p c n", c=4)
        m0t = A16.alloc(128)
        m4t = A16.alloc(128)
        QF = A16.alloc(512)
        KF = A16.alloc(512)
        SQ16 = A16.alloc(512)
        RH = A16.alloc(512)
        M2 = A16.alloc(512)
        RB8 = A16.alloc(16)
        if not park_waited:
            P.sync.wait_ge(sXP[0].h, sXP[0].n)
            P.sync.wait_ge(sXP[1].h, sXP[1].n)
            P.gpsimd.wait_ge(sXP[0].h, sXP[0].n)
            P.gpsimd.wait_ge(sXP[1].h, sXP[1].n)
            park_waited = True
        for k in ("qcos", "qsin", "kcos", "ksin"):
            P.sync.dma_start(tabs[k], dconst[k]).then_inc(sDS.h, 16); sDS.inc(16)
        P.sync.dma_start(m0t, dconst["m0t"]).then_inc(sDS.h, 16); sDS.inc(16)
        P.sync.dma_start(m4t, dconst["m4t"]).then_inc(sDS.h, 16); sDS.inc(16)
        tab_ready = sDS.n
        P.gpsimd.dma_start(WQKVv, dwqkv[l].rearrange("(c p) n -> p c n", p=128)) \
            .then_inc(sDG.h, 16); sDG.inc(16)
        P.gpsimd.dma_start(WOv, dwo[l].rearrange("(c p) n -> p c n", p=128)) \
            .then_inc(sDG.h, 16); sDG.inc(16)
        wqkv_ready = sDG.n
        P.tensor.wait_ge(sDS.h, tab_ready)
        P.tensor.wait_ge(sPL.h, 1)
        P.tensor.wait_ge(sDV.h, sDV.n)
        P.tensor.matmul(PR[:, 0:64], row(ones128f), row(qsrowb[l]),
                        start=True, stop=True)
        P.tensor.matmul(PR[:, 64:128], row(ones128f), row(ksrowb[l]),
                        start=True, stop=True).then_inc(sPE.h, 1); sPE.inc()
        P.vector.wait_ge(sPE.h, sPE.n)
        P.vector.tensor_copy(qsb, PR[:, 0:64])
        P.vector.tensor_copy(ksb, PR[:, 64:128]).then_inc(sDV.h, 1); sDV.inc()
        P.gpsimd.wait_ge(sDS.h, tab_ready)
        P.gpsimd.wait_ge(sDV.h, sDV.n)
        for k, sb in (("qcos", qsb), ("qsin", qsb), ("kcos", ksb), ("ksin", ksb)):
            for i in range(NT):
                op = P.gpsimd.tensor_tensor(tabv[k][:, i, :], tabv[k][:, i, :], sb,
                                            op=ALU.mult)
        op.then_inc(sPL.h, 1); sPL.inc()
        tabs_done = sPL.n
        P.gpsimd.memset(yTv[:, 0, :, 0:1], 0.0)
        P.gpsimd.memset(yTv[:, 1, :, 0:1], 0.0).then_inc(sPL.h, 1); sPL.inc()
        P.gpsimd.memset(VB65v[:, :, :, 64:65], 1.0).then_inc(sPL.h, 1); sPL.inc()
        vones_pl = sPL.n
        P.run()

        st = {"yc": 0, "qf": (0, 0), "qkt": 0, "wo": 0, "otb": 0, "norm": 0}
        ytr = {}
        qkvpe = {}
        qkvevac = {}
        rotp1 = {}
        rotrsq = {}
        rotdone = {}
        qkready = {}
        exp_done_sc = [0, 0]
        pts_rd_pe = [0, 0]

        def emitA_yroll(j):
            pj = j % 2
            P.vector.wait_ge(sPE.h, st["yc"])
            P.vector.tensor_copy(YC, XYv[:, j, :]).then_inc(sDV.h, 1); sDV.inc()
            P.tensor.wait_ge(sDV.h, sDV.n)
            P.tensor.wait_ge(sDV.h, qkready.get(j - 1, 0))
            for c in range(8):
                op = P.tensor.transpose(PT16b[:, c * 128:(c + 1) * 128],
                                        YC[:, c * 128:(c + 1) * 128], ident)
            op.then_inc(sPE.h, 1); sPE.inc()
            st["yc"] = sPE.n
            P.vector.wait_ge(sPE.h, sPE.n)
            P.vector.tensor_copy(yTv[:, pj, :, 1:129],
                                 PT16b.rearrange("p (c n) -> p c n", c=8))
            if j > 0:
                P.vector.tensor_copy(yTv[:, pj, :, 0:1],
                                     yTv[:, 1 - pj, :, 128:129]) \
                    .then_inc(sDV.h, 1); sDV.inc()
            ytr[j] = sDV.n

        def emitA_qkv(j, ca, cb):
            pj = j % 2
            if ca == 0:
                P.tensor.wait_ge(sDG.h, wqkv_ready)
                P.tensor.wait_ge(sDV.h, ytr[j])
                P.tensor.wait_ge(sPL.h, tabs_done + 1)
                if qkvevac.get(j - 1):
                    P.tensor.wait_ge(sDV.h, qkvevac[j - 1])
                if st["wo"]:
                    P.tensor.wait_ge(sDV.h, st["wo"])
            for c in range(ca, cb):
                lhs = yTv[:, pj, c, 1:129] if c < 4 else yTv[:, pj, c, 0:128]
                for n3 in range(3):
                    dst = PR if n3 == 2 else PBa[n3]
                    op = P.tensor.matmul(dst, lhs,
                                         WQKVv[:, c, n3 * 512:(n3 + 1) * 512],
                                         start=(c == 0), stop=(c == 7))
            op.then_inc(sPE.h, 1); sPE.inc()
            if cb == 8:
                qkvpe[j] = sPE.n

        def emitA_evac(j):
            P.vector.wait_ge(sPE.h, qkvpe[j])
            P.vector.tensor_copy(QF, PBa[0])
            P.vector.tensor_copy(KF, PBa[1])
            P.vector.tensor_copy(VB65v[:, j, :, 0:64],
                                 PR.rearrange("p (h e) -> p h e", h=8)) \
                .then_inc(sDV.h, 1); sDV.inc()
            qkvevac[j] = sDV.n

        rv3 = lambda a: a.rearrange("p (h e) -> p h e", h=8)
        rv4 = lambda a: a.rearrange("p (h b e) -> p h b e", h=8, b=2)

        def emitA_rotP1(j):
            ss_q = cp[:, 16:24]
            ss_k = cp[:, 48:56]
            P.vector.wait_ge(sPE.h, st["qkt"])
            P.vector.tensor_tensor(SQ16, QF, QF, op=ALU.mult)
            P.vector.tensor_reduce(ss_q, rv3(SQ16), axis=AX.X, op=ALU.add)
            P.vector.tensor_tensor(SQ16, KF, KF, op=ALU.mult)
            P.vector.tensor_reduce(ss_k, rv3(SQ16), axis=AX.X, op=ALU.add)
            P.vector.tensor_scalar_max(ss_q, ss_q, 1e-24)
            P.vector.tensor_scalar_max(ss_k, ss_k, 1e-24)
            P.vector.tensor_copy(rv4(RH)[:, :, 0, :], rv4(QF)[:, :, 1, :])
            P.vector.tensor_copy(rv4(RH)[:, :, 1, :], rv4(QF)[:, :, 0, :])
            P.vector.tensor_copy(rv4(M2)[:, :, 0, :], rv4(KF)[:, :, 1, :])
            P.vector.tensor_copy(rv4(M2)[:, :, 1, :], rv4(KF)[:, :, 0, :]) \
                .then_inc(sDV.h, 1); sDV.inc()
            rotp1[j] = sDV.n

        def emitA_rotRSQ(j):
            P.scalar.wait_ge(sDV.h, rotp1[j])
            P.scalar.activation(cp[:, 24:32], cp[:, 16:24], AF.Ln)
            P.scalar.activation(cp[:, 24:32], cp[:, 24:32], AF.Exp, scale=-0.5)
            P.scalar.activation(cp[:, 56:64], cp[:, 48:56], AF.Ln)
            P.scalar.activation(cp[:, 56:64], cp[:, 56:64], AF.Exp, scale=-0.5) \
                .then_inc(sSC.h, 1); sSC.inc()
            rotrsq[j] = sSC.n

        def emitA_rotP2(j):
            P.vector.wait_ge(sSC.h, rotrsq[j])
            rotq_dv = 0
            for SRC, RHx, OUTB, rn, ck, sk in (
                    (QF, RH, QRB, cp[:, 24:32], "qcos", "qsin"),
                    (KF, M2, KRB, cp[:, 56:64], "kcos", "ksin")):
                cosb = bcast_axis(tabv[ck][:, j, :], 1, 8)
                sinb = bcast_axis(tabv[sk][:, j, :], 1, 8)
                P.vector.tensor_tensor(rv3(SRC), rv3(SRC), cosb, op=ALU.mult)
                P.vector.tensor_tensor(rv3(RHx), rv3(RHx), sinb, op=ALU.mult)
                P.vector.tensor_tensor(SRC, SRC, RHx, op=ALU.add)
                rb = bcast_axis(rn, 2, 64)
                P.vector.tensor_tensor(rv3(OUTB), rv3(SRC), rb, op=ALU.mult) \
                    .then_inc(sDV.h, 1); sDV.inc()
                if SRC is QF:
                    rotq_dv = sDV.n
            rotdone[j] = (rotq_dv, sDV.n)
            if dbg_qr is not None and l == 0:
                P.gpsimd.wait_ge(sDV.h, rotdone[j][1])
                P.gpsimd.dma_start(dbg_qr.rearrange("(i p) n -> p i n", p=128)[:, j, :],
                                   QRB).then_inc(sDG.h, 16); sDG.inc(16)
                P.gpsimd.dma_start(dbg_kr.rearrange("(i p) n -> p i n", p=128)[:, j, :],
                                   KRB).then_inc(sDG.h, 16); sDG.inc(16)
                P.tensor.wait_ge(sDG.h, sDG.n)

        def emitA_qkT(j, half):
            if half == 0:
                P.tensor.wait_ge(sDV.h, rotdone[j][0])
                P.tensor.wait_ge(sDV.h, qkvevac[j])
                st["qkt"] = sPE.n + 1
                for c in range(4):
                    op = P.tensor.transpose(PT16b[:, c * 128:(c + 1) * 128],
                                            QRB[:, c * 128:(c + 1) * 128], ident)
                op.then_inc(sPE.h, 1); sPE.inc()
                return
            P.tensor.wait_ge(sDV.h, rotdone[j][1])
            for c in range(4):
                op = P.tensor.transpose(PT16b[:, 512 + c * 128:512 + (c + 1) * 128],
                                        KRB[:, c * 128:(c + 1) * 128], ident)
            op.then_inc(sPE.h, 1); sPE.inc()
            st["qkt"] = sPE.n
            P.vector.wait_ge(sPE.h, sPE.n)
            jsl = slice(j * 128, (j + 1) * 128)
            P.vector.tensor_copy(QT, PT16b[:, 0:512])
            P.vector.tensor_copy(
                KTv[:, :, jsl],
                PT16b[:, 512:1024].rearrange("p (c n) -> p c n", c=4)) \
                .then_inc(sDV.h, 1); sDV.inc()
            qkready[j] = sDV.n

        emitA_yroll(0)
        emitA_qkv(0, 0, 8)
        emitA_evac(0)
        emitA_rotP1(0)
        emitA_rotRSQ(0)
        emitA_rotP2(0)
        emitA_qkT(0, 0)
        emitA_qkT(0, 1)
        emitA_yroll(1)

        def tile_geo(j):
            if j < 4:
                return 128 * (j + 1), 0
            return 640, j * 128 - 512

        sim_pe = {}
        exp_sc = {}
        pv_pe = {}
        oev_sc = {}

        def emit_sim(j, h):
            nk, k0 = tile_geo(j)
            nch = nk // 128
            p = h % 2
            ch, pb = h // 2, 64 * (h % 2)
            Sb = SIMa[p]
            P.tensor.wait_ge(sDV.h, qkready[j])
            P.tensor.wait_ge(sSC.h, exp_done_sc[p])
            P.tensor.wait_ge(sDV.h, st["norm"])
            for c in range(nch):
                m = m4t if c == nch - 1 else \
                    (m0t if (c == 0 and nk == 640) else None)
                op = P.tensor.matmul(
                    Sb[:, c * 128:(c + 1) * 128],
                    KTv[:, ch, k0 + c * 128:k0 + (c + 1) * 128][pb:pb + 64, :],
                    QTv[:, ch, :][pb:pb + 64, :],
                    start=True, stop=(m is None))
                if m is not None:
                    op = P.tensor.matmul(Sb[:, c * 128:(c + 1) * 128],
                                         ident, m, start=False, stop=True)
            op.then_inc(sPE.h, 1); sPE.inc()
            sim_pe[(j, h)] = sPE.n

        def emit_exp(j, h):
            nk, k0 = tile_geo(j)
            p = h % 2
            P.scalar.wait_ge(sPE.h, sim_pe[(j, h)])
            P.scalar.wait_ge(sPE.h, pts_rd_pe[p])
            P.scalar.activation(PTS[p][:, 0:nk], SIMa[p][:, 0:nk], AF.Exp)
            exp_sc[(j, h)] = sSC.n
            exp_done_sc[p] = sSC.n

        emit_sim(0, 0)
        emit_sim(0, 1)
        emit_exp(0, 0)
        for i in range(NT):
            nk, k0 = tile_geo(i)
            nch = nk // 128
            tsl = slice(i * 128, (i + 1) * 128)
            if i + 1 < NT:
                emitA_qkv(i + 1, 0, 8)
            for h in range(HEADS):
                p = h % 2
                obank, ocol = SIMa[p], 640 + (h // 2) * 65
                if h + 2 < HEADS:
                    emit_sim(i, h + 2)
                P.tensor.wait_ge(sSC.h, exp_sc[(i, h)])
                if h == 0:
                    P.tensor.wait_ge(sPL.h, vones_pl)
                for c in range(nch):
                    vt = (k0 + c * 128) // 128
                    op = P.tensor.matmul(obank[:, ocol:ocol + 65],
                                         PTS[p][:, c * 128:(c + 1) * 128],
                                         VB65v[:, vt, h, :],
                                         start=(c == 0), stop=(c == nch - 1))
                op.then_inc(sPE.h, 1); sPE.inc()
                pv_pe[(i, h)] = sPE.n
                pts_rd_pe[p] = sPE.n
                if i + 1 < NT:
                    if h == 3:
                        emitA_evac(i + 1)
                        emitA_rotP1(i + 1)
                    elif h == 5:
                        emitA_rotRSQ(i + 1)
                        emitA_rotP2(i + 1)
                    elif h == 6:
                        emitA_qkT(i + 1, 0)
                    elif h == 7:
                        emitA_qkT(i + 1, 1)
                if h + 1 < HEADS:
                    emit_exp(i, h + 1)
                P.scalar.wait_ge(sPE.h, pv_pe[(i, h)])
                P.scalar.copy(ONBv[:, h, :], obank[:, ocol:ocol + 64])
                oev_sc[h] = sSC.n
            tails0 = SIMa[0][:, 640:900].rearrange("p (k s) -> p k s", s=65)
            tails1 = SIMa[1][:, 640:900].rearrange("p (k s) -> p k s", s=65)
            rd8 = cp[:, 32:40]
            rd8v = rd8.rearrange("p (k e) -> p k e", e=2)
            P.vector.wait_ge(sPE.h, pv_pe[(i, HEADS - 1)])
            P.vector.reciprocal(rd8v[:, :, 0:1], tails0[:, :, 64:65])
            P.vector.reciprocal(rd8v[:, :, 1:2], tails1[:, :, 64:65])
            P.vector.wait_ge(sSC.h, oev_sc[HEADS - 1])
            P.vector.tensor_tensor(rv3(ONB), rv3(ONB), bcast_axis(rd8, 2, 64),
                                   op=ALU.mult).then_inc(sDV.h, 1); sDV.inc()
            st["norm"] = sDV.n
            P.tensor.wait_ge(sDV.h, st["norm"])
            P.tensor.wait_ge(sSC.h, st["otb"])
            for c in range(4):
                op = P.tensor.transpose(PT16[:, c * 128:(c + 1) * 128],
                                        ONB[:, c * 128:(c + 1) * 128], ident)
            op.then_inc(sPE.h, 1); sPE.inc()
            ot_pe = sPE.n
            P.scalar.wait_ge(sPE.h, ot_pe)
            P.scalar.copy(OTB, PT16[:, 0:512])
            st["otb"] = sSC.n
            ot_ready = sSC.n
            if i + 1 < NT:
                emit_sim(i + 1, 0)
                emit_sim(i + 1, 1)
                emit_exp(i + 1, 0)
            if i + 2 < NT:
                emitA_yroll(i + 2)

            P.tensor.wait_ge(sSC.h, ot_ready)
            if i + 1 < NT:
                P.tensor.wait_ge(sDV.h, qkvevac[i + 1])
            if st["wo"]:
                P.tensor.wait_ge(sDV.h, st["wo"])
            for c in range(4):
                for half in range(2):
                    op = P.tensor.matmul(PBa[half],
                                         OTBv[:, c, :],
                                         WOv[:, c, half * 512:(half + 1) * 512],
                                         start=(c == 0), stop=(c == 3))
            op.then_inc(sPE.h, 1); sPE.inc()
            P.vector.wait_ge(sPE.h, sPE.n)
            P.vector.tensor_tensor(XYv[:, i, 0:512], XYv[:, i, 0:512], PBa[0],
                                   op=ALU.add)
            P.vector.tensor_tensor(XYv[:, i, 512:1024], XYv[:, i, 512:1024], PBa[1],
                                   op=ALU.add).then_inc(sDV.h, 1); sDV.inc()
            st["wo"] = sDV.n

            if dbg_o is not None and l == 0:
                P.gpsimd.wait_ge(sSC.h, ot_ready)
                P.gpsimd.dma_start(dbg_o.rearrange("(i p) n -> p i n", p=128)[:, i, :],
                                   OTB).then_inc(sDG.h, 16); sDG.inc(16)
                P.tensor.wait_ge(sDG.h, sDG.n)
        P.run()

        if dbg_yn is not None and l == 0:
            P.sync.wait_ge(sDV.h, sDV.n)
            P.sync.dma_start(dbg_yn.rearrange("(i p) n -> p i n", p=128), XYv) \
                .then_inc(sDS.h, 16); sDS.inc(16)
            P.vector.wait_ge(sDS.h, sDS.n)
            P.run()

        if stop_after == f"L{l}C":
            _dev_finish(P, sDX, dout, XYv)
            return nc, dbg_out

        for _b in (WQKV, WO, KT, VB, QT, yT, QRB, KRB, ONB,
                   PTS[0], PTS[1], OTB):
            A16.freeap(_b)
        for _b in (m0t, m4t, QF, KF, SQ16, RH, M2, RB8):
            A16.freeap(_b)
        for _k in tabs:
            A32.freeap(tabs[_k])
        A32.freeap(qsb)
        A32.freeap(ksb)

        P.mark(f"L{l}_attnABC")
        yTN = A16.alloc(8 * 2064, top=True)
        yTNv = yTN.rearrange("p (c n) -> p c n", c=8)[:, :, 0:2049]
        murow = A16.alloc(T)
        rsrow = A16.alloc(T)
        frow_t = A32.alloc(T)
        H2a = A16.alloc(4 * T)
        H2b = A16.alloc(4 * T)
        _h2av = H2a.rearrange("p (c n) -> p c n", c=4)
        _h2bv = H2b.rearrange("p (c n) -> p c n", c=4)

        class _H2V:
            def __getitem__(self, idx):
                _, c, sl = idx
                return (_h2av if c < 4 else _h2bv)[:, c % 4, sl]
        H2v = _H2V()
        SQB = A16.alloc(512)
        P.gpsimd.memset(yTNv[:, :, 0:1], 0.0).then_inc(sPL.h, 1); sPL.inc()

        for i in range(NT):
            P.vector.wait_ge(sPE.h, sPE.n)
            P.vector.tensor_copy(YC, XYv[:, i, :]).then_inc(sDV.h, 1); sDV.inc()
            P.tensor.wait_ge(sDV.h, sDV.n)
            P.tensor.wait_ge(sSC.h, sSC.n)
            for c in range(8):
                op = P.tensor.transpose(PT16[:, c * 128:(c + 1) * 128],
                                        YC[:, c * 128:(c + 1) * 128], ident)
            op.then_inc(sPE.h, 1); sPE.inc()
            P.scalar.wait_ge(sPE.h, sPE.n)
            for c in range(8):
                op = P.scalar.copy(yTNv[:, c, 1 + i * 128:1 + (i + 1) * 128],
                                   PT16[:, c * 128:(c + 1) * 128])
            op.then_inc(sSC.h, 1); sSC.inc()
        ytn_done = sSC.n
        P.run()

        def h2src(c, tsl):
            if c < 4:
                return yTNv[:, c, 1 + tsl.start:1 + tsl.stop]
            return yTNv[:, c, tsl.start:tsl.stop]

        for tt in range(4):
            tsl = slice(tt * 512, tt * 512 + 512)
            P.tensor.wait_ge(sSC.h, ytn_done)
            P.tensor.wait_ge(sPL.h, sPL.n)
            if tt > 0:
                P.tensor.wait_ge(sDV.h, sDV.n)
            for c in range(8):
                op = P.tensor.matmul(PB[0][0:1, 0:512], onescol[:, 0:1],
                                     h2src(c, tsl), start=(c == 0), stop=(c == 7))
            op.then_inc(sPE.h, 1); sPE.inc()
            for c in range(8):
                P.scalar.wait_ge(sPE.h, sPE.n)
                op = P.scalar.activation(SQB, h2src(c, tsl), AF.Square)
                op.then_inc(sSC.h, 1); sSC.inc()
                P.tensor.wait_ge(sSC.h, sSC.n)
                op = P.tensor.matmul(PB[1][0:1, 0:512], onescol[:, 0:1],
                                     SQB, start=(c == 0), stop=(c == 7))
                op.then_inc(sPE.h, 1); sPE.inc()
            P.vector.wait_ge(sPE.h, sPE.n)
            mrow = row(murow)[:, tsl]
            rrow = row(rsrow)[:, tsl]
            frow = row(frow_t)[:, tsl]
            P.vector.tensor_scalar_mul(frow, PB[0][0:1, 0:512], 1.0 / D)
            P.vector.tensor_copy(mrow, frow)
            P.vector.scalar_tensor_tensor(frow, frow, -1.0, frow,
                                          op0=ALU.mult, op1=ALU.mult)
            P.vector.scalar_tensor_tensor(frow, PB[1][0:1, 0:512], 1.0 / D, frow,
                                          op0=ALU.mult, op1=ALU.add)
            P.vector.tensor_scalar_add(frow, frow, 1e-05).then_inc(sDV.h, 1); sDV.inc()
            P.scalar.wait_ge(sDV.h, sDV.n)
            P.scalar.activation(frow, frow, AF.Sqrt).then_inc(sSC.h, 1); sSC.inc()
            P.vector.wait_ge(sSC.h, sSC.n)
            P.vector.reciprocal(frow, frow)
            P.vector.tensor_copy(rrow, frow).then_inc(sDV.h, 1); sDV.inc()
        P.run()

        MUB = A16.alloc(T)
        RSB = A16.alloc(T)
        for tt in range(4):
            tsl = slice(tt * 512, tt * 512 + 512)
            P.tensor.wait_ge(sDV.h, sDV.n)
            if tt > 0:
                P.tensor.wait_ge(sSC.h, sSC.n)
            P.tensor.matmul(PB[2][0:128, 0:512], onesrow5[0:1, 0:128],
                            row(murow)[:, tsl], start=True, stop=True)
            P.tensor.matmul(PB[3][0:128, 0:512], onesrow5[0:1, 0:128],
                            row(rsrow)[:, tsl], start=True, stop=True) \
                .then_inc(sPE.h, 1); sPE.inc()
            P.scalar.wait_ge(sPE.h, sPE.n)
            P.scalar.copy(MUB[:, tsl], PB[2][0:128, 0:512])
            P.scalar.copy(RSB[:, tsl], PB[3][0:128, 0:512]) \
                .then_inc(sSC.h, 1); sSC.inc()
        bc_done = sSC.n

        for c in range(8):
            src_ = yTNv[:, c, 1:2049] if c < 4 else yTNv[:, c, 0:2048]
            P.vector.wait_ge(sSC.h, bc_done)
            P.vector.tensor_tensor(H2v[:, c, :], src_, MUB, op=ALU.subtract)
            P.vector.tensor_tensor(H2v[:, c, :], H2v[:, c, :], RSB, op=ALU.mult) \
                .then_inc(sDV.h, 1); sDV.inc()
            P.scalar.wait_ge(sDV.h, sDV.n)
            P.scalar.activation(H2v[:, c, :], H2v[:, c, :], AF.Identity,
                                scale=lgcol[l][:, c:c + 1], bias=lbcol[l][:, c:c + 1]) \
                .then_inc(sSC.h, 1); sSC.inc()
        P.run()
        A16.freeap(yTN)
        A16.freeap(MUB)
        A16.freeap(RSB)
        A16.freeap(SQB)
        A16.freeap(murow)
        A16.freeap(rsrow)
        A32.freeap(frow_t)

        if dbg_h2 is not None and l == 0:
            scr = A32.alloc(T)
            for c in range(8):
                P.vector.tensor_copy(scr, H2v[:, c, :]).then_inc(sDV.h, 1); sDV.inc()
                P.sync.wait_ge(sDV.h, sDV.n)
                P.sync.dma_start(dbg_h2.rearrange("p (c n) -> p c n", c=8)[:, c, :],
                                 scr).then_inc(sDS.h, 16); sDS.inc(16)
                P.vector.wait_ge(sDS.h, sDS.n)
            P.run()
            A32.freeap(scr)

        if stop_after == f"L{l}D":
            _dev_finish(P, sDX, dout, XYv)
            return nc, dbg_out

        P.mark(f"L{l}_D")
        A16.freeap(YC)
        PP = A16.alloc(22 * 512, top=True)
        W1S = A16.alloc(2 * 2 * 8 * 128)
        W2C = [[A16.alloc(512) for _ in range(22)] for _ in range(2)]
        W1Sv = W1S.rearrange("p (b a k n) -> p b a k n", b=2, a=2, k=8)
        B1L = A16.alloc(2816)
        B2L = A16.alloc(512)

        def b1ap(off, n):
            if off < 2816:
                return B1L[0:1, off:off + n]
            return B1L[32:33, off - 2816:off - 2816 + n]
        P.gpsimd.dma_start(B1L[0:1, :], db1[l][:, 0:2816]).then_inc(sDG.h, 16)
        sDG.inc(16)
        P.gpsimd.dma_start(B1L[32:33, 0:FF2 - 2816], db1[l][:, 2816:FF2]) \
            .then_inc(sDG.h, 16); sDG.inc(16)
        P.gpsimd.dma_start(B2L[0:1, :], db2[l][:, 0:512]).then_inc(sDG.h, 16)
        sDG.inc(16)
        P.gpsimd.dma_start(B2L[32:33, :], db2[l][:, 512:1024]) \
            .then_inc(sDG.h, 16); sDG.inc(16)
        P.sync.wait_ge(sDG.h, pc_gate[l])
        P.run()

        w1ch = _w1_chunks()
        w1pairs = [(w1ch[2 * b], w1ch[2 * b + 1]) for b in range(22)]
        w1off = []
        off = 0
        for coff, npart, kind, pch in w1ch:
            w1off.append(off)
            off += 8 * npart
        if l == 1:
            fstat_dv = {}
        w2h_pe = [0, 0]
        w2h_ready = [0, 0]
        pair_pe = {}
        pair_rd = {}

        def load_w2h(colh):
            csl = slice(colh * 512, colh * 512 + 512)
            eng = P.sync if colh == 0 else P.gpsimd
            eng.wait_ge(sPE.h, w2h_pe[colh])
            for ic in range(22):
                npart = 128 if ic < 21 else 42
                eng.dma_start(W2C[colh][ic][0:npart, :],
                              dw2b[l][ic * 128:ic * 128 + npart, csl]) \
                    .then_inc(sDS.h, 16); sDS.inc(16)
            w2h_ready[colh] = sDS.n

        for q in range(4):
            qsl = slice(q * 512, q * 512 + 512)
            load_w2h(0)

            pa_free = {}
            gel_free = {}
            gelp = [None, None]
            for b, (ach, gch) in enumerate(w1pairs):
                npart = ach[1]
                par = b % 2
                weng = P.sync if par == 0 else P.gpsimd
                weng.wait_ge(sPE.h, pair_pe.get((q, b - 2), 0 if q == 0 else
                                                pair_pe.get((q - 1, 20 + b), 0)))
                boff = w1off[b * 2]
                weng.dma_start(
                    W1Sv[:, par, :, :, 0:npart] if npart == 128 else
                    W1Sv[:, par, :, :, 0:npart],
                    dw1b[l][:, boff:boff + 16 * npart]
                    .rearrange("p (a k n) -> p a k n", a=2, k=8)) \
                    .then_inc(sW3[par].h, 16)
                sW3[par].inc(16)
                pair_rd[(q, b)] = sW3[par].n
                P.tensor.wait_ge(sW3[par].h, pair_rd[(q, b)])
                P.tensor.wait_ge(sDV.h, pa_free.get(b % 2, 0))
                psl_a = PB[2 + 2 * (b % 2)][0:npart, 0:512]
                psl_g = PB[3 if b % 2 == 0 else 5][0:npart, 0:512]
                for kc in range(8):
                    P.tensor.matmul(psl_a, W1Sv[:, par, 0, kc, 0:npart],
                                    H2v[:, kc, qsl], start=(kc == 0), stop=False)
                P.tensor.matmul(psl_a, b1ap(ach[0], npart),
                                onesrow5[0:1, :] if ach[0] < 2816 else
                                onesrow5[32:33, :], start=False, stop=True)
                P.tensor.wait_ge(sSC.h, gel_free.get(b % 2, 0))
                gel_free[b % 2] = sSC.n + 1
                for kc in range(8):
                    P.tensor.matmul(psl_g, W1Sv[:, par, 1, kc, 0:npart],
                                    H2v[:, kc, qsl], start=(kc == 0), stop=False)
                op = P.tensor.matmul(psl_g, b1ap(gch[0], npart),
                                     onesrow5[0:1, :] if gch[0] < 2816 else
                                     onesrow5[32:33, :], start=False, stop=True)
                op.then_inc(sPE.h, 1); sPE.inc()
                pair_pe[(q, b)] = sPE.n
                P.scalar.wait_ge(sPE.h, sPE.n)
                dst = PP[0:npart, ach[3] * 512:ach[3] * 512 + 512]
                P.scalar.activation(dst, psl_g, AF.Gelu)
                P.vector.wait_ge(sSC.h, sSC.n)
                P.vector.tensor_tensor(dst, dst, psl_a, op=ALU.mult)
                pa_free[b % 2] = sDV.n
            pp_done_dv = sDV.n
            load_w2h(1)
            P.run()

            for colh in range(2):
                csl = slice(colh * 512, colh * 512 + 512)
                w2ps_free = [0, 0]
                for it in range(4):
                    i = q * 4 + it
                    bank = 4 + (it % 2)
                    P.tensor.wait_ge(sDS.h, w2h_ready[colh])
                    if it == 0:
                        P.tensor.wait_ge(sDV.h, pp_done_dv)
                    P.tensor.wait_ge(sDV.h, w2ps_free[it % 2])
                    for ic in range(22):
                        npart = 128 if ic < 21 else 42
                        lhs = PP[0:npart, ic * 512 + it * 128:ic * 512 + (it + 1) * 128]
                        rhs = W2C[colh][ic][0:npart, :]
                        P.tensor.matmul(PB[bank][0:128, 0:512], lhs, rhs,
                                        start=(ic == 0), stop=False)
                    _r = 0 if colh == 0 else 32
                    op = P.tensor.matmul(PB[bank][0:128, 0:512],
                                         onesrow5[_r:_r + 1, 0:128],
                                         B2L[_r:_r + 1, :], start=False, stop=True)
                    op.then_inc(sPE.h, 1); sPE.inc()
                    w2h_pe[colh] = sPE.n
                    P.vector.wait_ge(sPE.h, sPE.n)
                    P.vector.tensor_tensor(XYv[:, i, csl], XYv[:, i, csl],
                                           PB[bank][0:128, 0:512], op=ALU.add)
                    w2ps_free[it % 2] = sDV.n
                    if l == 1 and colh == 1:
                        ln_stats(XYv[:, i, :], fstatv[:, i, 0:1], fstatv[:, i, 1:2])
                        fstat_dv[i] = sDV.n
            P.run()

        A16.freeap(PP)
        for _cl in W2C:
            for _b in _cl:
                A16.freeap(_b)
        A16.freeap(W1S)
        A16.freeap(B1L)
        A16.freeap(B2L)
        A16.freeap(H2a)
        A16.freeap(H2b)

        if dbg_y3 is not None and l == 0:
            P.sync.wait_ge(sDV.h, sDV.n)
            P.sync.dma_start(dbg_y3.rearrange("(i p) n -> p i n", p=128), XYv) \
                .then_inc(sDS.h, 16); sDS.inc(16)
            P.vector.wait_ge(sDS.h, sDS.n)
            P.run()

        if stop_after == f"L{l}E":
            _dev_finish(P, sDX, dout, XYv)
            return nc, dbg_out

    P.mark("L_E_last")
    x1r = A32.alloc(2 * D)
    x1rv = x1r.rearrange("p (b n) -> p b n", b=2)
    OT = A32.alloc(2 * D)
    OTv = OT.rearrange("p (b n) -> p b n", b=2)
    for i in range(min(2, NT)):
        P.scalar.dma_start(x1rv[:, i % 2, :],
                           park.rearrange("(i p) n -> p i n", p=128)[:, i, :]) \
            .then_inc(sUP[i % 2].h, 16)
        sUP[i % 2].inc(16)
    for i in range(NT):
        bpar = i % 2
        mu, var, rs, nb = (cp[:, k:k + 1] for k in (40, 41, 42, 43))
        P.scalar.wait_ge(sDV.h, sDV.n)
        P.scalar.activation(rs, fstatv[:, i, 1:2], AF.Sqrt).then_inc(sSC.h, 1)
        sSC.inc()
        P.vector.wait_ge(sSC.h, sSC.n)
        P.vector.reciprocal(rs, rs)
        P.vector.tensor_scalar(nb, fstatv[:, i, 0:1], rs, -1.0, op0=ALU.mult,
                               op1=ALU.mult).then_inc(sDV.h, 1); sDV.inc()
        P.scalar.wait_ge(sDV.h, sDV.n)
        P.scalar.activation(sq_scr, XYv[:, i, :], AF.Identity, scale=rs, bias=nb) \
            .then_inc(sSC.h, 1); sSC.inc()
        P.vector.wait_ge(sSC.h, sSC.n)
        P.vector.wait_ge(sUP[bpar].h, sUP[bpar].n)
        if i >= 2:
            P.vector.wait_ge(sST[bpar].h, sST[bpar].n)
        P.vector.tensor_tensor(OTv[:, bpar, :], x1rv[:, bpar, :], sq_scr,
                               op=ALU.add).then_inc(sDV.h, 1); sDV.inc()
        P.scalar.wait_ge(sDV.h, sDV.n)
        P.scalar.dma_start(dout.rearrange("(i p) n -> p i n", p=128)[:, i, :],
                           OTv[:, bpar, :]).then_inc(sST[bpar].h, 16)
        sST[bpar].inc(16)
        if i + 2 < NT:
            P.scalar.wait_ge(sDV.h, sDV.n)
            P.scalar.dma_start(x1rv[:, (i + 2) % 2, :],
                               park.rearrange("(i p) n -> p i n", p=128)[:, i + 2, :]) \
                .then_inc(sUP[(i + 2) % 2].h, 16)
            sUP[(i + 2) % 2].inc(16)
    P.scalar.wait_ge(sST[0].h, sST[0].n)
    P.scalar.wait_ge(sST[1].h, sST[1].n)
    P.run()

    P.mark("P6")
    return nc, dbg_out


def _dev_finish(P, sDX, dout, XYv):
    P.scalar.dma_start(dout.rearrange("(i p) n -> p i n", p=128), XYv) \
        .then_inc(sDX.h, 16); sDX.inc(16)
    P.scalar.wait_ge(sDX.h, sDX.n)
    P.run()


def _prep_inputs(inputs, core):
    f = np.float32
    perm = _w1_perm()
    d = {}
    d["x"] = np.ascontiguousarray(np.asarray(inputs["L0_attn_x"], f)[core])
    d["WoutT"] = np.ascontiguousarray(np.asarray(inputs["L0_attn_W_out"], f).T)
    ssm = []
    for key in ("L0_attn_W_in", "L0_attn_W_ig", "L0_attn_W_rg"):
        Wt = np.asarray(inputs[key], f).T
        ssm.append(Wt.reshape(8, 128, 8, 128).transpose(1, 2, 0, 3))
    ssm = np.stack(ssm, axis=2)
    d["ssmprep"] = np.ascontiguousarray(ssm.reshape(128, 8 * 3 * 8 * 128))
    d["b_ig"] = np.asarray(inputs["L0_attn_b_ig"], f).reshape(1, D)
    d["b_rg"] = np.asarray(inputs["L0_attn_b_rg"], f).reshape(1, D)
    d["lam8"] = np.ascontiguousarray(np.asarray(inputs["L0_attn_lam"], f).reshape(8, 128))
    d["h08"] = np.ascontiguousarray(np.asarray(inputs["L0_attn_h0"], f).reshape(8, 128))
    d["nw8"] = np.ascontiguousarray(np.asarray(inputs["L0_attn_norm_w"], f).reshape(8, 128))
    wqkv = np.asarray(inputs["L0_ffn_wqkv"], f)
    wo = np.asarray(inputs["L0_ffn_wo"], f)
    w1 = np.asarray(inputs["L0_ffn_w1"], f)
    b1 = np.asarray(inputs["L0_ffn_b1"], f)
    w2 = np.asarray(inputs["L0_ffn_w2"], f)
    b2 = np.asarray(inputs["L0_ffn_b2"], f)
    qs = np.asarray(inputs["L0_ffn_q_scale"], f)
    ks = np.asarray(inputs["L0_ffn_k_scale"], f)
    lg = np.asarray(inputs["L0_ffn_ln_g"], f)
    lb = np.asarray(inputs["L0_ffn_ln_b"], f)
    for l in range(2):
        d[f"wqkv{l}"] = np.ascontiguousarray(wqkv[l])
        d[f"wo{l}"] = np.ascontiguousarray(wo[l])
        wp3 = w1[l][:, perm].reshape(8, 128, FF2)
        segs = []
        off = 0
        for coff, npart, kind, pch in _w1_chunks():
            segs.append(wp3[:, :, coff:coff + npart].transpose(1, 0, 2)
                        .reshape(128, 8 * npart))
        d[f"w1prep{l}"] = np.ascontiguousarray(np.concatenate(segs, axis=1))
        d[f"b1p{l}"] = np.ascontiguousarray(b1[l][perm].reshape(1, FF2))
        d[f"w2_{l}"] = np.ascontiguousarray(w2[l])
        d[f"b2_{l}"] = np.ascontiguousarray(b2[l].reshape(1, D))
        d[f"qs{l}"] = np.ascontiguousarray(qs[l].reshape(1, DH))
        d[f"ks{l}"] = np.ascontiguousarray(ks[l].reshape(1, DH))
        d[f"lg8_{l}"] = np.ascontiguousarray(lg[l].reshape(8, 128))
        d[f"lb8_{l}"] = np.ascontiguousarray(lb[l].reshape(8, 128))
    return d


_CACHE = {}


def kernel(**inputs):
    key = "main"
    if key not in _CACHE:
        _CACHE[key] = build()
    nc, _ = _CACHE[key]
    in_maps = [_prep_inputs(inputs, c) for c in range(8)]
    res = run_bass_kernel_spmd(nc, in_maps, core_ids=list(range(8)))
    out = np.stack([res.results[c]["out"] for c in range(8)], axis=0)
    return out.astype(np.float32)
